# version 14
# baseline (speedup 1.0000x reference)
import contextlib
import numpy as np
import ml_dtypes
import concourse.bass as bass
import concourse.mybir as mybir
from concourse.bass_utils import run_bass_kernel_spmd

F32 = mybir.dt.float32
F16 = mybir.dt.float16
BF16 = mybir.dt.bfloat16
I32 = mybir.dt.int32
AF = mybir.ActivationFunctionType
ALU = mybir.AluOpType
AXX = mybir.AxisListType.X

B, T, HID = 4, 2048, 2048
DBG_HEADS = 99
DBG_CHUNKS = 99
SCALE = 128.0 ** -0.5
NCH = 16
HL = 8
NIK = 44
RLNC = float(np.log(0.5) + 0.5 * np.log(128.0))

ESEM = {"tensor": "sT", "vector": "sV", "scalar": "sS", "gpsimd": "sP", "sync": "sY"}
DMA_STREAMS = ["s_p0", "s_h", "s_pk", "s_osh", "s_oT5", "s_gu0", "s_gu1", "s_dn0",
               "s_dn1", "s_xs", "s_out", "g_p0", "g_w0", "g_w1", "g_wo0", "g_wo1",
               "a_ldq", "a_cp", "a_flg", "s_dbg"]
ALL_SEMS = sorted(set(list(ESEM.values()) + DMA_STREAMS + ["sB"]))


def _tile_w(Wf, half):
    Wh = Wf[:, half * 1024:(half + 1) * 1024]
    return np.ascontiguousarray(
        Wh.reshape(16, 128, 8, 128).transpose(2, 1, 0, 3).reshape(8, 128, 2048))


def make_consts():
    s = np.arange(128)[:, None]
    t = np.arange(128)[None, :]
    relm = np.zeros((128, 136), np.float32)
    relm[:, :128] = (s <= t).astype(np.float32) - (s <= 63).astype(np.float32)
    relm[:, 128] = (np.arange(128) <= 63).astype(np.float32)
    relm[:, 129] = (np.arange(128) > 63).astype(np.float32)
    relm[:, 132] = 1e-6
    relm[:, 133] = 1e-5
    relm[:, 134] = float(np.log(0.5) + 0.5 * np.log(128.0))
    ident = np.eye(128, dtype=np.float32)
    mask = (s <= t).astype(np.float32)
    cbf = np.concatenate([ident, mask, np.ones((128, 128), np.float32)], axis=1)
    return {"relm": relm, "cbf16": cbf.astype(ml_dtypes.bfloat16)}


def make_inputs_for_core(c, inp, nonce, consts):
    b, half = c // 2, c & 1
    g = lambda k: np.asarray(inp[k])
    hb = g("L0_attn_hidden_states")[b]
    if half == 0:
        gu_rows = np.r_[0:2816, 5632:8448]
    else:
        gu_rows = np.r_[2816:5632, 8448:11264]
    d = {
        "h_my": np.ascontiguousarray(hb),
        "x_res": np.ascontiguousarray(hb[half * 1024:(half + 1) * 1024]),
        "wq_t": _tile_w(g("L0_attn_Wq"), half),
        "wk_t": _tile_w(g("L0_attn_Wk"), half),
        "wv_t": _tile_w(g("L0_attn_Wv"), half),
        "ww_t": _tile_w(g("L0_attn_Ww"), half),
        "wo_t": np.ascontiguousarray(
            g("L0_attn_Wo").reshape(16, 128, 16, 128).transpose(2, 1, 0, 3).reshape(16, 128, 2048)),
        "wf1_t": np.ascontiguousarray(
            g("L0_attn_Wf1").reshape(16, 128, 128).transpose(1, 0, 2).reshape(128, 2048)),
        "wg1_t": np.ascontiguousarray(
            g("L0_attn_Wg1").reshape(16, 128, 128).transpose(1, 0, 2).reshape(128, 2048)),
        "wf2_h": np.ascontiguousarray(g("L0_attn_Wf2")[:, half * 1024:(half + 1) * 1024]),
        "wg2_h": np.ascontiguousarray(g("L0_attn_Wg2")[:, half * 1024:(half + 1) * 1024]),
        "wb_t": np.ascontiguousarray(
            g("L0_attn_Wb")[:, half * 8:(half + 1) * 8].reshape(16, 128, 8)
            .transpose(1, 0, 2).reshape(128, 128)),
        "conv_t": np.ascontiguousarray(np.stack(
            [g("L0_attn_cq")[half * 1024:(half + 1) * 1024].reshape(8, 128, 4).transpose(1, 0, 2),
             g("L0_attn_ck")[half * 1024:(half + 1) * 1024].reshape(8, 128, 4).transpose(1, 0, 2),
             g("L0_attn_cv")[half * 1024:(half + 1) * 1024].reshape(8, 128, 4).transpose(1, 0, 2)],
            axis=1).reshape(128, 96)),
        "bg2_my": np.ascontiguousarray(g("L0_attn_bg2")[half * 1024:(half + 1) * 1024][None, :]),
        "nw_col": np.ascontiguousarray(g("L0_attn_norm_w")[:, None]),
        "gup_pk": np.ascontiguousarray(g("L0_ffn_gate_up_packed")[gu_rows]),
        "gup_sc": np.ascontiguousarray(
            g("L0_ffn_gate_up_scales").T[gu_rows].reshape(44, 128, 16)
            .transpose(1, 0, 2).reshape(128, 704)),
        "dn_pk": np.ascontiguousarray(g("L0_ffn_down_packed")[half * 1024:(half + 1) * 1024]),
        "dn_sc": np.ascontiguousarray(
            g("L0_ffn_down_scales").T[half * 1024:(half + 1) * 1024].reshape(8, 128, 44)
            .transpose(1, 0, 2).reshape(128, 352)),
        "nonce": nonce,
    }
    d.update(consts)
    return d


class Sched:
    def __init__(self):
        self.ops = []
        self.cnt = {s: 0 for s in ALL_SEMS}
        self.bank_w = [None] * 8
        self.bank_r = [None] * 8

    def op(self, eng, emit_fn, waits=(), inc=None, inc_amt=0, manual=False):
        tokv = None
        if inc is not None:
            self.cnt[inc] += inc_amt
            tokv = (inc, self.cnt[inc])
        self.ops.append((eng, tuple(w for w in waits if w is not None), emit_fn,
                         None if (manual or inc is None) else (inc, inc_amt)))
        return tokv

    def dma(self, eng, dst, src, waits=(), stream=None):
        assert stream in DMA_STREAMS, stream
        return self.op(eng, lambda E: E.e.dma_start(dst, src), waits, stream, 16)

    def eop(self, eng, emit_fn, waits=()):
        return self.op(eng, emit_fn, waits, ESEM[eng], 1)

    def bw(self, i, tok):
        self.bank_w[i] = tok

    def br_(self, i, tok):
        self.bank_r[i] = tok

    def need_w(self, i):
        return self.bank_r[i]

    def need_r(self, i):
        return self.bank_w[i]


class Eng:
    def __init__(self, engine, sems):
        self.e = engine
        self.sems = sems
        self.waited = {}

    def wait(self, tokv):
        s, v = tokv
        if self.waited.get(s, 0) >= v:
            return
        self.e.wait_ge(self.sems[s], v)
        self.waited[s] = v


def build_program(debug=(), no_poll=False, sim_local=False, stop_after="full"):
    nc = bass.Bass("TRN2", target_bir_lowering=False)
    S = Sched()

    din = {}
    for name, shape, dt in [
        ("h_my", [2048, 2048], F32), ("x_res", [1024, 2048], F32),
        ("wq_t", [8, 128, 2048], F32), ("wk_t", [8, 128, 2048], F32),
        ("wv_t", [8, 128, 2048], F32), ("ww_t", [8, 128, 2048], F32),
        ("wo_t", [16, 128, 2048], F32),
        ("wf1_t", [128, 2048], F32), ("wg1_t", [128, 2048], F32),
        ("wf2_h", [128, 1024], F32), ("wg2_h", [128, 1024], F32),
        ("wb_t", [128, 128], F32), ("conv_t", [128, 96], F32),
        ("bg2_my", [1, 1024], F32), ("nw_col", [128, 1], F32),
        ("gup_pk", [5632, 256], I32), ("gup_sc", [128, 704], F16),
        ("dn_pk", [1024, 704], I32), ("dn_sc", [128, 352], F16),
        ("nonce", [1, 4], I32),
        ("relm", [128, 136], F32), ("cbf16", [128, 384], BF16),
    ]:
        din[name] = nc.dram_tensor(name, shape, dt, kind="ExternalInput").ap()
    out_my = nc.dram_tensor("out_my", [1024, 2048], F32, kind="ExternalOutput").ap()

    dbg = {}

    def dbg_dump(name, shape, dt, src_ap, wait_toks):
        if name not in debug:
            return None
        dbg[name] = nc.dram_tensor("dbg_" + name, list(shape), dt, kind="ExternalOutput").ap()
        return S.dma("sync", dbg[name], src_ap, wait_toks, stream="s_dbg")

    _aspace = "Local" if sim_local else "Shared"
    sh_wgu = nc.dram_tensor("sh_wgu", [88, 128, 2048], BF16, addr_space=_aspace).ap()
    sh_wdn = nc.dram_tensor("sh_wdn", [16, 128, NIK * 128], BF16, addr_space=_aspace).ap()
    sh_o = nc.dram_tensor("sh_o", [2, 1024, 2048], BF16, addr_space=_aspace).ap()
    sh_flags = nc.dram_tensor("sh_flags", [2, 16], I32, addr_space=_aspace).ap()
    ldq_gu = nc.dram_tensor("ldq_gu", [44, 128, 2048], BF16).ap()
    ldq_dn = nc.dram_tensor("ldq_dn", [8, 128, NIK * 128], BF16).ap()

    big = contextlib.ExitStack()

    def sbuf(st, name, shape, dt):
        return st.enter_context(nc.sbuf_tensor(name, shape, dt))

    fT = sbuf(big, "fT", [128, 2048], BF16)
    fgT = sbuf(big, "fgT", [128, 2048], BF16)
    wf2s = sbuf(big, "wf2s", [128, 1024], BF16)
    wg2s = sbuf(big, "wg2s", [128, 1024], BF16)
    bg2s = sbuf(big, "bg2s", [1, 1024], BF16)
    nwc = sbuf(big, "nwc", [128, 1], F32)
    relmS = sbuf(big, "relmS", [128, 136], F32)
    cbS = sbuf(big, "cbS", [128, 384], BF16)
    noncS = sbuf(big, "noncS", [1, 4], I32)
    bbuf = sbuf(big, "bbuf", [128, NCH * 8], F32)
    convS = sbuf(big, "convS", [128, 96], F32)
    wbS = sbuf(big, "wbS", [128, 128], BF16)
    scg = sbuf(big, "scg", [128, 704], F32)
    scd = sbuf(big, "scd", [128, 352], F32)
    flagw = sbuf(big, "flagw", [1, 4], I32)
    flagr = sbuf(big, "flagr", [1, 32], I32)

    pb = [big.enter_context(nc.psum_tensor(f"pb{i}", [128, 512], F32)) for i in range(8)]
    sems = {s: big.enter_context(nc.semaphore(s)) for s in ALL_SEMS}
    blk = big.enter_context(nc.Block())

    def q128(bank, quarter):
        return pb[bank][:, quarter * 128:(quarter + 1) * 128]

    nc.const_aps.register(F32, 1e-6, relmS[:, 132:133])
    nc.const_aps.register(F32, 1e-5, relmS[:, 133:134])
    nc.const_aps.register(F32, float(np.log(0.5) + 0.5 * np.log(128.0)), relmS[:, 134:135])
    IDN = cbS[:, 0:128]
    MSK = cbS[:, 128:256]
    ONE = cbS[:, 256:384]

    t_relm = S.dma("sync", relmS[:, :], din["relm"][:, :], stream="s_p0")
    t_cb = S.dma("sync", cbS[:, :], din["cbf16"][:, :], stream="s_p0")
    t_non = S.dma("sync", noncS[:, :], din["nonce"][:, :], stream="s_p0")
    t_conv = S.dma("sync", convS[:, :], din["conv_t"][:, :], stream="s_p0")
    t_bg2 = S.dma("gpsimd", bg2s[:, :], din["bg2_my"][:, :], stream="g_p0")
    t_nwc = S.dma("sync", nwc[:, :], din["nw_col"][:, :], stream="s_p0")
    t_wf2 = S.dma("gpsimd", wf2s[:, :], din["wf2_h"][:, :], stream="g_p0")
    t_wg2 = S.dma("gpsimd", wg2s[:, :], din["wg2_h"][:, :], stream="g_p0")
    t_wb = S.dma("gpsimd", wbS[:, :], din["wb_t"][:, :], stream="g_p0")
    t_scg = S.dma("gpsimd", scg[:, :], din["gup_sc"][:, :], stream="g_p0")
    t_scd = S.dma("gpsimd", scd[:, :], din["dn_sc"][:, :], stream="g_p0")
    t_relm = t_cb = t_non = t_conv = t_nwc = ("s_p0", S.cnt["s_p0"])
    t_bg2 = t_wf2 = t_wg2 = t_wb = t_scg = t_scd = ("g_p0", S.cnt["g_p0"] + 16)
    t_scg2 = S.eop("vector", lambda E: E.e.tensor_scalar(
        scg[:, :], scg[:, :], float(2.0 ** 126), None, ALU.mult), [t_scg])
    t_scd2 = S.eop("vector", lambda E: E.e.tensor_scalar(
        scd[:, :], scd[:, :], float(2.0 ** 126), None, ALU.mult), [t_scd])
    t_flag1 = S.eop("vector", lambda E: E.e.tensor_scalar(
        flagw[0:1, 1:2], noncS[0:1, 0:1], 1, None, ALU.add), [t_non])
    t_flag2 = S.eop("vector", lambda E: E.e.tensor_scalar(
        flagw[0:1, 2:3], noncS[0:1, 0:1], 2, None, ALU.add), [t_non])

    p23 = contextlib.ExitStack()
    hT = sbuf(p23, "hT", [128, 16 * 2048], BF16)
    wtile = [sbuf(p23, f"wtile{i}", [128, 4 * 2048], BF16) for i in range(2)]
    pre = sbuf(p23, "pre", [128, 2051], BF16)
    qp = sbuf(p23, "qp", [128, 2048], BF16)
    kp = sbuf(p23, "kp", [128, 2048], BF16)
    vp = sbuf(p23, "vp", [128, 2048], BF16)
    wp = sbuf(p23, "wp", [128, 2048], BF16)
    lsb = sbuf(p23, "lsb", [128, 16 * 128], F32)
    zscr = sbuf(p23, "zscr", [128, 512], F32)
    ccol = sbuf(p23, "ccol", [128, NCH], F32)
    diag = sbuf(p23, "diag", [128, 4 * 128], BF16)
    ostg = sbuf(p23, "ostg", [128, 2048], BF16)
    thbuf = sbuf(p23, "thbuf", [128, 2048], BF16)
    sktS = sbuf(p23, "sktS", [128, HL * 128], BF16)
    svtS = sbuf(p23, "svtS", [128, HL * 128], BF16)
    lrelT = sbuf(p23, "lrelT", [128, 256], BF16)
    emtB = sbuf(p23, "emtB", [128, 128], BF16)
    lme3 = sbuf(p23, "lme3", [128, 4], F32)
    upT = sbuf(p23, "upT", [128, 256], BF16)
    uu = sbuf(p23, "uu", [128, 256], BF16)
    ktd = sbuf(p23, "ktd", [128, 256], BF16)
    vtd = sbuf(p23, "vtd", [128, 256], BF16)
    sk0s = sbuf(p23, "sk0s", [128, 128], BF16)
    psmB = sbuf(p23, "psmB", [128, 256], BF16)
    scrb = sbuf(p23, "scrb", [128, 128], BF16)
    ptil = sbuf(p23, "ptil", [128, 128], BF16)
    appB = sbuf(p23, "appB", [128, 128], BF16)
    appT = sbuf(p23, "appT", [128, 128], BF16)
    w2mB = sbuf(p23, "w2mB", [128, 128], BF16)
    denom = sbuf(p23, "denom", [128, 1], F32)
    o2s = sbuf(p23, "o2s", [128, 1], F32)
    rcol = sbuf(p23, "rcol", [128, 4], F32)
    sqs = sbuf(p23, "sqs", [128, 128], BF16)
    omb = sbuf(p23, "omb", [128, 128], BF16)
    ofin = sbuf(p23, "ofin", [128, 128], BF16)
    sktsB = sbuf(p23, "sktsB", [128, 128], BF16)
    svtsB = sbuf(p23, "svtsB", [128, 128], BF16)
    _pkb1 = sbuf(p23, "pkb0", [128, 704], I32)
    pkb = [_pkb1, _pkb1]
    inib = sbuf(p23, "inib", [128, 1408], I32)
    iv3 = sbuf(p23, "iv3", [128, 1408], I32)
    dqt = sbuf(p23, "dqt", [128, 2 * 1408], BF16)
    stg = sbuf(p23, "stg", [128, 2 * 1408], BF16)

    hsc = contextlib.ExitStack()
    _htmp321 = sbuf(hsc, "htmp32_0", [128, 2048], F32)
    htmp32 = [_htmp321, _htmp321]
    _htmpb1 = sbuf(hsc, "htmpb_0", [128, 2048], BF16)
    htmpb = [_htmpb1, _htmpb1]
    wf1s = sbuf(hsc, "wf1s", [128, 2048], BF16)
    wg1s = wf1s
    t_wf1 = S.dma("gpsimd", wf1s[:, :], din["wf1_t"][:, :], stream="g_p0")
    dc_tok = {}
    ev_last = {}
    for tb in range(16):
        pp = tb % 2
        ld = S.dma("sync", htmp32[pp][:, :], din["h_my"][tb * 128:(tb + 1) * 128, :],
                   [dc_tok.get(tb - 1)], stream="s_h")
        dc_tok[tb] = S.eop("vector", (lambda pp: lambda E: E.e.tensor_copy(
            htmpb[pp][:, :], htmp32[pp][:, :]))(pp), [ld, ev_last.get(tb - 1)])
        for g4 in range(4):
            bank = g4 % 2
            def mk_tr(pp=pp, g4=g4, bank=bank):
                def f(E):
                    mm = None
                    for j in range(4):
                        db = g4 * 4 + j
                        mm = E.e.matmul(pb[bank][:, j * 128:(j + 1) * 128],
                                        htmpb[pp][:, db * 128:(db + 1) * 128],
                                        IDN, start=True, stop=True)
                    return mm
                return f
            tk = S.eop("tensor", mk_tr(), [dc_tok[tb], t_cb, S.need_w(bank)])
            S.bw(bank, tk)
            def mk_ev(tb=tb, g4=g4, bank=bank):
                def f(E):
                    dst = hT[:, :].rearrange("p (kb t) -> p kb t", t=2048)[
                        :, g4 * 4:(g4 + 1) * 4, tb * 128:(tb + 1) * 128]
                    src = pb[bank][:, :].rearrange("p (a b) -> p a b", b=128)
                    return E.e.activation(dst, src, AF.Copy)
                return f
            evt = S.eop("scalar", mk_ev(), [tk])
            S.br_(bank, evt)
        ev_last[tb] = evt
    t_hT = ev_last[15]

    t_fT_first = None
    for widx, (wsb, dstT) in enumerate([(wf1s, fT), (wg1s, fgT)]):
        if widx == 0:
            t_w = t_wf1
        else:
            t_w = S.dma("gpsimd", wg1s[:, :], din["wg1_t"][:, :], [t_fT_first], stream="g_p0")
        for tc in range(4):
            bank = 2 + (tc % 2)
            def mk_f(wsb=wsb, tc=tc, bank=bank):
                def f(E):
                    mm = None
                    for kbi in range(16):
                        mm = E.e.matmul(pb[bank][:, :], wsb[:, kbi * 128:(kbi + 1) * 128],
                                        hT[:, kbi * 2048 + tc * 512: kbi * 2048 + (tc + 1) * 512],
                                        start=(kbi == 0), stop=(kbi == 15))
                    return mm
                return f
            tk = S.eop("tensor", mk_f(), [t_w, t_hT, S.need_w(bank)])
            S.bw(bank, tk)
            evt = S.eop("scalar", (lambda dstT=dstT, tc=tc, bank=bank:
                                   lambda E: E.e.activation(
                                       dstT[:, tc * 512:(tc + 1) * 512], pb[bank][:, :],
                                       AF.Copy))(), [tk])
            S.br_(bank, evt)
        if widx == 0:
            t_fT_first = tk
    t_fT = evt
    hsc.close()

    t_beta = None
    for ch in range(NCH):
        def mk_b(ch=ch):
            def f(E):
                mm = None
                for kbi in range(16):
                    mm = E.e.matmul(pb[4][:, 0:8],
                                    hT[:, kbi * 2048 + ch * 128: kbi * 2048 + (ch + 1) * 128],
                                    wbS[:, kbi * 8:(kbi + 1) * 8],
                                    start=(kbi == 0), stop=(kbi == 15))
                return mm
            return f
        tk = S.eop("tensor", mk_b(), [t_wb, t_hT, S.need_w(4)])
        S.bw(4, tk)
        t_beta = S.eop("scalar", (lambda ch=ch: lambda E: E.e.activation(
            bbuf[:, ch * 8:(ch + 1) * 8], pb[4][:, 0:8], AF.Sigmoid))(), [tk])
        S.br_(4, t_beta)

    dq = {}

    def _dq_params(kind):
        if kind == "gu":
            return din["gup_pk"], scg, t_scg2, 256, 2, ldq_gu
        return din["dn_pk"], scd, t_scd2, 704, 4, ldq_dn

    dq_passes = [("gu", o, q) for o in range(44) for q in range(2)] + \
                [("dn", o, q) for o in range(8) for q in range(4)]
    dq_state = {"a": 0, "pending": [], "gu_last": None, "dn_last": None,
                "last_val": None, "dqt_tr": [None, None], "stg_dma": [None, None],
                "par": {}}

    def emit_dq_A(kind, o, q):
        pk_ap, sc_sb, t_sc, wrow, npass, _ = _dq_params(kind)
        par = len(dq_state["par"]) % 2
        dq_state["par"][(kind, o, q)] = par
        base = par * 1408
        if q == 0:
            guard = dq.get((kind, "pkread", o - 1)) if o > 0 else \
                (dq.get(("gu", "pkread", 43)) if kind == "dn" else None)
            dq[(kind, "pk", o)] = S.dma(
                "sync", pkb[0][:, 0:wrow], pk_ap[o * 128:(o + 1) * 128, :],
                [guard], stream="s_pk")
        t_pk = dq[(kind, "pk", o)]
        wpp = wrow // npass
        nin = wpp * 8

        def mk_v3(q=q, wpp=wpp):
            def f(E):
                last = None
                for i in range(8):
                    sh = 22 - 4 * i
                    op = ALU.logical_shift_left if sh >= 0 else ALU.logical_shift_right
                    last = E.e.tensor_scalar(iv3[:, i * wpp:(i + 1) * wpp],
                                             pkb[0][:, q * wpp:(q + 1) * wpp],
                                             abs(sh), 7 << 22, op, ALU.bitwise_and)
                return last
            return f
        t_v3 = S.eop("vector", mk_v3(), [t_pk, dq_state["last_val"]])

        def mk_gs1(q=q, wpp=wpp):
            def f(E):
                last = None
                for i in range(8):
                    last = E.e.tensor_scalar(inib[:, i * wpp:(i + 1) * wpp],
                                             pkb[0][:, q * wpp:(q + 1) * wpp],
                                             28 - 4 * i, -2147483648,
                                             ALU.logical_shift_left, ALU.bitwise_and)
                return last
            return f
        t_g = S.eop("vector", mk_gs1(), [t_pk, dq_state["last_val"]])
        t_gn = t_g

        def mk_vb(wpp=wpp):
            def f(E):
                last = None
                for i in range(8):
                    last = E.e.tensor_tensor(iv3[:, i * wpp:(i + 1) * wpp],
                                             iv3[:, i * wpp:(i + 1) * wpp],
                                             inib[:, i * wpp:(i + 1) * wpp], ALU.add)
                return last
            return f
        t_v = S.eop("gpsimd", mk_vb(), [t_v3, t_g])
        dq[(kind, "pkread", o)] = t_gn
        ng = wpp // 16

        def mk_val(o=o, q=q, nin=nin, ng=ng, base=base, wrow=wrow, sc_sb=sc_sb):
            def f(E):
                src = iv3[:, 0:nin].bitcast(F32).rearrange(
                    "p (i g w) -> p i g w", g=ng, w=16)
                scs = sc_sb[:, o * (wrow // 16) + q * ng: o * (wrow // 16) + (q + 1) * ng] \
                    .unsqueeze(1).unsqueeze(3).broadcast_to([128, 8, ng, 16])
                dst = dqt[:, base:base + nin].rearrange("p (g w i) -> p i g w", w=16, i=8)
                return E.e.tensor_tensor(dst, src, scs, ALU.mult)
            return f
        t_val = S.eop("vector", mk_val(), [t_v, t_sc, dq_state["dqt_tr"][par]])
        dq[(kind, "int", o, q)] = t_val
        dq_state["last_val"] = t_val

    def emit_dq_B(kind, o, q):
        _, _, _, wrow, npass, ldq_dst = _dq_params(kind)
        par = dq_state["par"][(kind, o, q)]
        base = par * 1408
        wpp = wrow // npass
        nin = wpp * 8
        nikp = nin // 128
        t_val = dq[(kind, "int", o, q)]
        t_ev = None
        t_trlast = None
        for g4 in range((nikp + 3) // 4):
            bank = 2 + (g4 % 2)
            nj = min(4, nikp - g4 * 4)

            def mk_tr(g4=g4, bank=bank, nj=nj, base=base):
                def f(E):
                    mm = None
                    for j in range(nj):
                        ikb = g4 * 4 + j
                        lhs = dqt[:, base + ikb * 128:base + (ikb + 1) * 128]
                        mm = E.e.matmul(pb[bank][:, j * 128:(j + 1) * 128],
                                        lhs, IDN, start=True, stop=True)
                    return mm
                return f
            tk = S.eop("tensor", mk_tr(), [t_val, S.need_w(bank)])
            S.bw(bank, tk)
            t_trlast = tk

            def mk_ev(g4=g4, bank=bank, nj=nj, base=base):
                def f(E):
                    sb = base + (g4 * 4) * 128
                    return E.e.activation(stg[:, sb: sb + nj * 128],
                                          pb[bank][:, 0:nj * 128], AF.Copy)
                return f
            evt = S.eop("scalar", mk_ev(), [tk, dq_state["stg_dma"][par]])
            S.br_(bank, evt)
            t_ev = evt
        dq_state["dqt_tr"][par] = t_trlast
        last_dma = S.dma("scalar",
                         ldq_dst[o, :, q * nikp * 128:(q + 1) * nikp * 128],
                         stg[:, base:base + nikp * 128], [t_ev], stream="a_ldq")
        dq_state["stg_dma"][par] = last_dma
        dq_state["gu_last" if kind == "gu" else "dn_last"] = last_dma

    def emit_dq_slot():
        for p in dq_state["pending"]:
            emit_dq_B(*p)
        dq_state["pending"] = []
        for _ in range(2):
            if dq_state["a"] < len(dq_passes):
                pp = dq_passes[dq_state["a"]]
                dq_state["a"] += 1
                emit_dq_A(*pp)
                dq_state["pending"].append(pp)

    def flush_dq():
        while dq_state["pending"] or dq_state["a"] < len(dq_passes):
            emit_dq_slot()

    def mk_dqcopy(E):
        last = None
        for arm in nc.switch_core_id(engines=[E.e], n=128):
            half = (arm.logical % 8) & 1
            if half == 0:
                i1 = E.e.dma_start(sh_wgu[0:22, :, :], ldq_gu[0:22, :, :])
                i2 = E.e.dma_start(sh_wgu[44:66, :, :], ldq_gu[22:44, :, :])
                i3 = E.e.dma_start(sh_wdn[0:8, :, :], ldq_dn[:, :, :])
            else:
                i1 = E.e.dma_start(sh_wgu[22:44, :, :], ldq_gu[0:22, :, :])
                i2 = E.e.dma_start(sh_wgu[66:88, :, :], ldq_gu[22:44, :, :])
                i3 = E.e.dma_start(sh_wdn[8:16, :, :], ldq_dn[:, :, :])
            for i in (i1, i2, i3):
                i.then_inc(E.sems["a_cp"], 16)
            last = i3
        return last

    def mk_flag(stage):
        def f(E):
            last = None
            for arm in nc.switch_core_id(engines=[E.e], n=128):
                slot = (arm.logical % 8) & 1
                i = E.e.dma_start(sh_flags[slot:slot + 1, stage:stage + 1],
                                  flagw[0:1, stage:stage + 1])
                i.then_inc(E.sems["a_flg"], 16)
                last = i
            return last
        return f

    if stop_after == "deq":
        flush_dq()

    prev = {}
    _heads = HL if stop_after != "deq" else 0
    _heads = min(_heads, DBG_HEADS)
    _nch = min(NCH, DBG_CHUNKS)
    for hl in range(_heads):
        ping = hl % 2
        t_wld = []
        for ki, wname in enumerate(["wq_t", "wk_t", "wv_t", "ww_t"]):
            t_wld.append(S.dma("gpsimd", wtile[ping][:, ki * 2048:(ki + 1) * 2048],
                               din[wname][hl, :, :], [prev.get(("scan", hl - 2))],
                               stream=f"g_w{ping}"))
        if hl == 0:
            t_pad = S.eop("vector", lambda E: E.e.memset(pre[:, 0:3], 0.0), [])
        else:
            t_pad = None

        post_sb = {0: qp, 1: kp, 2: vp, 3: wp}
        kind3_of = {0: 0, 1: 1, 2: 2, 3: 2}
        t_post = {}
        t_convprev = None
        for ki in range(4):
            k3 = kind3_of[ki]
            def mk_diag(hl=hl, k3=k3):
                def f(E):
                    i = None
                    for tap in range(4):
                        i = E.e.tensor_scalar(
                            diag[:, tap * 128:(tap + 1) * 128],
                            IDN, convS[:, k3 * 32 + hl * 4 + tap: k3 * 32 + hl * 4 + tap + 1],
                            None, ALU.mult)
                    return i
                return f
            t_diag = S.eop("vector", mk_diag(), [t_conv, t_cb, t_convprev,
                                                 prev.get(("scan", hl - 1))])
            for tc in range(4):
                bank = tc % 2
                def mk_p(hl=hl, ki=ki, tc=tc, bank=bank, ping=ping):
                    def f(E):
                        mm = None
                        for kbi in range(16):
                            mm = E.e.matmul(
                                pb[bank][:, :],
                                wtile[ping][:, ki * 2048 + kbi * 128: ki * 2048 + (kbi + 1) * 128],
                                hT[:, kbi * 2048 + tc * 512: kbi * 2048 + (tc + 1) * 512],
                                start=(kbi == 0), stop=(kbi == 15))
                        return mm
                    return f
                tk = S.eop("tensor", mk_p(), [t_wld[3], t_hT, S.need_w(bank)])
                S.bw(bank, tk)
                def mk_ev(ki=ki, tc=tc, bank=bank):
                    def f(E):
                        return E.e.activation(
                            pre[:, 3 + tc * 512: 3 + (tc + 1) * 512],
                            pb[bank][:, :], AF.Copy)
                    return f
                evt = S.eop("scalar", mk_ev(), [tk, prev.get(("scan", hl - 1)), t_pad])
                S.br_(bank, evt)
                cbank = 2 + (tc % 2)
                def mk_c(ki=ki, tc=tc, cbank=cbank, hl=hl):
                    def f(E):
                        mm = None
                        for tap in range(4):
                            mm = E.e.matmul(
                                pb[cbank][:, :],
                                diag[:, tap * 128:(tap + 1) * 128],
                                pre[:, tap + tc * 512: tap + (tc + 1) * 512],
                                start=(tap == 0), stop=(tap == 3))
                        return mm
                    return f
                ctk = S.eop("tensor", mk_c(), [evt, t_diag, S.need_w(cbank)])
                S.bw(cbank, ctk)
                def mk_s(ki=ki, tc=tc, cbank=cbank):
                    def f(E):
                        return E.e.activation(post_sb[ki][:, tc * 512:(tc + 1) * 512],
                                              pb[cbank][:, :], AF.Silu)
                    return f
                st = S.eop("scalar", mk_s(), [ctk, prev.get(("scan", hl - 1)),
                                              prev.get(("dbgall", hl - 1))])
                S.br_(cbank, st)
                t_post[(ki, tc)] = st
                t_convprev = ctk
                if tc in (0, 2):
                    emit_dq_slot()
        t_qs = S.eop("vector", lambda E: E.e.tensor_scalar(
            qp[:, :], qp[:, :], float(SCALE), None, ALU.mult),
            [t_post[(0, tc2)] for tc2 in range(4)])

        t_ls = {}
        for g4 in range(4):
            def mk_z(hl=hl, g4=g4):
                def f(E):
                    mm = None
                    for j in range(4):
                        ch = g4 * 4 + j
                        mm = E.e.matmul(pb[4][:, j * 128:(j + 1) * 128],
                                        fT[:, ch * 128:(ch + 1) * 128],
                                        wf2s[:, hl * 128:(hl + 1) * 128],
                                        start=True, stop=True)
                    return mm
                return f
            ztk = S.eop("tensor", mk_z(), [t_fT, t_wf2, S.need_w(4),
                                           prev.get(("ls", hl - 1))])
            S.bw(4, ztk)
            t_a = S.eop("scalar", (lambda g4=g4: lambda E: E.e.activation(
                lsb[:, g4 * 512:(g4 + 1) * 512], pb[4][:, :], AF.Abs))(),
                [ztk, prev.get(("dbgall", hl - 1))])
            t_b = S.eop("scalar", (lambda g4=g4: lambda E: E.e.activation(
                lsb[:, g4 * 512:(g4 + 1) * 512], lsb[:, g4 * 512:(g4 + 1) * 512],
                AF.Exp, scale=-1.0))(), [t_a])
            t_c = S.eop("scalar", (lambda g4=g4: lambda E: E.e.activation(
                lsb[:, g4 * 512:(g4 + 1) * 512], lsb[:, g4 * 512:(g4 + 1) * 512],
                AF.Ln, bias=1.0))(), [t_b])
            t_rl = S.eop("scalar", (lambda g4=g4: lambda E: E.e.activation(
                zscr[:, :], pb[4][:, :], AF.Relu, scale=-1.0))(),
                [t_c, (t_ls[g4 - 1] if g4 > 0 else None)])
            S.br_(4, t_rl)
            t_ls[g4] = S.eop("vector", (lambda g4=g4: lambda E: E.e.tensor_tensor(
                lsb[:, g4 * 512:(g4 + 1) * 512], lsb[:, g4 * 512:(g4 + 1) * 512],
                zscr[:, :], ALU.add))(), [t_rl, t_c])

        t_thh = None
        for g4 in range(4):
            def mk_gh(hl=hl, g4=g4):
                def f(E):
                    mm = None
                    for j in range(4):
                        ch = g4 * 4 + j
                        E.e.matmul(pb[3][:, j * 128:(j + 1) * 128],
                                   fgT[:, ch * 128:(ch + 1) * 128],
                                   wg2s[:, hl * 128:(hl + 1) * 128],
                                   start=True, stop=False)
                        mm = E.e.matmul(pb[3][:, j * 128:(j + 1) * 128],
                                        ONE[0:1, 0:128],
                                        bg2s[0:1, hl * 128:(hl + 1) * 128],
                                        start=False, stop=True)
                    return mm
                return f
            gtk = S.eop("tensor", mk_gh(), [t_fT, t_wg2, t_bg2, S.need_w(3),
                                            prev.get(("thb", hl - 1))])
            S.bw(3, gtk)
            t_thh = S.eop("scalar", (lambda g4=g4: lambda E: E.e.activation(
                thbuf[:, g4 * 512:(g4 + 1) * 512], pb[3][:, :], AF.Tanh, scale=0.5))(), [gtk])
            S.br_(3, t_thh)
        prev[("thb", hl)] = None

        t_w2c = []
        for ch in range(NCH):
            t_w2c.append(S.eop("scalar", (lambda ch=ch: lambda E: E.e.activation(
                sqs[:, :], wp[:, ch * 128:(ch + 1) * 128], AF.Square))(),
                [t_post[(3, ch // 4)], prev.get(("sw", ch - 1, hl)), prev.get(("sqs", hl - 1))]))
            def mk_sw(ch=ch):
                def f(E):
                    return E.e.matmul(pb[3][:, 256 + ch:256 + ch + 1], sqs[:, :],
                                      ONE[:, 0:1], start=True, stop=True)
                return f
            tk = S.eop("tensor", mk_sw(), [t_w2c[-1], S.need_w(3)])
            prev[("sw", ch, hl)] = tk
        S.bw(3, tk)
        t_sq = S.eop("scalar", lambda E: E.e.activation(
            ccol[:, :], pb[3][:, 256:256 + NCH], AF.Ln, bias=1e-6),
            [tk, prev.get(("dbgall", hl - 1))])
        S.br_(3, t_sq)
        t_rc1 = S.eop("scalar", lambda E: E.e.activation(
            ccol[:, :], ccol[:, :], AF.Exp, scale=-0.5), [t_sq])
        def mk_cc(hl=hl):
            def f(E):
                bsl = bbuf[:, :].rearrange("p (c h) -> p c h", h=8)[:, :, hl]
                return E.e.tensor_tensor(ccol[:, :], ccol[:, :], bsl, ALU.mult)
            return f
        t_cc = S.eop("vector", mk_cc(), [t_rc1, t_beta])

        if hl == 0:
            t_w2z = S.eop("vector", lambda E: E.e.memset(w2mB[:, :], 0.0), [])
        t_st0 = S.eop("vector", (lambda hl=hl: lambda E: E.e.memset(
            sktS[:, hl * 128:(hl + 1) * 128], 0.0))(), [prev.get(("scan", hl - 1))])
        t_sv0 = S.eop("vector", (lambda hl=hl: lambda E: E.e.memset(
            svtS[:, hl * 128:(hl + 1) * 128], 0.0))(), [t_st0])

        P = {}

        def mb(ch, b):
            return b if ch % 2 == 0 else b - 4

        def qm(ch, b, quarter):
            return q128(mb(ch, b), quarter)

        def F(ch):
            sl0, sl1 = ch * 128, (ch + 1) * 128
            b4 = mb(ch, 4)
            par2 = (ch % 2) * 2
            pp0, pp1 = (ch % 2) * 128, (ch % 2) * 128 + 128
            def mk_rel(b4=b4, sl0=sl0, sl1=sl1):
                def f(E):
                    E.e.matmul(pb[b4][:, 0:128], relmS[:, 0:128], lsb[:, sl0:sl1],
                               start=True, stop=True)
                    return E.e.matmul(pb[b4][:, 128:258], lsb[:, sl0:sl1], relmS[:, 0:130],
                                      start=True, stop=True)
                return f
            t_rel = S.eop("tensor", mk_rel(), [t_ls[ch // 4], t_relm, t_cc,
                                               P.get(("svna", ch - 2)),
                                               (S.need_w(b4) if ch < 2 else None)])
            t_lrel = S.eop("scalar", (lambda b4=b4, pp0=pp0, pp1=pp1: lambda E: E.e.activation(
                lrelT[:, pp0:pp1], pb[b4][:, 0:128], AF.Exp, scale=-1.0))(),
                [t_rel, P.get(("app", ch - 2))])
            t_emt = S.eop("scalar", (lambda b4=b4: lambda E: E.e.activation(
                emtB[:, :], pb[b4][:, 128:256], AF.Exp))(),
                [t_rel, P.get(("up", ch - 1))])
            t_lme = S.eop("scalar", (lambda b4=b4, par2=par2: lambda E: E.e.activation(
                lme3[:, par2:par2 + 2], pb[b4][:, 256:258], AF.Exp, scale=-1.0))(),
                [t_rel, P.get(("svn", ch - 2))])
            def mk_up(sl0=sl0, sl1=sl1, pp0=pp0, pp1=pp1):
                def f(E):
                    return E.e.tensor_tensor(upT[:, pp0:pp1], wp[:, sl0:sl1],
                                             emtB[:, :], ALU.mult)
                return f
            t_up = S.eop("vector", mk_up(), [t_emt, t_post[(3, ch // 4)],
                                             P.get(("w2mm", ch - 2))])
            t_utr = S.eop("tensor", (lambda ch=ch, pp0=pp0, pp1=pp1: lambda E: E.e.matmul(
                qm(ch, 6, 0), upT[:, pp0:pp1], IDN, start=True, stop=True))(),
                [t_up, P.get(("w2m", ch - 2)),
                 (S.need_w(mb(ch, 6)) if ch < 2 else None)])
            def mk_uev(ch=ch, pp0=pp0, pp1=pp1):
                def f(E):
                    return E.e.tensor_scalar(uu[:, pp0:pp1], qm(ch, 6, 0),
                                             ccol[:, ch:ch + 1], None, ALU.mult)
                return f
            t_uev = S.eop("vector", mk_uev(), [t_utr, t_cc, P.get(("dmm", ch - 2))])
            def mk_tr2(ch=ch, sl0=sl0, sl1=sl1):
                def f(E):
                    E.e.matmul(qm(ch, 5, 0), kp[:, sl0:sl1], IDN, start=True, stop=True)
                    return E.e.matmul(qm(ch, 5, 1), vp[:, sl0:sl1], IDN, start=True, stop=True)
                return f
            t_tr2 = S.eop("tensor", mk_tr2(), [t_post[(1, ch // 4)], t_post[(2, ch // 4)],
                                               P.get(("appe", ch - 2)),
                                               (S.need_w(mb(ch, 5)) if ch < 2 else None)])
            t_kev = S.eop("scalar", (lambda ch=ch, pp0=pp0, pp1=pp1: lambda E: E.e.activation(
                ktd[:, pp0:pp1], qm(ch, 5, 0), AF.Copy))(),
                [t_tr2, P.get(("dmm", ch - 2))])
            t_vev = S.eop("scalar", (lambda ch=ch, pp0=pp0, pp1=pp1: lambda E: E.e.activation(
                vtd[:, pp0:pp1], qm(ch, 5, 1), AF.Copy))(),
                [t_tr2, P.get(("omm", ch - 2))])
            def mk_ps(ch=ch, sl0=sl0, sl1=sl1):
                def f(E):
                    return E.e.matmul(qm(ch, 6, 1), kp[:, sl0:sl1], qp[:, sl0:sl1],
                                      start=True, stop=True)
                return f
            t_ps = S.eop("tensor", mk_ps(), [t_qs, t_post[(1, ch // 4)], t_uev])
            def mk_psm(ch=ch, pp0=pp0, pp1=pp1):
                def f(E):
                    return E.e.scalar_tensor_tensor(psmB[:, pp0:pp1], qm(ch, 6, 1),
                                                    ccol[:, ch:ch + 1], MSK,
                                                    ALU.mult, ALU.mult)
                return f
            t_psm = S.eop("vector", mk_psm(), [t_ps, t_cc, P.get(("i1", ch - 2))])
            P[("rel", ch)] = t_rel
            P[("lrel", ch)] = t_lrel
            P[("lme", ch)] = t_lme
            P[("up", ch)] = t_up
            P[("uev", ch)] = t_uev
            P[("kev", ch)] = t_kev
            P[("vev", ch)] = t_vev
            P[("psm", ch)] = t_psm

        def M(ch):
            sl0, sl1 = ch * 128, (ch + 1) * 128
            par2 = (ch % 2) * 2
            pp0, pp1 = (ch % 2) * 128, (ch % 2) * 128 + 128
            def mk_skts(hl=hl, par2=par2):
                def f(E):
                    return E.e.tensor_scalar(sktsB[:, :], sktS[:, hl * 128:(hl + 1) * 128],
                                             lme3[:, par2:par2 + 1], None, ALU.mult)
                return f
            t_skts = S.eop("vector", mk_skts(), [P[("lme", ch)], t_st0,
                                                 P.get(("skn", ch - 1))])
            def mk_svts(hl=hl, par2=par2):
                def f(E):
                    return E.e.tensor_scalar(svtsB[:, :], svtS[:, hl * 128:(hl + 1) * 128],
                                             lme3[:, par2:par2 + 1], None, ALU.mult)
                return f
            t_svts = S.eop("vector", mk_svts(), [P[("lme", ch)], t_sv0,
                                                 P.get(("svn", ch - 1))])
            t_sk0t = S.eop("tensor", (lambda ch=ch: lambda E: E.e.matmul(
                qm(ch, 5, 2), sktsB[:, :], IDN, start=True, stop=True))(),
                [t_skts, P[("kev", ch)], P[("vev", ch)]])
            t_sk0e = S.eop("scalar", (lambda ch=ch: lambda E: E.e.activation(
                sk0s[:, :], qm(ch, 5, 2), AF.Copy))(), [t_sk0t, P.get(("i1", ch - 1))])
            def mk_i1(ch=ch, sl0=sl0, sl1=sl1, pp0=pp0, pp1=pp1):
                def f(E):
                    E.e.matmul(qm(ch, 6, 2), psmB[:, pp0:pp1], uu[:, pp0:pp1],
                               start=True, stop=False)
                    return E.e.matmul(qm(ch, 6, 2), qp[:, sl0:sl1], sk0s[:, :],
                                      start=False, stop=True)
                return f
            t_i1 = S.eop("tensor", mk_i1(), [P[("psm", ch)], P[("uev", ch)], t_sk0e, t_qs])
            t_scr = S.eop("vector", (lambda ch=ch, pp0=pp0, pp1=pp1: lambda E: E.e.tensor_tensor(
                scrb[:, :], qm(ch, 6, 2), lrelT[:, pp0:pp1], ALU.mult))(),
                [t_i1, P[("lrel", ch)], P.get(("pt", ch - 1))])
            t_pt = S.eop("scalar", lambda E: E.e.activation(
                ptil[:, :], scrb[:, :], AF.Exp, accum_out=denom[:, 0:1]),
                [t_scr, P.get(("app", ch - 1)), P.get(("d2", ch - 1))])
            t_app = S.eop("vector", (lambda pp0=pp0, pp1=pp1: lambda E: E.e.tensor_tensor(
                appB[:, :], ptil[:, :], lrelT[:, pp0:pp1], ALU.mult))(),
                [t_pt, P.get(("appt", ch - 1))])
            t_appt = S.eop("tensor", (lambda ch=ch: lambda E: E.e.matmul(
                qm(ch, 5, 3), appB[:, :], IDN, start=True, stop=True))(),
                [t_app, t_sk0e])
            t_appe = S.eop("scalar", (lambda ch=ch: lambda E: E.e.activation(
                appT[:, :], qm(ch, 5, 3), AF.Copy))(),
                [t_appt, P.get(("omm", ch - 1))])
            b6 = mb(ch, 6)
            def mk_w2mm(b6=b6, pp0=pp0, pp1=pp1):
                def f(E):
                    E.e.matmul(pb[b6][0:64, 384:448], upT[:, pp0:pp0 + 64], appT[:, 0:64],
                               start=True, stop=True)
                    return E.e.matmul(pb[b6][:, 448:512], upT[:, pp0:pp1], appT[:, 64:128],
                                      start=True, stop=True)
                return f
            t_w2mm = S.eop("tensor", mk_w2mm(), [t_appe, P[("up", ch)], t_scr])
            def mk_w2ma(ch=ch, b6=b6):
                def f(E):
                    return E.e.scalar_tensor_tensor(w2mB[0:64, 0:64], pb[b6][0:64, 384:448],
                                                    ccol[0:64, ch:ch + 1], MSK[0:64, 0:64],
                                                    ALU.mult, ALU.mult)
                return f
            def mk_w2mb(ch=ch, b6=b6):
                def f(E):
                    return E.e.scalar_tensor_tensor(w2mB[:, 64:128], pb[b6][:, 448:512],
                                                    ccol[:, ch:ch + 1], MSK[:, 64:128],
                                                    ALU.mult, ALU.mult)
                return f
            t_w2ma = S.eop("vector", mk_w2ma(), [t_w2mm, P.get(("omm", ch - 1)),
                                                 (t_w2z if (hl == 0 and ch == 0) else None)])
            t_w2m = S.eop("vector", mk_w2mb(), [t_w2ma])
            b4 = mb(ch, 4)
            def mk_d12(ch=ch, b4=b4, pp0=pp0, pp1=pp1):
                def f(E):
                    E.e.matmul(pb[b4][:, 0:128], uu[:, pp0:pp1], ktd[:, pp0:pp1],
                               start=True, stop=True)
                    return E.e.matmul(pb[b4][:, 128:256], uu[:, pp0:pp1], vtd[:, pp0:pp1],
                                      start=True, stop=True)
                return f
            t_dmm = S.eop("tensor", mk_d12(), [P[("uev", ch)], P[("kev", ch)],
                                               P[("vev", ch)], P[("lme", ch)],
                                               P.get(("svna", ch - 2))])
            b7 = mb(ch, 7)
            def mk_o(b7=b7, pp0=pp0, pp1=pp1):
                def f(E):
                    E.e.matmul(pb[b7][:, 384:512], w2mB[:, :], vtd[:, pp0:pp1],
                               start=True, stop=False)
                    return E.e.matmul(pb[b7][:, 384:512], appT[:, :], svtsB[:, :],
                                      start=False, stop=True)
                return f
            t_omm = S.eop("tensor", mk_o(), [t_w2m, P[("vev", ch)], t_svts, t_appe,
                                             P.get(("oev", ch - 2)),
                                             (S.need_w(b7) if ch < 2 else None)])
            P[("skts", ch)] = t_skts
            P[("svts", ch)] = t_svts
            P[("sk0e", ch)] = t_sk0e
            P[("i1", ch)] = t_i1
            P[("pt", ch)] = t_pt
            P[("app", ch)] = t_app
            P[("appt", ch)] = t_appt
            P[("appe", ch)] = t_appe
            P[("w2mm", ch)] = t_w2mm
            P[("w2m", ch)] = t_w2m
            P[("dmm", ch)] = t_dmm
            P[("omm", ch)] = t_omm

        def B1(ch):
            par2 = (ch % 2) * 2
            b7 = mb(ch, 7)
            t_skn_a = S.eop("vector", (lambda hl=hl, ch=ch: lambda E: E.e.scalar_tensor_tensor(
                sktS[:, hl * 128:(hl + 1) * 128], qm(ch, 4, 0),
                1.0, sktsB[:, :], ALU.mult, ALU.add))(),
                [P[("dmm", ch)], P[("skts", ch)]])
            t_skn = S.eop("vector", (lambda hl=hl, par2=par2: lambda E: E.e.tensor_scalar(
                sktS[:, hl * 128:(hl + 1) * 128], sktS[:, hl * 128:(hl + 1) * 128],
                lme3[:, par2 + 1:par2 + 2], None, ALU.mult))(), [t_skn_a])
            t_svn_a = S.eop("vector", (lambda hl=hl, ch=ch: lambda E: E.e.scalar_tensor_tensor(
                svtS[:, hl * 128:(hl + 1) * 128], qm(ch, 4, 1),
                1.0, svtsB[:, :], ALU.mult, ALU.add))(),
                [P[("dmm", ch)], P[("svts", ch)]])
            t_svn = S.eop("vector", (lambda hl=hl, par2=par2: lambda E: E.e.tensor_scalar(
                svtS[:, hl * 128:(hl + 1) * 128], svtS[:, hl * 128:(hl + 1) * 128],
                lme3[:, par2 + 1:par2 + 2], None, ALU.mult))(), [t_svn_a])
            t_sqo = S.eop("scalar", (lambda b7=b7: lambda E: E.e.activation(
                sqs[:, :], pb[b7][:, 384:512], AF.Square, accum_out=o2s[:, 0:1]))(),
                [P[("omm", ch)], P.get(("arg", ch - 1))])
            t_d2 = S.eop("vector", lambda E: E.e.tensor_tensor(
                rcol[:, 0:1], denom[:, 0:1], denom[:, 0:1], ALU.mult),
                [P[("pt", ch)], P.get(("arg", ch - 1))])
            t_arg = S.eop("vector", lambda E: E.e.scalar_tensor_tensor(
                rcol[:, 1:2], rcol[:, 0:1], 128e-5, o2s[:, 0:1], ALU.mult, ALU.add),
                [t_d2, t_sqo, P.get(("lnr", ch - 1))])
            t_lnr = S.eop("scalar", lambda E: E.e.activation(
                rcol[:, 2:3], rcol[:, 1:2], AF.Ln), [t_arg, P.get(("r", ch - 1))])
            t_r = S.eop("scalar", lambda E: E.e.activation(
                rcol[:, 3:4], rcol[:, 2:3], AF.Exp, scale=-0.5, bias=RLNC),
                [t_lnr, P.get(("om", ch - 1))])
            P[("sqo", ch)] = t_sqo
            P[("d2", ch)] = t_d2
            P[("arg", ch)] = t_arg
            P[("lnr", ch)] = t_lnr
            P[("r", ch)] = t_r
            P[("skn", ch)] = t_skn
            P[("svna", ch)] = t_svn_a
            P[("svn", ch)] = t_svn

        def B2(ch):
            sl0, sl1 = ch * 128, (ch + 1) * 128
            b7 = mb(ch, 7)
            t_om = S.eop("vector", (lambda b7=b7: lambda E: E.e.tensor_scalar(
                omb[:, :], pb[b7][:, 384:512], rcol[:, 3:4], None, ALU.mult))(),
                [P[("r", ch)], P[("omm", ch)], P.get(("of", ch - 1))])
            def mk_of(sl0=sl0, sl1=sl1):
                def f(E):
                    return E.e.scalar_tensor_tensor(ofin[:, :], thbuf[:, sl0:sl1], 1.0,
                                                    omb[:, :], ALU.add, ALU.mult)
                return f
            t_of = S.eop("vector", mk_of(), [t_om, t_thh, P.get(("otr", ch - 1))])
            t_otr = S.eop("tensor", (lambda ch=ch: lambda E: E.e.matmul(
                qm(ch, 7, 2), ofin[:, :], IDN, start=True, stop=True))(),
                [t_of])
            def mk_oev(ch=ch):
                def f(E):
                    return E.e.activation(ostg[:, ch * 128:(ch + 1) * 128],
                                          qm(ch, 7, 2), AF.Copy)
                return f
            t_oev = S.eop("scalar", mk_oev(), [t_otr, prev.get(("ostg", hl - 1)),
                                               prev.get(("dbgall", hl - 1))])
            P[("om", ch)] = t_om
            P[("of", ch)] = t_of
            P[("otr", ch)] = t_otr
            P[("oev", ch)] = t_oev

        for ch in range(_nch):
            if ch >= 2:
                B1(ch - 2)
            F(ch)
            if ch >= 1:
                M(ch - 1)
            if ch >= 2:
                B2(ch - 2)
        if _nch >= 2:
            B1(_nch - 2)
        M(_nch - 1)
        if _nch >= 2:
            B2(_nch - 2)
        B1(_nch - 1)
        B2(_nch - 1)

        for par in (0, 1):
            lc = _nch - 1 if (_nch - 1) % 2 == par else _nch - 2
            if lc < 0:
                continue
            off = 0 if par == 0 else -4
            S.bw(4 + off, P[("dmm", lc)])
            S.br_(4 + off, P[("svna", lc)])
            S.bw(5 + off, P[("appt", lc)])
            S.br_(5 + off, P[("appe", lc)])
            S.bw(6 + off, P[("w2mm", lc)])
            S.br_(6 + off, P[("w2m", lc)])
            S.bw(7 + off, P[("otr", lc)])
            S.br_(7 + off, P[("oev", lc)])

        def mk_oship(hl=hl):
            def f(E):
                last = None
                for arm in nc.switch_core_id(engines=[E.e], n=128):
                    slot = (arm.logical % 8) & 1
                    i = E.e.dma_start(sh_o[slot, hl * 128:(hl + 1) * 128, :], ostg[:, :])
                    i.then_inc(E.sems["s_osh"], 16)
                    last = i
                return last
            return f
        t_oship = S.op("sync", mk_oship(), [P[("oev", _nch - 1)]], "s_osh", 16, manual=True)
        prev[("ostg", hl)] = t_oship
        prev[("thb", hl)] = P[("of", _nch - 1)]
        prev[("scan", hl)] = P[("svn", _nch - 1)]
        prev[("ls", hl)] = P[("rel", _nch - 1)]
        prev[("sqs", hl)] = P[("arg", _nch - 1)]
        if hl == 0:
            _dts = [
                dbg_dump("qp0", [128, 2048], BF16, qp[:, :], [t_qs]),
                dbg_dump("kp0", [128, 2048], BF16, kp[:, :], [t_post[(1, tc2)] for tc2 in range(4)]),
                dbg_dump("wp0", [128, 2048], BF16, wp[:, :], [t_post[(3, tc2)] for tc2 in range(4)]),
                dbg_dump("lsb0", [128, 2048], F32, lsb[:, :], [t_ls[g] for g in range(4)]),
                dbg_dump("ccol0", [128, NCH], F32, ccol[:, :], [t_cc]),
                dbg_dump("ostg0", [128, 2048], BF16, ostg[:, :], [P[("oev", _nch - 1)]]),
                dbg_dump("fT0", [128, 2048], BF16, fT[:, :], [t_fT]),
                dbg_dump("bbuf0", [128, 128], F32, bbuf[:, :], [t_beta]),
            ]
            _dts = [t for t in _dts if t is not None]
            prev[("dbgall", 0)] = ("s_dbg", S.cnt["s_dbg"]) if _dts else None
    flush_dq()
    t_gu_l, t_dn_l = dq_state["gu_last"], dq_state["dn_last"]
    t_dqcp = S.op("scalar", mk_dqcopy, [t_gu_l, t_dn_l], "a_cp", 48, manual=True)
    t_dqflag = S.op("scalar", mk_flag(1), [t_dqcp, t_flag1], "a_flg", 16, manual=True)
    if stop_after != "deq" and _heads > 0:
        t_scan_all = prev.get(("scan", _heads - 1))
        t_oship_all = [prev[("ostg", hl2)] for hl2 in range(_heads)]
        t_oflag = S.op("scalar", mk_flag(2), t_oship_all + [t_flag2, t_dqflag],
                       "a_flg", 16, manual=True)
    else:
        t_oflag = None

    def mk_poll(E):
        if no_poll:
            return E.e.nop()
        with E.e.register("expd") as expd, E.e.register("expo") as expo, \
                E.e.register("wv") as wv, E.e.register("cnt") as cnt:
            nr = E.e.value_load(noncS[0:1, 0:1])
            E.e.reg_alu(expd, nr, 1, op=ALU.add)
            E.e.reg_alu(expo, nr, 2, op=ALU.add)
            E.e.reg_mov(wv, 0)
            E.e.reg_mov(cnt, 0)
            nc.br("ploop", engines=[E.e.engine])
            with nc.body("ploop", valid_engines=[E.e.engine]):
                E.e.dma_start(flagr[0:1, 0:32],
                              sh_flags[0:2, 0:16].rearrange("a b -> (a b)").unsqueeze(0)) \
                    .then_inc(E.sems["sB"], 16)
                E.e.reg_alu(wv, wv, 16, op=ALU.add)
                E.e.wait_ge(E.sems["sB"], wv)
                E.e.reg_alu(cnt, cnt, 1, op=ALU.add)
                f0 = E.e.value_load(flagr[0:1, 1:2])
                f1 = E.e.value_load(flagr[0:1, 17:18])
                f2 = E.e.value_load(flagr[0:1, 2:3])
                f3 = E.e.value_load(flagr[0:1, 18:19])
                E.e.br_cmp(f0, expd, on_true="pm1", on_false="pchk", comp_op="IS_EQ")
            with nc.body("pm1", valid_engines=[E.e.engine]):
                E.e.br_cmp(f1, expd, on_true="pm2", on_false="pchk", comp_op="IS_EQ")
            with nc.body("pm2", valid_engines=[E.e.engine]):
                E.e.br_cmp(f2, expo, on_true="pm3", on_false="pchk", comp_op="IS_EQ")
            with nc.body("pm3", valid_engines=[E.e.engine]):
                E.e.br_cmp(f3, expo, on_true="pend", on_false="pchk", comp_op="IS_EQ")
            with nc.body("pchk", valid_engines=[E.e.engine]):
                E.e.br_cmp(cnt, 3000000, on_true="ploop", on_false="pend", comp_op="IS_LT")
            nc.switch_bb("pend")
            E.e.reg_save(flagr[0:1, 30:31], expd)
            E.e.reg_save(flagr[0:1, 31:32], cnt)
        return E.e.nop()
    if stop_after in ("deq", "scan"):
        no_poll = True
    t_barrier = S.op("gpsimd", mk_poll, [t_dqflag, t_oflag, t_non], "sP", 1)
    dbg_poll = None
    t_pdump = None
    if "poll" in debug:
        dbg_poll = nc.dram_tensor("dbg_poll", [1, 32], I32, kind="ExternalOutput").ap()
        def mk_pdump(E):
            return E.e.dma_start(dbg_poll, flagr[0:1, 0:32])
        t_pdump = S.op("gpsimd", mk_pdump, [t_barrier], "sB", 16)

    if stop_after != "full":
        def mk_dummy_out(E):
            return E.e.dma_start(out_my[0:128, 0:136], relmS[:, :])
        t_do = S.dma("sync", out_my[0:128, 0:136], relmS[:, :], [t_barrier], stream="s_out")
        t_fin0 = S.op("sync", lambda E: E.e.nop(),
                      [t_do] + ([t_pdump] if t_pdump else []), "sY", 1)
        def run_for(eng_name):
            def f(engine):
                E = Eng(engine, sems)
                for (en, waits, fn, inc) in S.ops:
                    if en != eng_name:
                        continue
                    for w in waits:
                        E.wait(w)
                    instr = fn(E)
                    if inc is not None:
                        instr.then_inc(sems[inc[0]], inc[1])
            return f
        blk.sync(run_for("sync"))
        blk.gpsimd(run_for("gpsimd"))
        blk.tensor(run_for("tensor"))
        blk.vector(run_for("vector"))
        blk.scalar(run_for("scalar"))
        p4s_dummy = None
        p23.close()
        big.close()
        return nc, dbg
    p23.close()
    p4s = contextlib.ExitStack()
    x2f = sbuf(p4s, "x2f", [128, 4 * 2048], F32)
    x2T = sbuf(p4s, "x2T", [128, 16 * 512], BF16)
    x2b = sbuf(p4s, "x2b", [128, 2048], BF16)
    hff = sbuf(p4s, "hff", [128, NIK * 512], BF16)
    fmst = sbuf(p4s, "fmst", [128, 16 * 512], BF16)
    scrA = [sbuf(p4s, f"scrA{i}", [128, 2048], BF16) for i in range(4)]
    dnslab = [sbuf(p4s, f"dnslab{i}", [128, NIK * 128], BF16) for i in range(2)]
    xs = sbuf(p4s, "xs", [128, 2048], F32)
    x1b = sbuf(p4s, "x1b", [128, 2048], F32)
    xout = sbuf(p4s, "xout", [128, 2048], F32)
    sgl = sbuf(p4s, "sgl", [128, 512], F32)
    lncol = sbuf(p4s, "lncol", [128, 16], F32)
    oT5 = hff[:, 0:16 * 512]

    t_out_toks = []
    p4p = {}
    for t5 in range(2):
        def mk_oT5(t5=t5):
            def f(E):
                last = None
                for arm in nc.switch_core_id(engines=[E.e], n=128):
                    half = (arm.logical % 8) & 1
                    c0 = half * 1024 + t5 * 512
                    i1 = E.e.dma_start(
                        oT5[:, 0:8 * 512].rearrange("p (h t) -> p h t", t=512),
                        sh_o[0, :, c0:c0 + 512].rearrange("(h p) t -> p h t", p=128))
                    i1.then_inc(E.sems["s_oT5"], 16)
                    i2 = E.e.dma_start(
                        oT5[:, 8 * 512:16 * 512].rearrange("p (h t) -> p h t", t=512),
                        sh_o[1, :, c0:c0 + 512].rearrange("(h p) t -> p h t", p=128))
                    i2.then_inc(E.sems["s_oT5"], 16)
                    last = i2
                return last
            return f
        t_o2 = S.op("sync", mk_oT5(), [t_barrier, p4p.get("hff")],
                    "s_oT5", 32, manual=True)
        loc_gate = [prev.get(("ostg", _heads - 1)), prev.get(("scan", _heads - 1))] \
            if (_heads > 0 and stop_after == "full") else [t_barrier]
        t_attn_ev = {}
        for ob in range(16):
            pg = ob % 4
            t_wo = S.dma("gpsimd", scrA[pg][:, :], din["wo_t"][ob, :, :],
                         [p4p.get(("wofree", t5, ob - 4))] + loc_gate +
                         [p4p.get(("scrA_gu", t5 - 1))], stream=f"g_wo{pg % 2}")
            t_nw = S.eop("vector", (lambda pg=pg: lambda E: E.e.tensor_scalar(
                scrA[pg][:, :], scrA[pg][:, :], nwc[:, 0:1], None, ALU.mult))(),
                [t_wo, t_nwc])
            bank = ob % 2
            def mk_at(pg=pg, bank=bank):
                def f(E):
                    mm = None
                    for j in range(16):
                        mm = E.e.matmul(pb[bank][:, :], scrA[pg][:, j * 128:(j + 1) * 128],
                                        oT5[:, j * 512:(j + 1) * 512],
                                        start=(j == 0), stop=(j == 15))
                    return mm
                return f
            tk = S.eop("tensor", mk_at(), [t_nw, t_o2, S.need_w(bank)])
            S.bw(bank, tk)
            evt = S.eop("scalar", (lambda ob=ob, bank=bank: lambda E: E.e.activation(
                fmst[:, ob * 512:(ob + 1) * 512], pb[bank][:, :], AF.Copy))(),
                [tk, p4p.get("fmst")])
            S.br_(bank, evt)
            t_attn_ev[ob] = evt
            p4p[("wofree", t5, ob)] = tk
        p4p["oT5r"] = p4p[("wofree", t5, 15)]
        t_at_trs = []
        for tt in range(4):
            t_xs = S.dma("sync", xs[:, :],
                         din["x_res"][t5 * 512 + tt * 128: t5 * 512 + (tt + 1) * 128, :],
                         [p4p.get("xs")] + loc_gate, stream="s_xs")
            t_x1 = []
            for og in range(4):
                def mk_atr(og=og, tt=tt):
                    def f(E):
                        mm = None
                        for j in range(4):
                            ob = og * 4 + j
                            mm = E.e.matmul(q128(2, j),
                                            fmst[:, ob * 512 + tt * 128: ob * 512 + (tt + 1) * 128],
                                            IDN, start=True, stop=True)
                        return mm
                    return f
                tk = S.eop("tensor", mk_atr(), [t_attn_ev[og * 4 + j2] for j2 in range(4)] +
                           [S.need_w(2)])
                S.bw(2, tk)
                t_at_trs.append(tk)
                def mk_x1(og=og):
                    def f(E):
                        return E.e.scalar_tensor_tensor(
                            x1b[:, og * 512:(og + 1) * 512], pb[2][:, :], 1.0,
                            xs[:, og * 512:(og + 1) * 512], ALU.mult, ALU.add,
                            accum_out=lncol[:, og:og + 1])
                    return f
                t1 = S.eop("vector", mk_x1(), [tk, t_xs, p4p.get("x1b"), p4p.get("dbg4")])
                S.br_(2, t1)
                t_x1.append(t1)
            t_sqx = []
            for og2 in range(4):
                t_sqx.append(S.eop("scalar", (lambda og=og2: lambda E: E.e.activation(
                    sgl[:, :], x1b[:, og * 512:(og + 1) * 512], AF.Square,
                    accum_out=lncol[:, 4 + og:5 + og]))(),
                    [t_x1[og2], p4p.get("sgl"), (t_sqx[-1] if t_sqx else None)]))
            t_s1 = S.eop("vector", lambda E: E.e.reduce_sum(
                lncol[:, 8:9], lncol[:, 0:4], axis=AXX), t_x1)
            t_s2 = S.eop("vector", lambda E: E.e.reduce_sum(
                lncol[:, 9:10], lncol[:, 4:8], axis=AXX), t_sqx)
            t_mu = S.eop("scalar", lambda E: E.e.activation(
                lncol[:, 10:11], lncol[:, 8:9], AF.Copy, scale=1.0 / 2048), [t_s1])
            t_mu2 = S.eop("scalar", lambda E: E.e.activation(
                lncol[:, 11:12], lncol[:, 10:11], AF.Square), [t_mu])
            t_var = S.eop("vector", lambda E: E.e.scalar_tensor_tensor(
                lncol[:, 12:13], lncol[:, 9:10], 1.0 / 2048, lncol[:, 11:12],
                ALU.mult, ALU.subtract), [t_s2, t_mu2])
            t_sd = S.eop("scalar", lambda E: E.e.activation(
                lncol[:, 13:14], lncol[:, 12:13], AF.Ln, bias=1e-5), [t_var])
            t_rstd = S.eop("scalar", lambda E: E.e.activation(
                lncol[:, 14:15], lncol[:, 13:14], AF.Exp, scale=-0.5), [t_sd])
            t_x2 = []
            for og in range(4):
                def mk_x2(og=og, tt=tt):
                    def f(E):
                        rb = lncol[:, 14:15].broadcast_to([128, 512])
                        return E.e.scalar_tensor_tensor(
                            x2f[:, tt * 2048 + og * 512: tt * 2048 + (og + 1) * 512],
                            x1b[:, og * 512:(og + 1) * 512], lncol[:, 10:11], rb,
                            ALU.subtract, ALU.mult)
                    return f
                t_x2.append(S.eop("vector", mk_x2(), [t_rstd, t_mu, t_x1[og],
                                                      p4p.get("x2f")]))
            t_x2b = S.eop("vector", (lambda tt=tt: lambda E: E.e.tensor_copy(
                x2b[:, :], x2f[:, tt * 2048:(tt + 1) * 2048]))(),
                t_x2 + [p4p.get("x2b")])
            for g4 in range(4):
                def mk_xt(g4=g4):
                    def f(E):
                        mm = None
                        for j in range(4):
                            kb2 = g4 * 4 + j
                            mm = E.e.matmul(q128(3, j), x2b[:, kb2 * 128:(kb2 + 1) * 128],
                                            IDN, start=True, stop=True)
                        return mm
                    return f
                tk = S.eop("tensor", mk_xt(), [t_x2b, S.need_w(3)])
                S.bw(3, tk)
                def mk_xte(g4=g4, tt=tt):
                    def f(E):
                        dst = x2T[:, :].rearrange("p (kb t) -> p kb t", t=512)[
                            :, g4 * 4:(g4 + 1) * 4, tt * 128:(tt + 1) * 128]
                        return E.e.activation(dst, pb[3][:, :].rearrange(
                            "p (a b) -> p a b", b=128), AF.Copy)
                    return f
                evt = S.eop("scalar", mk_xte(), [tk, p4p.get("x2T")])
                S.br_(3, evt)
            p4p["xs"] = t_x1[-1]
            p4p["x1b"] = t_x2[-1]
            p4p["sgl"] = t_s2
            p4p[("x2T", t5, tt)] = evt
            if t5 == 0 and tt == 0:
                _d1 = dbg_dump("x1_0", [128, 2048], F32, x1b[:, :], t_x1)
                _d2 = dbg_dump("attnb0", [128, 8192], BF16, fmst[:, :],
                               [t_attn_ev[obx] for obx in range(16)])
                if _d1 is not None or _d2 is not None:
                    p4p["dbg4"] = ("s_dbg", S.cnt["s_dbg"])
        p4p["x2b"] = t_x2b
        p4p["fmst"] = t_at_trs[-1]
        t_x2T_all = [p4p[("x2T", t5, tt2)] for tt2 in range(4)]

        t_hff = {}
        for gb in range(44):
            pg = gb % 2
            t_lg = S.dma("sync", scrA[pg][:, :], sh_wgu[gb, :, :],
                         [p4p.get(("gfree", t5, gb - 2)), t_barrier, p4p.get("oT5r")],
                         stream=f"s_gu{pg}")
            t_lu = S.dma("sync", scrA[2 + pg][:, :], sh_wgu[44 + gb, :, :],
                         [p4p.get(("ufree", t5, gb - 2)), t_barrier, p4p.get("oT5r")],
                         stream=f"s_gu{pg}")
            bank, bank_u = pg, 2 + pg
            def mk_gu(pg=pg, bank=bank, bank_u=bank_u):
                def f(E):
                    mm = None
                    for kbi in range(16):
                        E.e.matmul(pb[bank][:, :], scrA[pg][:, kbi * 128:(kbi + 1) * 128],
                                   x2T[:, kbi * 512:(kbi + 1) * 512],
                                   start=(kbi == 0), stop=(kbi == 15))
                    for kbi in range(16):
                        mm = E.e.matmul(pb[bank_u][:, :],
                                        scrA[2 + pg][:, kbi * 128:(kbi + 1) * 128],
                                        x2T[:, kbi * 512:(kbi + 1) * 512],
                                        start=(kbi == 0), stop=(kbi == 15))
                    return mm
                return f
            tk = S.eop("tensor", mk_gu(), t_x2T_all + [t_lu, S.need_w(bank),
                                                       S.need_w(bank_u)])
            S.bw(bank, tk)
            S.bw(bank_u, tk)
            t_sil = S.eop("scalar", (lambda bank=bank: lambda E: E.e.activation(
                sgl[:, :], pb[bank][:, :], AF.Silu))(), [tk, p4p.get("sgl")])
            S.br_(bank, t_sil)
            def mk_h(gb=gb, bank_u=bank_u):
                def f(E):
                    return E.e.tensor_tensor(hff[:, gb * 512:(gb + 1) * 512],
                                             sgl[:, :], pb[bank_u][:, :], ALU.mult)
                return f
            t_h = S.eop("vector", mk_h(), [t_sil, p4p.get(("hffw", t5, gb))])
            S.br_(bank_u, t_h)
            p4p["sgl"] = t_h
            p4p[("gfree", t5, gb)] = tk
            p4p[("ufree", t5, gb)] = tk
            t_hff[gb] = t_h
        p4p[("scrA_gu", t5)] = tk
        t_hff_all = [t_hff[g2] for g2 in range(44)]

        t_dn_ev = {}
        for ob in range(16):
            pg = ob % 2
            t_ld = S.dma("sync", dnslab[pg][:, :], sh_wdn[ob, :, :],
                         [p4p.get(("dfree", t5, ob - 2)), t_barrier], stream=f"s_dn{pg}")
            bank = 4 + (ob % 2)
            def mk_dn(pg=pg, bank=bank):
                def f(E):
                    mm = None
                    for ikb in range(NIK):
                        mm = E.e.matmul(pb[bank][:, :], dnslab[pg][:, ikb * 128:(ikb + 1) * 128],
                                        hff[:, ikb * 512:(ikb + 1) * 512],
                                        start=(ikb == 0), stop=(ikb == NIK - 1))
                    return mm
                return f
            tk = S.eop("tensor", mk_dn(), t_hff_all + [t_ld, S.need_w(bank)])
            S.bw(bank, tk)
            evt = S.eop("scalar", (lambda ob=ob, bank=bank: lambda E: E.e.activation(
                fmst[:, ob * 512:(ob + 1) * 512], pb[bank][:, :], AF.Copy))(),
                [tk, p4p.get("fmst2"), p4p.get("dbg4")])
            S.br_(bank, evt)
            t_dn_ev[ob] = evt
            p4p[("dfree", t5, ob)] = tk

        for tt in range(4):
            t_ytr = []
            for og in range(4):
                def mk_yt(og=og, tt=tt):
                    def f(E):
                        mm = None
                        for j in range(4):
                            ob = og * 4 + j
                            mm = E.e.matmul(q128(6, j),
                                            fmst[:, ob * 512 + tt * 128: ob * 512 + (tt + 1) * 128],
                                            IDN, start=True, stop=True)
                        return mm
                    return f
                tk = S.eop("tensor", mk_yt(), [t_dn_ev[og * 4 + j2] for j2 in range(4)] +
                           [S.need_w(6)])
                S.bw(6, tk)
                evt = S.eop("scalar", (lambda og=og: lambda E: E.e.activation(
                    xout[:, og * 512:(og + 1) * 512], pb[6][:, :], AF.Copy))(),
                    [tk, p4p.get("xout")])
                S.br_(6, evt)
                t_ytr.append(evt)
            t_sqy = []
            for og2 in range(4):
                t_sqy.append(S.eop("scalar", (lambda og=og2: lambda E: E.e.activation(
                    sgl[:, :], xout[:, og * 512:(og + 1) * 512], AF.Square,
                    accum_out=lncol[:, og:og + 1]))(),
                    [t_ytr[og2], p4p.get("sgl"), (t_sqy[-1] if t_sqy else None)]))
            t_sy = S.eop("vector", lambda E: E.e.reduce_sum(
                lncol[:, 8:9], lncol[:, 0:4], axis=AXX), t_sqy)
            t_sdy = S.eop("scalar", lambda E: E.e.activation(
                lncol[:, 9:10], lncol[:, 8:9], AF.Ln, scale=1.0 / 2048, bias=1e-6), [t_sy])
            t_ry = S.eop("scalar", lambda E: E.e.activation(
                lncol[:, 10:11], lncol[:, 9:10], AF.Exp, scale=-0.5), [t_sdy])
            t_x3 = []
            for og in range(4):
                def mk_x3(og=og, tt=tt):
                    def f(E):
                        return E.e.scalar_tensor_tensor(
                            x1b[:, og * 512:(og + 1) * 512], xout[:, og * 512:(og + 1) * 512],
                            lncol[:, 10:11],
                            x2f[:, tt * 2048 + og * 512: tt * 2048 + (og + 1) * 512],
                            ALU.mult, ALU.add, accum_out=lncol[:, 4 + og:5 + og])
                    return f
                t_x3.append(S.eop("vector", mk_x3(), [t_ry, t_ytr[og], p4p.get("x1c")]))
            t_sqx3 = []
            for og2 in range(4):
                t_sqx3.append(S.eop("scalar", (lambda og=og2: lambda E: E.e.activation(
                    sgl[:, :], x1b[:, og * 512:(og + 1) * 512], AF.Square,
                    accum_out=lncol[:, 11 + og:12 + og]))(),
                    [t_x3[og2], p4p.get("sgl"), (t_sqx3[-1] if t_sqx3 else None)]))
            t_s1b = S.eop("vector", lambda E: E.e.reduce_sum(
                lncol[:, 8:9], lncol[:, 4:8], axis=AXX), t_x3)
            t_s2b = S.eop("vector", lambda E: E.e.reduce_sum(
                lncol[:, 9:10], lncol[:, 11:15], axis=AXX), t_sqx3)
            t_mub = S.eop("scalar", lambda E: E.e.activation(
                lncol[:, 10:11], lncol[:, 8:9], AF.Copy, scale=1.0 / 2048), [t_s1b])
            t_mu2b = S.eop("scalar", lambda E: E.e.activation(
                lncol[:, 15:16], lncol[:, 10:11], AF.Square), [t_mub])
            t_varb = S.eop("vector", lambda E: E.e.scalar_tensor_tensor(
                lncol[:, 12:13], lncol[:, 9:10], 1.0 / 2048, lncol[:, 15:16],
                ALU.mult, ALU.subtract), [t_s2b, t_mu2b])
            t_sdb = S.eop("scalar", lambda E: E.e.activation(
                lncol[:, 13:14], lncol[:, 12:13], AF.Ln, bias=1e-5), [t_varb])
            t_rsb = S.eop("scalar", lambda E: E.e.activation(
                lncol[:, 14:15], lncol[:, 13:14], AF.Exp, scale=-0.5), [t_sdb])
            t_x4 = []
            for og in range(4):
                def mk_x4(og=og):
                    def f(E):
                        rb = lncol[:, 14:15].broadcast_to([128, 512])
                        return E.e.scalar_tensor_tensor(
                            xout[:, og * 512:(og + 1) * 512], x1b[:, og * 512:(og + 1) * 512],
                            lncol[:, 10:11], rb, ALU.subtract, ALU.mult)
                    return f
                t_x4.append(S.eop("vector", mk_x4(), [t_rsb, t_mub, t_sqx3[og],
                                                      t_sqy[og]]))
            t_wr = S.dma("sync", out_my[t5 * 512 + tt * 128: t5 * 512 + (tt + 1) * 128, :],
                         xout[:, :], t_x4, stream="s_out")
            t_out_toks.append(t_wr)
            p4p["xout"] = t_wr
            p4p["x1c"] = t_sqx3[-1]
            p4p["sgl"] = t_sqx3[-1]
        p4p["fmst2"] = tk
        p4p["hff"] = tk
        p4p["x2f"] = t_x4[-1]

    t_fin = S.op("sync", lambda E: E.e.nop(),
                 t_out_toks + ([t_pdump] if t_pdump else []), "sY", 1)

    def run_for(eng_name):
        def f(engine):
            E = Eng(engine, sems)
            for (en, waits, fn, inc) in S.ops:
                if en != eng_name:
                    continue
                for w in waits:
                    E.wait(w)
                instr = fn(E)
                if inc is not None:
                    instr.then_inc(sems[inc[0]], inc[1])
        return f

    blk.sync(run_for("sync"))
    blk.gpsimd(run_for("gpsimd"))
    blk.tensor(run_for("tensor"))
    blk.vector(run_for("vector"))
    blk.scalar(run_for("scalar"))

    p4s.close()
    big.close()
    return nc, dbg


def kernel(**inputs):
    consts = make_consts()
    nonce = np.array([[int(np.random.randint(1, 2 ** 24 - 16)), 0, 0, 0]], dtype=np.int32)
    nc, _ = build_program()
    in_maps = [make_inputs_for_core(c, inputs, nonce, consts) for c in range(8)]
    res = run_bass_kernel_spmd(nc, in_maps, core_ids=list(range(8)))
    out = np.zeros((B, T, HID), np.float32)
    for c in range(8):
        b, half = c // 2, c & 1
        out[b, half * 1024:(half + 1) * 1024] = res.results[c]["out_my"]
    return out


# version 15
# speedup vs baseline: 1.1269x; 1.1269x over previous
import contextlib
import numpy as np
import ml_dtypes
import concourse.bass as bass
import concourse.mybir as mybir
from concourse.bass_utils import run_bass_kernel_spmd

F32 = mybir.dt.float32
F16 = mybir.dt.float16
BF16 = mybir.dt.bfloat16
I32 = mybir.dt.int32
AF = mybir.ActivationFunctionType
ALU = mybir.AluOpType
AXX = mybir.AxisListType.X

B, T, HID = 4, 2048, 2048
DBG_HEADS = 99
DBG_CHUNKS = 99
SCALE = 128.0 ** -0.5
NCH = 16
HL = 8
NIK = 44
RLNC = float(np.log(0.5) + 0.5 * np.log(128.0))

ESEM = {"tensor": "sT", "vector": "sV", "scalar": "sS", "gpsimd": "sP", "sync": "sY"}
DMA_STREAMS = ["s_p0", "s_h", "s_pk", "s_osh", "s_oT5", "s_gu0", "s_gu1", "s_dn0",
               "s_dn1", "s_xs", "s_out", "g_p0", "g_w0", "g_w1", "g_wo0", "g_wo1",
               "a_ldq", "a_cp", "a_flg", "s_dbg"]
ALL_SEMS = sorted(set(list(ESEM.values()) + DMA_STREAMS + ["sB"]))


def _tile_w(Wf, half):
    Wh = Wf[:, half * 1024:(half + 1) * 1024]
    return np.ascontiguousarray(
        Wh.reshape(16, 128, 8, 128).transpose(2, 1, 0, 3).reshape(8, 128, 2048))


def make_consts():
    s = np.arange(128)[:, None]
    t = np.arange(128)[None, :]
    relm = np.zeros((128, 136), np.float32)
    relm[:, :128] = (s <= t).astype(np.float32) - (s <= 63).astype(np.float32)
    relm[:, 128] = (np.arange(128) <= 63).astype(np.float32)
    relm[:, 129] = (np.arange(128) > 63).astype(np.float32)
    relm[:, 132] = 1e-6
    relm[:, 133] = 1e-5
    relm[:, 134] = float(np.log(0.5) + 0.5 * np.log(128.0))
    ident = np.eye(128, dtype=np.float32)
    mask = (s <= t).astype(np.float32)
    cbf = np.concatenate([ident, mask, np.ones((128, 128), np.float32)], axis=1)
    return {"relm": relm, "cbf16": cbf.astype(ml_dtypes.bfloat16)}


def make_inputs_for_core(c, inp, nonce, consts):
    b, half = c // 2, c & 1
    g = lambda k: np.asarray(inp[k])
    hb = g("L0_attn_hidden_states")[b]
    if half == 0:
        gu_rows = np.r_[0:2816, 5632:8448]
    else:
        gu_rows = np.r_[2816:5632, 8448:11264]
    d = {
        "h_my": np.ascontiguousarray(hb),
        "x_res": np.ascontiguousarray(hb[half * 1024:(half + 1) * 1024]),
        "wq_t": _tile_w(g("L0_attn_Wq"), half),
        "wk_t": _tile_w(g("L0_attn_Wk"), half),
        "wv_t": _tile_w(g("L0_attn_Wv"), half),
        "ww_t": _tile_w(g("L0_attn_Ww"), half),
        "wo_t": np.ascontiguousarray(
            g("L0_attn_Wo").reshape(16, 128, 16, 128).transpose(2, 1, 0, 3).reshape(16, 128, 2048)),
        "wf1_t": np.ascontiguousarray(
            g("L0_attn_Wf1").reshape(16, 128, 128).transpose(1, 0, 2).reshape(128, 2048)),
        "wg1_t": np.ascontiguousarray(
            g("L0_attn_Wg1").reshape(16, 128, 128).transpose(1, 0, 2).reshape(128, 2048)),
        "wf2_h": np.ascontiguousarray(g("L0_attn_Wf2")[:, half * 1024:(half + 1) * 1024]),
        "wg2_h": np.ascontiguousarray(g("L0_attn_Wg2")[:, half * 1024:(half + 1) * 1024]),
        "wb_t": np.ascontiguousarray(
            g("L0_attn_Wb")[:, half * 8:(half + 1) * 8].reshape(16, 128, 8)
            .transpose(1, 0, 2).reshape(128, 128)),
        "conv_t": np.ascontiguousarray(np.stack(
            [g("L0_attn_cq")[half * 1024:(half + 1) * 1024].reshape(8, 128, 4).transpose(1, 0, 2),
             g("L0_attn_ck")[half * 1024:(half + 1) * 1024].reshape(8, 128, 4).transpose(1, 0, 2),
             g("L0_attn_cv")[half * 1024:(half + 1) * 1024].reshape(8, 128, 4).transpose(1, 0, 2)],
            axis=1).reshape(128, 96)),
        "bg2_my": np.ascontiguousarray(g("L0_attn_bg2")[half * 1024:(half + 1) * 1024][None, :]),
        "nw_col": np.ascontiguousarray(g("L0_attn_norm_w")[:, None]),
        "gup_pk": np.ascontiguousarray(g("L0_ffn_gate_up_packed")[gu_rows]),
        "gup_sc": np.ascontiguousarray(
            g("L0_ffn_gate_up_scales").T[gu_rows].reshape(44, 128, 16)
            .transpose(1, 0, 2).reshape(128, 704)),
        "dn_pk": np.ascontiguousarray(g("L0_ffn_down_packed")[half * 1024:(half + 1) * 1024]),
        "dn_sc": np.ascontiguousarray(
            g("L0_ffn_down_scales").T[half * 1024:(half + 1) * 1024].reshape(8, 128, 44)
            .transpose(1, 0, 2).reshape(128, 352)),
        "nonce": nonce,
    }
    d.update(consts)
    return d


class Sched:
    def __init__(self):
        self.ops = []
        self.cnt = {s: 0 for s in ALL_SEMS}
        self.bank_w = [None] * 8
        self.bank_r = [None] * 8

    def op(self, eng, emit_fn, waits=(), inc=None, inc_amt=0, manual=False):
        tokv = None
        if inc is not None:
            self.cnt[inc] += inc_amt
            tokv = (inc, self.cnt[inc])
        self.ops.append((eng, tuple(w for w in waits if w is not None), emit_fn,
                         None if (manual or inc is None) else (inc, inc_amt)))
        return tokv

    def dma(self, eng, dst, src, waits=(), stream=None):
        assert stream in DMA_STREAMS, stream
        return self.op(eng, lambda E: E.e.dma_start(dst, src), waits, stream, 16)

    def eop(self, eng, emit_fn, waits=()):
        return self.op(eng, emit_fn, waits, ESEM[eng], 1)

    def bw(self, i, tok):
        self.bank_w[i] = tok

    def br_(self, i, tok):
        self.bank_r[i] = tok

    def need_w(self, i):
        return self.bank_r[i]

    def need_r(self, i):
        return self.bank_w[i]


class Eng:
    def __init__(self, engine, sems):
        self.e = engine
        self.sems = sems
        self.waited = {}

    def wait(self, tokv):
        s, v = tokv
        if self.waited.get(s, 0) >= v:
            return
        self.e.wait_ge(self.sems[s], v)
        self.waited[s] = v


def build_program(debug=(), no_poll=False, sim_local=False, stop_after="full"):
    nc = bass.Bass("TRN2", target_bir_lowering=False)
    S = Sched()

    din = {}
    for name, shape, dt in [
        ("h_my", [2048, 2048], F32), ("x_res", [1024, 2048], F32),
        ("wq_t", [8, 128, 2048], F32), ("wk_t", [8, 128, 2048], F32),
        ("wv_t", [8, 128, 2048], F32), ("ww_t", [8, 128, 2048], F32),
        ("wo_t", [16, 128, 2048], F32),
        ("wf1_t", [128, 2048], F32), ("wg1_t", [128, 2048], F32),
        ("wf2_h", [128, 1024], F32), ("wg2_h", [128, 1024], F32),
        ("wb_t", [128, 128], F32), ("conv_t", [128, 96], F32),
        ("bg2_my", [1, 1024], F32), ("nw_col", [128, 1], F32),
        ("gup_pk", [5632, 256], I32), ("gup_sc", [128, 704], F16),
        ("dn_pk", [1024, 704], I32), ("dn_sc", [128, 352], F16),
        ("nonce", [1, 4], I32),
        ("relm", [128, 136], F32), ("cbf16", [128, 384], BF16),
    ]:
        din[name] = nc.dram_tensor(name, shape, dt, kind="ExternalInput").ap()
    out_my = nc.dram_tensor("out_my", [1024, 2048], F32, kind="ExternalOutput").ap()

    dbg = {}

    def dbg_dump(name, shape, dt, src_ap, wait_toks):
        if name not in debug:
            return None
        dbg[name] = nc.dram_tensor("dbg_" + name, list(shape), dt, kind="ExternalOutput").ap()
        return S.dma("sync", dbg[name], src_ap, wait_toks, stream="s_dbg")

    _aspace = "Local" if sim_local else "Shared"
    sh_wgu = nc.dram_tensor("sh_wgu", [88, 128, 2048], BF16, addr_space=_aspace).ap()
    sh_wdn = nc.dram_tensor("sh_wdn", [16, 128, NIK * 128], BF16, addr_space=_aspace).ap()
    sh_o = nc.dram_tensor("sh_o", [2, 1024, 2048], BF16, addr_space=_aspace).ap()
    sh_flags = nc.dram_tensor("sh_flags", [2, 16], I32, addr_space=_aspace).ap()
    ldq_gu = nc.dram_tensor("ldq_gu", [44, 128, 2048], BF16).ap()
    ldq_dn = nc.dram_tensor("ldq_dn", [8, 128, NIK * 128], BF16).ap()

    big = contextlib.ExitStack()

    def sbuf(st, name, shape, dt):
        return st.enter_context(nc.sbuf_tensor(name, shape, dt))

    fT = sbuf(big, "fT", [128, 2048], BF16)
    fgT = sbuf(big, "fgT", [128, 2048], BF16)
    wf2s = sbuf(big, "wf2s", [128, 1024], BF16)
    wg2s = sbuf(big, "wg2s", [128, 1024], BF16)
    bg2s = sbuf(big, "bg2s", [1, 1024], BF16)
    nwc = sbuf(big, "nwc", [128, 1], F32)
    relmS = sbuf(big, "relmS", [128, 136], F32)
    cbS = sbuf(big, "cbS", [128, 384], BF16)
    noncS = sbuf(big, "noncS", [1, 4], I32)
    bbuf = sbuf(big, "bbuf", [128, NCH * 8], F32)
    convS = sbuf(big, "convS", [128, 96], F32)
    wbS = sbuf(big, "wbS", [128, 128], BF16)
    scg = sbuf(big, "scg", [128, 704], F32)
    scd = sbuf(big, "scd", [128, 352], F32)
    flagw = sbuf(big, "flagw", [1, 4], I32)
    flagr = sbuf(big, "flagr", [1, 32], I32)

    pb = [big.enter_context(nc.psum_tensor(f"pb{i}", [128, 512], F32)) for i in range(8)]
    sems = {s: big.enter_context(nc.semaphore(s)) for s in ALL_SEMS}
    blk = big.enter_context(nc.Block())

    def q128(bank, quarter):
        return pb[bank][:, quarter * 128:(quarter + 1) * 128]

    nc.const_aps.register(F32, 1e-6, relmS[:, 132:133])
    nc.const_aps.register(F32, 1e-5, relmS[:, 133:134])
    nc.const_aps.register(F32, float(np.log(0.5) + 0.5 * np.log(128.0)), relmS[:, 134:135])
    IDN = cbS[:, 0:128]
    MSK = cbS[:, 128:256]
    ONE = cbS[:, 256:384]

    t_relm = S.dma("sync", relmS[:, :], din["relm"][:, :], stream="s_p0")
    t_cb = S.dma("sync", cbS[:, :], din["cbf16"][:, :], stream="s_p0")
    t_non = S.dma("sync", noncS[:, :], din["nonce"][:, :], stream="s_p0")
    t_conv = S.dma("sync", convS[:, :], din["conv_t"][:, :], stream="s_p0")
    t_bg2 = S.dma("gpsimd", bg2s[:, :], din["bg2_my"][:, :], stream="g_p0")
    t_nwc = S.dma("sync", nwc[:, :], din["nw_col"][:, :], stream="s_p0")
    t_wf2 = S.dma("gpsimd", wf2s[:, :], din["wf2_h"][:, :], stream="g_p0")
    t_wg2 = S.dma("gpsimd", wg2s[:, :], din["wg2_h"][:, :], stream="g_p0")
    t_wb = S.dma("gpsimd", wbS[:, :], din["wb_t"][:, :], stream="g_p0")
    t_scg = S.dma("gpsimd", scg[:, :], din["gup_sc"][:, :], stream="g_p0")
    t_scd = S.dma("gpsimd", scd[:, :], din["dn_sc"][:, :], stream="g_p0")
    t_relm = t_cb = t_non = t_conv = t_nwc = ("s_p0", S.cnt["s_p0"])
    t_bg2 = t_wf2 = t_wg2 = t_wb = t_scg = t_scd = ("g_p0", S.cnt["g_p0"] + 16)
    t_scg2 = S.eop("vector", lambda E: E.e.tensor_scalar(
        scg[:, :], scg[:, :], float(2.0 ** 126), None, ALU.mult), [t_scg])
    t_scd2 = S.eop("vector", lambda E: E.e.tensor_scalar(
        scd[:, :], scd[:, :], float(2.0 ** 126), None, ALU.mult), [t_scd])
    t_flag1 = S.eop("vector", lambda E: E.e.tensor_scalar(
        flagw[0:1, 1:2], noncS[0:1, 0:1], 1, None, ALU.add), [t_non])
    t_flag2 = S.eop("vector", lambda E: E.e.tensor_scalar(
        flagw[0:1, 2:3], noncS[0:1, 0:1], 2, None, ALU.add), [t_non])

    p23 = contextlib.ExitStack()
    hT = sbuf(p23, "hT", [128, 16 * 2048], BF16)
    wtile = [sbuf(p23, f"wtile{i}", [128, 4 * 2048], BF16) for i in range(2)]
    pre = sbuf(p23, "pre", [128, 2051], BF16)
    qp = sbuf(p23, "qp", [128, 2048], BF16)
    kp = sbuf(p23, "kp", [128, 2048], BF16)
    vp = sbuf(p23, "vp", [128, 2048], BF16)
    wp = sbuf(p23, "wp", [128, 2048], BF16)
    lsb = sbuf(p23, "lsb", [128, 16 * 128], F32)
    zscr = sbuf(p23, "zscr", [128, 512], F32)
    ccol = sbuf(p23, "ccol", [128, NCH], F32)
    diag = sbuf(p23, "diag", [128, 4 * 128], BF16)
    ostg = sbuf(p23, "ostg", [128, 2048], BF16)
    thbuf = sbuf(p23, "thbuf", [128, 2048], BF16)
    sktS = sbuf(p23, "sktS", [128, HL * 128], BF16)
    svtS = sbuf(p23, "svtS", [128, HL * 128], BF16)
    lrelT = sbuf(p23, "lrelT", [128, 256], BF16)
    emtB = sbuf(p23, "emtB", [128, 128], BF16)
    lme3 = sbuf(p23, "lme3", [128, 4], F32)
    upT = sbuf(p23, "upT", [128, 256], BF16)
    uu = sbuf(p23, "uu", [128, 256], BF16)
    ktd = sbuf(p23, "ktd", [128, 256], BF16)
    vtd = sbuf(p23, "vtd", [128, 256], BF16)
    sk0s = sbuf(p23, "sk0s", [128, 128], BF16)
    psmB = sbuf(p23, "psmB", [128, 256], BF16)
    scrb = sbuf(p23, "scrb", [128, 128], BF16)
    ptil = sbuf(p23, "ptil", [128, 128], BF16)
    appB = sbuf(p23, "appB", [128, 128], BF16)
    appT = sbuf(p23, "appT", [128, 128], BF16)
    w2mB = sbuf(p23, "w2mB", [128, 128], BF16)
    denom = sbuf(p23, "denom", [128, 1], F32)
    o2s = sbuf(p23, "o2s", [128, 1], F32)
    rcol = sbuf(p23, "rcol", [128, 4], F32)
    sqs = sbuf(p23, "sqs", [128, 128], BF16)
    omb = sbuf(p23, "omb", [128, 128], BF16)
    ofin = sbuf(p23, "ofin", [128, 128], BF16)
    sktsB = sbuf(p23, "sktsB", [128, 128], BF16)
    svtsB = sbuf(p23, "svtsB", [128, 128], BF16)
    _pkb1 = sbuf(p23, "pkb0", [128, 704], I32)
    pkb = [_pkb1, _pkb1]
    inib = sbuf(p23, "inib", [128, 1408], I32)
    iv3 = sbuf(p23, "iv3", [128, 1408], I32)
    dqt = sbuf(p23, "dqt", [128, 2 * 1408], BF16)
    stg = sbuf(p23, "stg", [128, 2 * 1408], BF16)

    hsc = contextlib.ExitStack()
    _htmp321 = sbuf(hsc, "htmp32_0", [128, 2048], F32)
    htmp32 = [_htmp321, _htmp321]
    _htmpb1 = sbuf(hsc, "htmpb_0", [128, 2048], BF16)
    htmpb = [_htmpb1, _htmpb1]
    wf1s = sbuf(hsc, "wf1s", [128, 2048], BF16)
    wg1s = wf1s
    t_wf1 = S.dma("gpsimd", wf1s[:, :], din["wf1_t"][:, :], stream="g_p0")
    dc_tok = {}
    ev_last = {}
    for tb in range(16):
        pp = tb % 2
        ld = S.dma("sync", htmp32[pp][:, :], din["h_my"][tb * 128:(tb + 1) * 128, :],
                   [dc_tok.get(tb - 1)], stream="s_h")
        dc_tok[tb] = S.eop("vector", (lambda pp: lambda E: E.e.tensor_copy(
            htmpb[pp][:, :], htmp32[pp][:, :]))(pp), [ld, ev_last.get(tb - 1)])
        for g4 in range(4):
            bank = g4 % 2
            def mk_tr(pp=pp, g4=g4, bank=bank):
                def f(E):
                    mm = None
                    for j in range(4):
                        db = g4 * 4 + j
                        mm = E.e.matmul(pb[bank][:, j * 128:(j + 1) * 128],
                                        htmpb[pp][:, db * 128:(db + 1) * 128],
                                        IDN, start=True, stop=True)
                    return mm
                return f
            tk = S.eop("tensor", mk_tr(), [dc_tok[tb], t_cb, S.need_w(bank)])
            S.bw(bank, tk)
            def mk_ev(tb=tb, g4=g4, bank=bank):
                def f(E):
                    dst = hT[:, :].rearrange("p (kb t) -> p kb t", t=2048)[
                        :, g4 * 4:(g4 + 1) * 4, tb * 128:(tb + 1) * 128]
                    src = pb[bank][:, :].rearrange("p (a b) -> p a b", b=128)
                    return E.e.activation(dst, src, AF.Copy)
                return f
            evt = S.eop("scalar", mk_ev(), [tk])
            S.br_(bank, evt)
        ev_last[tb] = evt
    t_hT = ev_last[15]

    t_fT_first = None
    for widx, (wsb, dstT) in enumerate([(wf1s, fT), (wg1s, fgT)]):
        if widx == 0:
            t_w = t_wf1
        else:
            t_w = S.dma("gpsimd", wg1s[:, :], din["wg1_t"][:, :], [t_fT_first], stream="g_p0")
        for tc in range(4):
            bank = 2 + (tc % 2)
            def mk_f(wsb=wsb, tc=tc, bank=bank):
                def f(E):
                    mm = None
                    for kbi in range(16):
                        mm = E.e.matmul(pb[bank][:, :], wsb[:, kbi * 128:(kbi + 1) * 128],
                                        hT[:, kbi * 2048 + tc * 512: kbi * 2048 + (tc + 1) * 512],
                                        start=(kbi == 0), stop=(kbi == 15))
                    return mm
                return f
            tk = S.eop("tensor", mk_f(), [t_w, t_hT, S.need_w(bank)])
            S.bw(bank, tk)
            evt = S.eop("scalar", (lambda dstT=dstT, tc=tc, bank=bank:
                                   lambda E: E.e.activation(
                                       dstT[:, tc * 512:(tc + 1) * 512], pb[bank][:, :],
                                       AF.Copy))(), [tk])
            S.br_(bank, evt)
        if widx == 0:
            t_fT_first = tk
    t_fT = evt
    hsc.close()

    t_beta = None
    for ch in range(NCH):
        def mk_b(ch=ch):
            def f(E):
                mm = None
                for kbi in range(16):
                    mm = E.e.matmul(pb[4][:, 0:8],
                                    hT[:, kbi * 2048 + ch * 128: kbi * 2048 + (ch + 1) * 128],
                                    wbS[:, kbi * 8:(kbi + 1) * 8],
                                    start=(kbi == 0), stop=(kbi == 15))
                return mm
            return f
        tk = S.eop("tensor", mk_b(), [t_wb, t_hT, S.need_w(4)])
        S.bw(4, tk)
        t_beta = S.eop("scalar", (lambda ch=ch: lambda E: E.e.activation(
            bbuf[:, ch * 8:(ch + 1) * 8], pb[4][:, 0:8], AF.Sigmoid))(), [tk])
        S.br_(4, t_beta)

    dq = {}

    def _dq_params(kind):
        if kind == "gu":
            return din["gup_pk"], scg, t_scg2, 256, 2, ldq_gu
        return din["dn_pk"], scd, t_scd2, 704, 4, ldq_dn

    dq_passes = [("gu", o, q) for o in range(44) for q in range(2)] + \
                [("dn", o, q) for o in range(8) for q in range(4)]
    dq_state = {"a": 0, "pending": [], "gu_last": None, "dn_last": None,
                "last_val": None, "dqt_tr": [None, None], "stg_dma": [None, None],
                "par": {}}

    def emit_dq_A(kind, o, q):
        pk_ap, sc_sb, t_sc, wrow, npass, _ = _dq_params(kind)
        par = len(dq_state["par"]) % 2
        dq_state["par"][(kind, o, q)] = par
        base = par * 1408
        if q == 0:
            guard = dq.get((kind, "pkread", o - 1)) if o > 0 else \
                (dq.get(("gu", "pkread", 43)) if kind == "dn" else None)
            dq[(kind, "pk", o)] = S.dma(
                "sync", pkb[0][:, 0:wrow], pk_ap[o * 128:(o + 1) * 128, :],
                [guard], stream="s_pk")
        t_pk = dq[(kind, "pk", o)]
        wpp = wrow // npass
        nin = wpp * 8

        def mk_v3(q=q, wpp=wpp):
            def f(E):
                last = None
                for i in range(8):
                    sh = 22 - 4 * i
                    op = ALU.logical_shift_left if sh >= 0 else ALU.logical_shift_right
                    last = E.e.tensor_scalar(iv3[:, i * wpp:(i + 1) * wpp],
                                             pkb[0][:, q * wpp:(q + 1) * wpp],
                                             abs(sh), 7 << 22, op, ALU.bitwise_and)
                return last
            return f
        t_v3 = S.eop("vector", mk_v3(), [t_pk, dq_state["last_val"]])

        def mk_gs1(q=q, wpp=wpp):
            def f(E):
                last = None
                for i in range(8):
                    last = E.e.tensor_scalar(inib[:, i * wpp:(i + 1) * wpp],
                                             pkb[0][:, q * wpp:(q + 1) * wpp],
                                             28 - 4 * i, -2147483648,
                                             ALU.logical_shift_left, ALU.bitwise_and)
                return last
            return f
        t_g = S.eop("vector", mk_gs1(), [t_pk, dq_state["last_val"]])
        t_gn = t_g

        def mk_vb(wpp=wpp):
            def f(E):
                last = None
                for i in range(8):
                    last = E.e.tensor_tensor(iv3[:, i * wpp:(i + 1) * wpp],
                                             iv3[:, i * wpp:(i + 1) * wpp],
                                             inib[:, i * wpp:(i + 1) * wpp], ALU.add)
                return last
            return f
        t_v = S.eop("gpsimd", mk_vb(), [t_v3, t_g])
        dq[(kind, "pkread", o)] = t_gn
        ng = wpp // 16

        def mk_val(o=o, q=q, nin=nin, ng=ng, base=base, wrow=wrow, sc_sb=sc_sb):
            def f(E):
                src = iv3[:, 0:nin].bitcast(F32).rearrange(
                    "p (i g w) -> p i g w", g=ng, w=16)
                scs = sc_sb[:, o * (wrow // 16) + q * ng: o * (wrow // 16) + (q + 1) * ng] \
                    .unsqueeze(1).unsqueeze(3).broadcast_to([128, 8, ng, 16])
                dst = dqt[:, base:base + nin].rearrange("p (g w i) -> p i g w", w=16, i=8)
                return E.e.tensor_tensor(dst, src, scs, ALU.mult)
            return f
        t_val = S.eop("vector", mk_val(), [t_v, t_sc, dq_state["dqt_tr"][par]])
        dq[(kind, "int", o, q)] = t_val
        dq_state["last_val"] = t_val

    def emit_dq_B(kind, o, q):
        _, _, _, wrow, npass, ldq_dst = _dq_params(kind)
        par = dq_state["par"][(kind, o, q)]
        base = par * 1408
        wpp = wrow // npass
        nin = wpp * 8
        nikp = nin // 128
        t_val = dq[(kind, "int", o, q)]
        t_ev = None
        t_trlast = None
        for g4 in range((nikp + 3) // 4):
            bank = 2 + (g4 % 2)
            nj = min(4, nikp - g4 * 4)

            def mk_tr(g4=g4, bank=bank, nj=nj, base=base):
                def f(E):
                    mm = None
                    for j in range(nj):
                        ikb = g4 * 4 + j
                        lhs = dqt[:, base + ikb * 128:base + (ikb + 1) * 128]
                        mm = E.e.matmul(pb[bank][:, j * 128:(j + 1) * 128],
                                        lhs, IDN, start=True, stop=True)
                    return mm
                return f
            tk = S.eop("tensor", mk_tr(), [t_val, S.need_w(bank)])
            S.bw(bank, tk)
            t_trlast = tk

            def mk_ev(g4=g4, bank=bank, nj=nj, base=base):
                def f(E):
                    sb = base + (g4 * 4) * 128
                    return E.e.activation(stg[:, sb: sb + nj * 128],
                                          pb[bank][:, 0:nj * 128], AF.Copy)
                return f
            evt = S.eop("scalar", mk_ev(), [tk, dq_state["stg_dma"][par]])
            S.br_(bank, evt)
            t_ev = evt
        dq_state["dqt_tr"][par] = t_trlast
        last_dma = S.dma("scalar",
                         ldq_dst[o, :, q * nikp * 128:(q + 1) * nikp * 128],
                         stg[:, base:base + nikp * 128], [t_ev], stream="a_ldq")
        dq_state["stg_dma"][par] = last_dma
        dq_state["gu_last" if kind == "gu" else "dn_last"] = last_dma

    def emit_dq_slot():
        for p in dq_state["pending"]:
            emit_dq_B(*p)
        dq_state["pending"] = []
        for _ in range(2):
            if dq_state["a"] < len(dq_passes):
                pp = dq_passes[dq_state["a"]]
                dq_state["a"] += 1
                emit_dq_A(*pp)
                dq_state["pending"].append(pp)

    def flush_dq():
        while dq_state["pending"] or dq_state["a"] < len(dq_passes):
            emit_dq_slot()

    def mk_dqcopy(E):
        last = None
        for arm in nc.switch_core_id(engines=[E.e], n=128):
            half = (arm.logical % 8) & 1
            if half == 0:
                i1 = E.e.dma_start(sh_wgu[0:22, :, :], ldq_gu[0:22, :, :])
                i2 = E.e.dma_start(sh_wgu[44:66, :, :], ldq_gu[22:44, :, :])
                i3 = E.e.dma_start(sh_wdn[0:8, :, :], ldq_dn[:, :, :])
            else:
                i1 = E.e.dma_start(sh_wgu[22:44, :, :], ldq_gu[0:22, :, :])
                i2 = E.e.dma_start(sh_wgu[66:88, :, :], ldq_gu[22:44, :, :])
                i3 = E.e.dma_start(sh_wdn[8:16, :, :], ldq_dn[:, :, :])
            for i in (i1, i2, i3):
                i.then_inc(E.sems["a_cp"], 16)
            last = i3
        return last

    def mk_flag(stage):
        def f(E):
            last = None
            for arm in nc.switch_core_id(engines=[E.e], n=128):
                slot = (arm.logical % 8) & 1
                i = E.e.dma_start(sh_flags[slot:slot + 1, stage:stage + 1],
                                  flagw[0:1, stage:stage + 1])
                i.then_inc(E.sems["a_flg"], 16)
                last = i
            return last
        return f

    if stop_after == "deq":
        flush_dq()

    prev = {}
    _heads = HL if stop_after != "deq" else 0
    _heads = min(_heads, DBG_HEADS)
    _nch = min(NCH, DBG_CHUNKS)
    for hl in range(_heads):
        ping = hl % 2
        t_wld = []
        for ki, wname in enumerate(["wq_t", "wk_t", "wv_t", "ww_t"]):
            t_wld.append(S.dma("gpsimd", wtile[ping][:, ki * 2048:(ki + 1) * 2048],
                               din[wname][hl, :, :], [prev.get(("scan", hl - 2))],
                               stream=f"g_w{ping}"))
        if hl == 0:
            t_pad = S.eop("vector", lambda E: E.e.memset(pre[:, 0:3], 0.0), [])
        else:
            t_pad = None

        post_sb = {0: qp, 1: kp, 2: vp, 3: wp}
        kind3_of = {0: 0, 1: 1, 2: 2, 3: 2}
        t_post = {}
        t_convprev = None
        for ki in range(4):
            k3 = kind3_of[ki]
            def mk_diag(hl=hl, k3=k3):
                def f(E):
                    i = None
                    for tap in range(4):
                        i = E.e.tensor_scalar(
                            diag[:, tap * 128:(tap + 1) * 128],
                            IDN, convS[:, k3 * 32 + hl * 4 + tap: k3 * 32 + hl * 4 + tap + 1],
                            None, ALU.mult)
                    return i
                return f
            t_diag = S.eop("vector", mk_diag(), [t_conv, t_cb, t_convprev,
                                                 prev.get(("scan", hl - 1))])
            for tc in range(4):
                bank = tc % 2
                def mk_p(hl=hl, ki=ki, tc=tc, bank=bank, ping=ping):
                    def f(E):
                        mm = None
                        for kbi in range(16):
                            mm = E.e.matmul(
                                pb[bank][:, :],
                                wtile[ping][:, ki * 2048 + kbi * 128: ki * 2048 + (kbi + 1) * 128],
                                hT[:, kbi * 2048 + tc * 512: kbi * 2048 + (tc + 1) * 512],
                                start=(kbi == 0), stop=(kbi == 15))
                        return mm
                    return f
                tk = S.eop("tensor", mk_p(), [t_wld[3], t_hT, S.need_w(bank)])
                S.bw(bank, tk)
                def mk_ev(ki=ki, tc=tc, bank=bank):
                    def f(E):
                        return E.e.activation(
                            pre[:, 3 + tc * 512: 3 + (tc + 1) * 512],
                            pb[bank][:, :], AF.Copy)
                    return f
                evt = S.eop("scalar", mk_ev(), [tk, prev.get(("scan", hl - 1)), t_pad])
                S.br_(bank, evt)
                cbank = 2 + (tc % 2)
                def mk_c(ki=ki, tc=tc, cbank=cbank, hl=hl):
                    def f(E):
                        mm = None
                        for tap in range(4):
                            mm = E.e.matmul(
                                pb[cbank][:, :],
                                diag[:, tap * 128:(tap + 1) * 128],
                                pre[:, tap + tc * 512: tap + (tc + 1) * 512],
                                start=(tap == 0), stop=(tap == 3))
                        return mm
                    return f
                ctk = S.eop("tensor", mk_c(), [evt, t_diag, S.need_w(cbank)])
                S.bw(cbank, ctk)
                def mk_s(ki=ki, tc=tc, cbank=cbank):
                    def f(E):
                        return E.e.activation(post_sb[ki][:, tc * 512:(tc + 1) * 512],
                                              pb[cbank][:, :], AF.Silu)
                    return f
                st = S.eop("scalar", mk_s(), [ctk, prev.get(("scan", hl - 1)),
                                              prev.get(("dbgall", hl - 1))])
                S.br_(cbank, st)
                t_post[(ki, tc)] = st
                t_convprev = ctk
                if tc in (0, 2):
                    emit_dq_slot()
        t_qs = S.eop("vector", lambda E: E.e.tensor_scalar(
            qp[:, :], qp[:, :], float(SCALE), None, ALU.mult),
            [t_post[(0, tc2)] for tc2 in range(4)])

        t_ls = {}
        for g4 in range(4):
            def mk_z(hl=hl, g4=g4):
                def f(E):
                    mm = None
                    for j in range(4):
                        ch = g4 * 4 + j
                        mm = E.e.matmul(pb[4][:, j * 128:(j + 1) * 128],
                                        fT[:, ch * 128:(ch + 1) * 128],
                                        wf2s[:, hl * 128:(hl + 1) * 128],
                                        start=True, stop=True)
                    return mm
                return f
            ztk = S.eop("tensor", mk_z(), [t_fT, t_wf2, S.need_w(4),
                                           prev.get(("ls", hl - 1))])
            S.bw(4, ztk)
            t_a = S.eop("scalar", (lambda g4=g4: lambda E: E.e.activation(
                lsb[:, g4 * 512:(g4 + 1) * 512], pb[4][:, :], AF.Abs))(),
                [ztk, prev.get(("dbgall", hl - 1))])
            t_b = S.eop("scalar", (lambda g4=g4: lambda E: E.e.activation(
                lsb[:, g4 * 512:(g4 + 1) * 512], lsb[:, g4 * 512:(g4 + 1) * 512],
                AF.Exp, scale=-1.0))(), [t_a])
            t_c = S.eop("scalar", (lambda g4=g4: lambda E: E.e.activation(
                lsb[:, g4 * 512:(g4 + 1) * 512], lsb[:, g4 * 512:(g4 + 1) * 512],
                AF.Ln, bias=1.0))(), [t_b])
            t_rl = S.eop("scalar", (lambda g4=g4: lambda E: E.e.activation(
                zscr[:, :], pb[4][:, :], AF.Relu, scale=-1.0))(),
                [t_c, (t_ls[g4 - 1] if g4 > 0 else None)])
            S.br_(4, t_rl)
            t_ls[g4] = S.eop("vector", (lambda g4=g4: lambda E: E.e.tensor_tensor(
                lsb[:, g4 * 512:(g4 + 1) * 512], lsb[:, g4 * 512:(g4 + 1) * 512],
                zscr[:, :], ALU.add))(), [t_rl, t_c])

        t_thh = None
        for g4 in range(4):
            def mk_gh(hl=hl, g4=g4):
                def f(E):
                    mm = None
                    for j in range(4):
                        ch = g4 * 4 + j
                        E.e.matmul(pb[3][:, j * 128:(j + 1) * 128],
                                   fgT[:, ch * 128:(ch + 1) * 128],
                                   wg2s[:, hl * 128:(hl + 1) * 128],
                                   start=True, stop=False)
                        mm = E.e.matmul(pb[3][:, j * 128:(j + 1) * 128],
                                        ONE[0:1, 0:128],
                                        bg2s[0:1, hl * 128:(hl + 1) * 128],
                                        start=False, stop=True)
                    return mm
                return f
            gtk = S.eop("tensor", mk_gh(), [t_fT, t_wg2, t_bg2, S.need_w(3),
                                            prev.get(("thb", hl - 1))])
            S.bw(3, gtk)
            t_thh = S.eop("scalar", (lambda g4=g4: lambda E: E.e.activation(
                thbuf[:, g4 * 512:(g4 + 1) * 512], pb[3][:, :], AF.Tanh, scale=0.5))(), [gtk])
            S.br_(3, t_thh)
        prev[("thb", hl)] = None

        t_w2c = []
        for ch in range(NCH):
            t_w2c.append(S.eop("scalar", (lambda ch=ch: lambda E: E.e.activation(
                sqs[:, :], wp[:, ch * 128:(ch + 1) * 128], AF.Square))(),
                [t_post[(3, ch // 4)], prev.get(("sw", ch - 1, hl)), prev.get(("sqs", hl - 1))]))
            def mk_sw(ch=ch):
                def f(E):
                    return E.e.matmul(pb[3][:, 256 + ch:256 + ch + 1], sqs[:, :],
                                      ONE[:, 0:1], start=True, stop=True)
                return f
            tk = S.eop("tensor", mk_sw(), [t_w2c[-1], S.need_w(3)])
            prev[("sw", ch, hl)] = tk
        S.bw(3, tk)
        t_sq = S.eop("scalar", lambda E: E.e.activation(
            ccol[:, :], pb[3][:, 256:256 + NCH], AF.Ln, bias=1e-6),
            [tk, prev.get(("dbgall", hl - 1))])
        S.br_(3, t_sq)
        t_rc1 = S.eop("scalar", lambda E: E.e.activation(
            ccol[:, :], ccol[:, :], AF.Exp, scale=-0.5), [t_sq])
        def mk_cc(hl=hl):
            def f(E):
                bsl = bbuf[:, :].rearrange("p (c h) -> p c h", h=8)[:, :, hl]
                return E.e.tensor_tensor(ccol[:, :], ccol[:, :], bsl, ALU.mult)
            return f
        t_cc = S.eop("vector", mk_cc(), [t_rc1, t_beta])

        if hl == 0:
            t_w2z = S.eop("vector", lambda E: E.e.memset(w2mB[:, :], 0.0), [])
        t_st0 = S.eop("vector", (lambda hl=hl: lambda E: E.e.memset(
            sktS[:, hl * 128:(hl + 1) * 128], 0.0))(), [prev.get(("scan", hl - 1))])
        t_sv0 = S.eop("vector", (lambda hl=hl: lambda E: E.e.memset(
            svtS[:, hl * 128:(hl + 1) * 128], 0.0))(), [t_st0])

        P = {}

        def mb(ch, b):
            return b if ch % 2 == 0 else b - 4

        def qm(ch, b, quarter):
            return q128(mb(ch, b), quarter)

        def F(ch):
            sl0, sl1 = ch * 128, (ch + 1) * 128
            b4 = mb(ch, 4)
            par2 = (ch % 2) * 2
            pp0, pp1 = (ch % 2) * 128, (ch % 2) * 128 + 128
            def mk_rel(b4=b4, sl0=sl0, sl1=sl1):
                def f(E):
                    E.e.matmul(pb[b4][:, 0:128], relmS[:, 0:128], lsb[:, sl0:sl1],
                               start=True, stop=True)
                    return E.e.matmul(pb[b4][:, 128:258], lsb[:, sl0:sl1], relmS[:, 0:130],
                                      start=True, stop=True)
                return f
            t_rel = S.eop("tensor", mk_rel(), [t_ls[ch // 4], t_relm, t_cc,
                                               P.get(("svna", ch - 2)),
                                               (S.need_w(b4) if ch < 2 else None)])
            t_lrel = S.eop("scalar", (lambda b4=b4, pp0=pp0, pp1=pp1: lambda E: E.e.activation(
                lrelT[:, pp0:pp1], pb[b4][:, 0:128], AF.Exp, scale=-1.0))(),
                [t_rel, P.get(("app", ch - 2))])
            t_emt = S.eop("scalar", (lambda b4=b4: lambda E: E.e.activation(
                emtB[:, :], pb[b4][:, 128:256], AF.Exp))(),
                [t_rel, P.get(("up", ch - 1))])
            t_lme = S.eop("scalar", (lambda b4=b4, par2=par2: lambda E: E.e.activation(
                lme3[:, par2:par2 + 2], pb[b4][:, 256:258], AF.Exp, scale=-1.0))(),
                [t_rel, P.get(("svn", ch - 2))])
            def mk_up(sl0=sl0, sl1=sl1, pp0=pp0, pp1=pp1):
                def f(E):
                    return E.e.tensor_tensor(upT[:, pp0:pp1], wp[:, sl0:sl1],
                                             emtB[:, :], ALU.mult)
                return f
            t_up = S.eop("vector", mk_up(), [t_emt, t_post[(3, ch // 4)],
                                             P.get(("w2mm", ch - 2))])
            t_utr = S.eop("tensor", (lambda ch=ch, pp0=pp0, pp1=pp1: lambda E: E.e.matmul(
                qm(ch, 6, 0), upT[:, pp0:pp1], IDN, start=True, stop=True))(),
                [t_up, P.get(("w2m", ch - 2)),
                 (S.need_w(mb(ch, 6)) if ch < 2 else None)])
            def mk_uev(ch=ch, pp0=pp0, pp1=pp1):
                def f(E):
                    return E.e.tensor_scalar(uu[:, pp0:pp1], qm(ch, 6, 0),
                                             ccol[:, ch:ch + 1], None, ALU.mult)
                return f
            t_uev = S.eop("vector", mk_uev(), [t_utr, t_cc, P.get(("dmm", ch - 2))])
            def mk_tr2(ch=ch, sl0=sl0, sl1=sl1):
                def f(E):
                    E.e.matmul(qm(ch, 5, 0), kp[:, sl0:sl1], IDN, start=True, stop=True)
                    return E.e.matmul(qm(ch, 5, 1), vp[:, sl0:sl1], IDN, start=True, stop=True)
                return f
            t_tr2 = S.eop("tensor", mk_tr2(), [t_post[(1, ch // 4)], t_post[(2, ch // 4)],
                                               P.get(("appe", ch - 2)),
                                               (S.need_w(mb(ch, 5)) if ch < 2 else None)])
            t_kev = S.eop("scalar", (lambda ch=ch, pp0=pp0, pp1=pp1: lambda E: E.e.activation(
                ktd[:, pp0:pp1], qm(ch, 5, 0), AF.Copy))(),
                [t_tr2, P.get(("dmm", ch - 2))])
            t_vev = S.eop("scalar", (lambda ch=ch, pp0=pp0, pp1=pp1: lambda E: E.e.activation(
                vtd[:, pp0:pp1], qm(ch, 5, 1), AF.Copy))(),
                [t_tr2, P.get(("omm", ch - 2))])
            def mk_ps(ch=ch, sl0=sl0, sl1=sl1):
                def f(E):
                    return E.e.matmul(qm(ch, 6, 1), kp[:, sl0:sl1], qp[:, sl0:sl1],
                                      start=True, stop=True)
                return f
            t_ps = S.eop("tensor", mk_ps(), [t_qs, t_post[(1, ch // 4)], t_uev])
            def mk_psm(ch=ch, pp0=pp0, pp1=pp1):
                def f(E):
                    return E.e.scalar_tensor_tensor(psmB[:, pp0:pp1], qm(ch, 6, 1),
                                                    ccol[:, ch:ch + 1], MSK,
                                                    ALU.mult, ALU.mult)
                return f
            t_psm = S.eop("vector", mk_psm(), [t_ps, t_cc, P.get(("i1", ch - 2))])
            P[("rel", ch)] = t_rel
            P[("lrel", ch)] = t_lrel
            P[("lme", ch)] = t_lme
            P[("up", ch)] = t_up
            P[("uev", ch)] = t_uev
            P[("kev", ch)] = t_kev
            P[("vev", ch)] = t_vev
            P[("psm", ch)] = t_psm

        def M(ch):
            sl0, sl1 = ch * 128, (ch + 1) * 128
            par2 = (ch % 2) * 2
            pp0, pp1 = (ch % 2) * 128, (ch % 2) * 128 + 128
            def mk_skts(hl=hl, par2=par2):
                def f(E):
                    return E.e.tensor_scalar(sktsB[:, :], sktS[:, hl * 128:(hl + 1) * 128],
                                             lme3[:, par2:par2 + 1], None, ALU.mult)
                return f
            t_skts = S.eop("vector", mk_skts(), [P[("lme", ch)], t_st0,
                                                 P.get(("skn", ch - 1))])
            def mk_svts(hl=hl, par2=par2):
                def f(E):
                    return E.e.tensor_scalar(svtsB[:, :], svtS[:, hl * 128:(hl + 1) * 128],
                                             lme3[:, par2:par2 + 1], None, ALU.mult)
                return f
            t_svts = S.eop("vector", mk_svts(), [P[("lme", ch)], t_sv0,
                                                 P.get(("svn", ch - 1))])
            b4 = mb(ch, 4)
            def mk_d12(ch=ch, b4=b4, pp0=pp0, pp1=pp1):
                def f(E):
                    E.e.matmul(pb[b4][:, 0:128], uu[:, pp0:pp1], ktd[:, pp0:pp1],
                               start=True, stop=True)
                    return E.e.matmul(pb[b4][:, 128:256], uu[:, pp0:pp1], vtd[:, pp0:pp1],
                                      start=True, stop=True)
                return f
            t_dmm = S.eop("tensor", mk_d12(), [P[("uev", ch)], P[("kev", ch)],
                                               P[("vev", ch)], P[("lme", ch)],
                                               P.get(("svna", ch - 2))])
            t_skn_a = S.eop("vector", (lambda hl=hl, ch=ch: lambda E: E.e.scalar_tensor_tensor(
                sktS[:, hl * 128:(hl + 1) * 128], qm(ch, 4, 0),
                1.0, sktsB[:, :], ALU.mult, ALU.add))(), [t_dmm, t_skts])
            t_skn = S.eop("vector", (lambda hl=hl, par2=par2: lambda E: E.e.tensor_scalar(
                sktS[:, hl * 128:(hl + 1) * 128], sktS[:, hl * 128:(hl + 1) * 128],
                lme3[:, par2 + 1:par2 + 2], None, ALU.mult))(), [t_skn_a])
            t_svn_a = S.eop("vector", (lambda hl=hl, ch=ch: lambda E: E.e.scalar_tensor_tensor(
                svtS[:, hl * 128:(hl + 1) * 128], qm(ch, 4, 1),
                1.0, svtsB[:, :], ALU.mult, ALU.add))(), [t_dmm, t_svts])
            t_svn = S.eop("vector", (lambda hl=hl, par2=par2: lambda E: E.e.tensor_scalar(
                svtS[:, hl * 128:(hl + 1) * 128], svtS[:, hl * 128:(hl + 1) * 128],
                lme3[:, par2 + 1:par2 + 2], None, ALU.mult))(), [t_svn_a])
            t_sk0t = S.eop("tensor", (lambda ch=ch: lambda E: E.e.matmul(
                qm(ch, 5, 2), sktsB[:, :], IDN, start=True, stop=True))(),
                [t_skts, P[("kev", ch)], P[("vev", ch)]])
            t_sk0e = S.eop("scalar", (lambda ch=ch: lambda E: E.e.activation(
                sk0s[:, :], qm(ch, 5, 2), AF.Copy))(), [t_sk0t, P.get(("i1", ch - 1))])
            def mk_i1(ch=ch, sl0=sl0, sl1=sl1, pp0=pp0, pp1=pp1):
                def f(E):
                    E.e.matmul(qm(ch, 6, 2), psmB[:, pp0:pp1], uu[:, pp0:pp1],
                               start=True, stop=False)
                    return E.e.matmul(qm(ch, 6, 2), qp[:, sl0:sl1], sk0s[:, :],
                                      start=False, stop=True)
                return f
            t_i1 = S.eop("tensor", mk_i1(), [P[("psm", ch)], P[("uev", ch)], t_sk0e, t_qs])
            t_scr = S.eop("vector", (lambda ch=ch, pp0=pp0, pp1=pp1: lambda E: E.e.tensor_tensor(
                scrb[:, :], qm(ch, 6, 2), lrelT[:, pp0:pp1], ALU.mult))(),
                [t_i1, P[("lrel", ch)], P.get(("pt", ch - 1))])
            t_pt = S.eop("scalar", lambda E: E.e.activation(
                ptil[:, :], scrb[:, :], AF.Exp, accum_out=denom[:, 0:1]),
                [t_scr, P.get(("app", ch - 1)), P.get(("d2", ch - 1))])
            t_app = S.eop("vector", (lambda pp0=pp0, pp1=pp1: lambda E: E.e.tensor_tensor(
                appB[:, :], ptil[:, :], lrelT[:, pp0:pp1], ALU.mult))(),
                [t_pt, P.get(("appt", ch - 1))])
            t_appt = S.eop("tensor", (lambda ch=ch: lambda E: E.e.matmul(
                qm(ch, 5, 3), appB[:, :], IDN, start=True, stop=True))(),
                [t_app, t_sk0e])
            t_appe = S.eop("scalar", (lambda ch=ch: lambda E: E.e.activation(
                appT[:, :], qm(ch, 5, 3), AF.Copy))(),
                [t_appt, P.get(("omm", ch - 1))])
            b6 = mb(ch, 6)
            def mk_w2mm(b6=b6, pp0=pp0, pp1=pp1):
                def f(E):
                    E.e.matmul(pb[b6][0:64, 384:448], upT[:, pp0:pp0 + 64], appT[:, 0:64],
                               start=True, stop=True)
                    return E.e.matmul(pb[b6][:, 448:512], upT[:, pp0:pp1], appT[:, 64:128],
                                      start=True, stop=True)
                return f
            t_w2mm = S.eop("tensor", mk_w2mm(), [t_appe, P[("up", ch)], t_scr])
            def mk_w2ma(ch=ch, b6=b6):
                def f(E):
                    return E.e.scalar_tensor_tensor(w2mB[0:64, 0:64], pb[b6][0:64, 384:448],
                                                    ccol[0:64, ch:ch + 1], MSK[0:64, 0:64],
                                                    ALU.mult, ALU.mult)
                return f
            def mk_w2mb(ch=ch, b6=b6):
                def f(E):
                    return E.e.scalar_tensor_tensor(w2mB[:, 64:128], pb[b6][:, 448:512],
                                                    ccol[:, ch:ch + 1], MSK[:, 64:128],
                                                    ALU.mult, ALU.mult)
                return f
            t_w2ma = S.eop("vector", mk_w2ma(), [t_w2mm, P.get(("omm", ch - 1)),
                                                 (t_w2z if (hl == 0 and ch == 0) else None)])
            t_w2m = S.eop("vector", mk_w2mb(), [t_w2ma])
            b7 = mb(ch, 7)
            def mk_o(b7=b7, pp0=pp0, pp1=pp1):
                def f(E):
                    E.e.matmul(pb[b7][:, 384:512], w2mB[:, :], vtd[:, pp0:pp1],
                               start=True, stop=False)
                    return E.e.matmul(pb[b7][:, 384:512], appT[:, :], svtsB[:, :],
                                      start=False, stop=True)
                return f
            t_omm = S.eop("tensor", mk_o(), [t_w2m, P[("vev", ch)], t_svts, t_appe,
                                             P.get(("oev", ch - 2)),
                                             (S.need_w(b7) if ch < 2 else None)])
            P[("skts", ch)] = t_skts
            P[("svts", ch)] = t_svts
            P[("sk0e", ch)] = t_sk0e
            P[("i1", ch)] = t_i1
            P[("pt", ch)] = t_pt
            P[("app", ch)] = t_app
            P[("appt", ch)] = t_appt
            P[("appe", ch)] = t_appe
            P[("w2mm", ch)] = t_w2mm
            P[("w2m", ch)] = t_w2m
            P[("dmm", ch)] = t_dmm
            P[("omm", ch)] = t_omm
            P[("skn", ch)] = t_skn
            P[("svna", ch)] = t_svn_a
            P[("svn", ch)] = t_svn

        def B1(ch):
            b7 = mb(ch, 7)
            t_sqo = S.eop("scalar", (lambda b7=b7: lambda E: E.e.activation(
                sqs[:, :], pb[b7][:, 384:512], AF.Square, accum_out=o2s[:, 0:1]))(),
                [P[("omm", ch)], P.get(("arg", ch - 1))])
            t_d2 = S.eop("vector", lambda E: E.e.tensor_tensor(
                rcol[:, 0:1], denom[:, 0:1], denom[:, 0:1], ALU.mult),
                [P[("pt", ch)], P.get(("arg", ch - 1))])
            t_arg = S.eop("vector", lambda E: E.e.scalar_tensor_tensor(
                rcol[:, 1:2], rcol[:, 0:1], 128e-5, o2s[:, 0:1], ALU.mult, ALU.add),
                [t_d2, t_sqo, P.get(("lnr", ch - 1))])
            t_lnr = S.eop("scalar", lambda E: E.e.activation(
                rcol[:, 2:3], rcol[:, 1:2], AF.Ln), [t_arg, P.get(("r", ch - 1))])
            t_r = S.eop("scalar", lambda E: E.e.activation(
                rcol[:, 3:4], rcol[:, 2:3], AF.Exp, scale=-0.5, bias=RLNC),
                [t_lnr, P.get(("om", ch - 1))])
            P[("sqo", ch)] = t_sqo
            P[("d2", ch)] = t_d2
            P[("arg", ch)] = t_arg
            P[("lnr", ch)] = t_lnr
            P[("r", ch)] = t_r

        def B2(ch):
            sl0, sl1 = ch * 128, (ch + 1) * 128
            b7 = mb(ch, 7)
            t_om = S.eop("vector", (lambda b7=b7: lambda E: E.e.tensor_scalar(
                omb[:, :], pb[b7][:, 384:512], rcol[:, 3:4], None, ALU.mult))(),
                [P[("r", ch)], P[("omm", ch)], P.get(("of", ch - 1))])
            def mk_of(sl0=sl0, sl1=sl1):
                def f(E):
                    return E.e.scalar_tensor_tensor(ofin[:, :], thbuf[:, sl0:sl1], 1.0,
                                                    omb[:, :], ALU.add, ALU.mult)
                return f
            t_of = S.eop("vector", mk_of(), [t_om, t_thh, P.get(("otr", ch - 1))])
            t_otr = S.eop("tensor", (lambda ch=ch: lambda E: E.e.matmul(
                qm(ch, 7, 2), ofin[:, :], IDN, start=True, stop=True))(),
                [t_of])
            def mk_oev(ch=ch):
                def f(E):
                    return E.e.activation(ostg[:, ch * 128:(ch + 1) * 128],
                                          qm(ch, 7, 2), AF.Copy)
                return f
            t_oev = S.eop("scalar", mk_oev(), [t_otr, prev.get(("ostg", hl - 1)),
                                               prev.get(("dbgall", hl - 1))])
            P[("om", ch)] = t_om
            P[("of", ch)] = t_of
            P[("otr", ch)] = t_otr
            P[("oev", ch)] = t_oev

        for ch in range(_nch):
            if ch >= 2:
                B1(ch - 2)
            F(ch)
            if ch >= 1:
                M(ch - 1)
            if ch >= 2:
                B2(ch - 2)
        if _nch >= 2:
            B1(_nch - 2)
        M(_nch - 1)
        if _nch >= 2:
            B2(_nch - 2)
        B1(_nch - 1)
        B2(_nch - 1)

        for par in (0, 1):
            lc = _nch - 1 if (_nch - 1) % 2 == par else _nch - 2
            if lc < 0:
                continue
            off = 0 if par == 0 else -4
            S.bw(4 + off, P[("dmm", lc)])
            S.br_(4 + off, P[("svna", lc)])
            S.bw(5 + off, P[("appt", lc)])
            S.br_(5 + off, P[("appe", lc)])
            S.bw(6 + off, P[("w2mm", lc)])
            S.br_(6 + off, P[("w2m", lc)])
            S.bw(7 + off, P[("otr", lc)])
            S.br_(7 + off, P[("oev", lc)])

        def mk_oship(hl=hl):
            def f(E):
                last = None
                for arm in nc.switch_core_id(engines=[E.e], n=128):
                    slot = (arm.logical % 8) & 1
                    i = E.e.dma_start(sh_o[slot, hl * 128:(hl + 1) * 128, :], ostg[:, :])
                    i.then_inc(E.sems["s_osh"], 16)
                    last = i
                return last
            return f
        t_oship = S.op("sync", mk_oship(), [P[("oev", _nch - 1)]], "s_osh", 16, manual=True)
        prev[("ostg", hl)] = t_oship
        prev[("thb", hl)] = P[("of", _nch - 1)]
        prev[("scan", hl)] = P[("svn", _nch - 1)]
        prev[("ls", hl)] = P[("rel", _nch - 1)]
        prev[("sqs", hl)] = P[("arg", _nch - 1)]
        if hl == 0:
            _dts = [
                dbg_dump("qp0", [128, 2048], BF16, qp[:, :], [t_qs]),
                dbg_dump("kp0", [128, 2048], BF16, kp[:, :], [t_post[(1, tc2)] for tc2 in range(4)]),
                dbg_dump("wp0", [128, 2048], BF16, wp[:, :], [t_post[(3, tc2)] for tc2 in range(4)]),
                dbg_dump("lsb0", [128, 2048], F32, lsb[:, :], [t_ls[g] for g in range(4)]),
                dbg_dump("ccol0", [128, NCH], F32, ccol[:, :], [t_cc]),
                dbg_dump("ostg0", [128, 2048], BF16, ostg[:, :], [P[("oev", _nch - 1)]]),
                dbg_dump("fT0", [128, 2048], BF16, fT[:, :], [t_fT]),
                dbg_dump("bbuf0", [128, 128], F32, bbuf[:, :], [t_beta]),
            ]
            _dts = [t for t in _dts if t is not None]
            prev[("dbgall", 0)] = ("s_dbg", S.cnt["s_dbg"]) if _dts else None
    flush_dq()
    t_gu_l, t_dn_l = dq_state["gu_last"], dq_state["dn_last"]
    t_dqcp = S.op("scalar", mk_dqcopy, [t_gu_l, t_dn_l], "a_cp", 48, manual=True)
    t_dqflag = S.op("scalar", mk_flag(1), [t_dqcp, t_flag1], "a_flg", 16, manual=True)
    if stop_after != "deq" and _heads > 0:
        t_scan_all = prev.get(("scan", _heads - 1))
        t_oship_all = [prev[("ostg", hl2)] for hl2 in range(_heads)]
        t_oflag = S.op("scalar", mk_flag(2), t_oship_all + [t_flag2, t_dqflag],
                       "a_flg", 16, manual=True)
    else:
        t_oflag = None

    def mk_poll(E):
        if no_poll:
            return E.e.nop()
        with E.e.register("expd") as expd, E.e.register("expo") as expo, \
                E.e.register("wv") as wv, E.e.register("cnt") as cnt:
            nr = E.e.value_load(noncS[0:1, 0:1])
            E.e.reg_alu(expd, nr, 1, op=ALU.add)
            E.e.reg_alu(expo, nr, 2, op=ALU.add)
            E.e.reg_mov(wv, 0)
            E.e.reg_mov(cnt, 0)
            nc.br("ploop", engines=[E.e.engine])
            with nc.body("ploop", valid_engines=[E.e.engine]):
                E.e.dma_start(flagr[0:1, 0:32],
                              sh_flags[0:2, 0:16].rearrange("a b -> (a b)").unsqueeze(0)) \
                    .then_inc(E.sems["sB"], 16)
                E.e.reg_alu(wv, wv, 16, op=ALU.add)
                E.e.wait_ge(E.sems["sB"], wv)
                E.e.reg_alu(cnt, cnt, 1, op=ALU.add)
                f0 = E.e.value_load(flagr[0:1, 1:2])
                f1 = E.e.value_load(flagr[0:1, 17:18])
                f2 = E.e.value_load(flagr[0:1, 2:3])
                f3 = E.e.value_load(flagr[0:1, 18:19])
                E.e.br_cmp(f0, expd, on_true="pm1", on_false="pchk", comp_op="IS_EQ")
            with nc.body("pm1", valid_engines=[E.e.engine]):
                E.e.br_cmp(f1, expd, on_true="pm2", on_false="pchk", comp_op="IS_EQ")
            with nc.body("pm2", valid_engines=[E.e.engine]):
                E.e.br_cmp(f2, expo, on_true="pm3", on_false="pchk", comp_op="IS_EQ")
            with nc.body("pm3", valid_engines=[E.e.engine]):
                E.e.br_cmp(f3, expo, on_true="pend", on_false="pchk", comp_op="IS_EQ")
            with nc.body("pchk", valid_engines=[E.e.engine]):
                E.e.br_cmp(cnt, 3000000, on_true="ploop", on_false="pend", comp_op="IS_LT")
            nc.switch_bb("pend")
            E.e.reg_save(flagr[0:1, 30:31], expd)
            E.e.reg_save(flagr[0:1, 31:32], cnt)
        return E.e.nop()
    if stop_after in ("deq", "scan"):
        no_poll = True
    t_barrier = S.op("gpsimd", mk_poll, [t_dqflag, t_oflag, t_non], "sP", 1)
    dbg_poll = None
    t_pdump = None
    if "poll" in debug:
        dbg_poll = nc.dram_tensor("dbg_poll", [1, 32], I32, kind="ExternalOutput").ap()
        def mk_pdump(E):
            return E.e.dma_start(dbg_poll, flagr[0:1, 0:32])
        t_pdump = S.op("gpsimd", mk_pdump, [t_barrier], "sB", 16)

    if stop_after != "full":
        def mk_dummy_out(E):
            return E.e.dma_start(out_my[0:128, 0:136], relmS[:, :])
        t_do = S.dma("sync", out_my[0:128, 0:136], relmS[:, :], [t_barrier], stream="s_out")
        t_fin0 = S.op("sync", lambda E: E.e.nop(),
                      [t_do] + ([t_pdump] if t_pdump else []), "sY", 1)
        def run_for(eng_name):
            def f(engine):
                E = Eng(engine, sems)
                for (en, waits, fn, inc) in S.ops:
                    if en != eng_name:
                        continue
                    for w in waits:
                        E.wait(w)
                    instr = fn(E)
                    if inc is not None:
                        instr.then_inc(sems[inc[0]], inc[1])
            return f
        blk.sync(run_for("sync"))
        blk.gpsimd(run_for("gpsimd"))
        blk.tensor(run_for("tensor"))
        blk.vector(run_for("vector"))
        blk.scalar(run_for("scalar"))
        p4s_dummy = None
        p23.close()
        big.close()
        return nc, dbg
    p23.close()
    p4s = contextlib.ExitStack()
    x2f = sbuf(p4s, "x2f", [128, 4 * 2048], F32)
    x2T = sbuf(p4s, "x2T", [128, 16 * 512], BF16)
    x2b = sbuf(p4s, "x2b", [128, 2048], BF16)
    hff = sbuf(p4s, "hff", [128, NIK * 512], BF16)
    fmst = sbuf(p4s, "fmst", [128, 16 * 512], BF16)
    scrA = [sbuf(p4s, f"scrA{i}", [128, 2048], BF16) for i in range(4)]
    dnslab = [sbuf(p4s, f"dnslab{i}", [128, NIK * 128], BF16) for i in range(2)]
    xs = sbuf(p4s, "xs", [128, 2048], F32)
    x1b = sbuf(p4s, "x1b", [128, 2048], F32)
    xout = sbuf(p4s, "xout", [128, 2048], F32)
    sgl = sbuf(p4s, "sgl", [128, 512], F32)
    lncol = sbuf(p4s, "lncol", [128, 16], F32)
    oT5 = hff[:, 0:16 * 512]

    t_out_toks = []
    p4p = {}
    for t5 in range(2):
        def mk_oT5(t5=t5):
            def f(E):
                last = None
                for arm in nc.switch_core_id(engines=[E.e], n=128):
                    half = (arm.logical % 8) & 1
                    c0 = half * 1024 + t5 * 512
                    i1 = E.e.dma_start(
                        oT5[:, 0:8 * 512].rearrange("p (h t) -> p h t", t=512),
                        sh_o[0, :, c0:c0 + 512].rearrange("(h p) t -> p h t", p=128))
                    i1.then_inc(E.sems["s_oT5"], 16)
                    i2 = E.e.dma_start(
                        oT5[:, 8 * 512:16 * 512].rearrange("p (h t) -> p h t", t=512),
                        sh_o[1, :, c0:c0 + 512].rearrange("(h p) t -> p h t", p=128))
                    i2.then_inc(E.sems["s_oT5"], 16)
                    last = i2
                return last
            return f
        t_o2 = S.op("sync", mk_oT5(), [t_barrier, p4p.get("hff")],
                    "s_oT5", 32, manual=True)
        loc_gate = [prev.get(("ostg", _heads - 1)), prev.get(("scan", _heads - 1))] \
            if (_heads > 0 and stop_after == "full") else [t_barrier]
        t_attn_ev = {}
        for ob in range(16):
            pg = ob % 4
            t_wo = S.dma("gpsimd", scrA[pg][:, :], din["wo_t"][ob, :, :],
                         [p4p.get(("wofree", t5, ob - 4))] + loc_gate +
                         [p4p.get(("scrA_gu", t5 - 1))], stream=f"g_wo{pg % 2}")
            t_nw = S.eop("vector", (lambda pg=pg: lambda E: E.e.tensor_scalar(
                scrA[pg][:, :], scrA[pg][:, :], nwc[:, 0:1], None, ALU.mult))(),
                [t_wo, t_nwc])
            bank = ob % 2
            def mk_at(pg=pg, bank=bank):
                def f(E):
                    mm = None
                    for j in range(16):
                        mm = E.e.matmul(pb[bank][:, :], scrA[pg][:, j * 128:(j + 1) * 128],
                                        oT5[:, j * 512:(j + 1) * 512],
                                        start=(j == 0), stop=(j == 15))
                    return mm
                return f
            tk = S.eop("tensor", mk_at(), [t_nw, t_o2, S.need_w(bank)])
            S.bw(bank, tk)
            evt = S.eop("scalar", (lambda ob=ob, bank=bank: lambda E: E.e.activation(
                fmst[:, ob * 512:(ob + 1) * 512], pb[bank][:, :], AF.Copy))(),
                [tk, p4p.get("fmst")])
            S.br_(bank, evt)
            t_attn_ev[ob] = evt
            p4p[("wofree", t5, ob)] = tk
        p4p["oT5r"] = p4p[("wofree", t5, 15)]
        t_at_trs = []
        for tt in range(4):
            t_xs = S.dma("sync", xs[:, :],
                         din["x_res"][t5 * 512 + tt * 128: t5 * 512 + (tt + 1) * 128, :],
                         [p4p.get("xs")] + loc_gate, stream="s_xs")
            t_x1 = []
            for og in range(4):
                def mk_atr(og=og, tt=tt):
                    def f(E):
                        mm = None
                        for j in range(4):
                            ob = og * 4 + j
                            mm = E.e.matmul(q128(2, j),
                                            fmst[:, ob * 512 + tt * 128: ob * 512 + (tt + 1) * 128],
                                            IDN, start=True, stop=True)
                        return mm
                    return f
                tk = S.eop("tensor", mk_atr(), [t_attn_ev[og * 4 + j2] for j2 in range(4)] +
                           [S.need_w(2)])
                S.bw(2, tk)
                t_at_trs.append(tk)
                def mk_x1(og=og):
                    def f(E):
                        return E.e.scalar_tensor_tensor(
                            x1b[:, og * 512:(og + 1) * 512], pb[2][:, :], 1.0,
                            xs[:, og * 512:(og + 1) * 512], ALU.mult, ALU.add,
                            accum_out=lncol[:, og:og + 1])
                    return f
                t1 = S.eop("vector", mk_x1(), [tk, t_xs, p4p.get("x1b"), p4p.get("dbg4")])
                S.br_(2, t1)
                t_x1.append(t1)
            t_sqx = []
            for og2 in range(4):
                t_sqx.append(S.eop("scalar", (lambda og=og2: lambda E: E.e.activation(
                    sgl[:, :], x1b[:, og * 512:(og + 1) * 512], AF.Square,
                    accum_out=lncol[:, 4 + og:5 + og]))(),
                    [t_x1[og2], p4p.get("sgl"), (t_sqx[-1] if t_sqx else None)]))
            t_s1 = S.eop("vector", lambda E: E.e.reduce_sum(
                lncol[:, 8:9], lncol[:, 0:4], axis=AXX), t_x1)
            t_s2 = S.eop("vector", lambda E: E.e.reduce_sum(
                lncol[:, 9:10], lncol[:, 4:8], axis=AXX), t_sqx)
            t_mu = S.eop("scalar", lambda E: E.e.activation(
                lncol[:, 10:11], lncol[:, 8:9], AF.Copy, scale=1.0 / 2048), [t_s1])
            t_mu2 = S.eop("scalar", lambda E: E.e.activation(
                lncol[:, 11:12], lncol[:, 10:11], AF.Square), [t_mu])
            t_var = S.eop("vector", lambda E: E.e.scalar_tensor_tensor(
                lncol[:, 12:13], lncol[:, 9:10], 1.0 / 2048, lncol[:, 11:12],
                ALU.mult, ALU.subtract), [t_s2, t_mu2])
            t_sd = S.eop("scalar", lambda E: E.e.activation(
                lncol[:, 13:14], lncol[:, 12:13], AF.Ln, bias=1e-5), [t_var])
            t_rstd = S.eop("scalar", lambda E: E.e.activation(
                lncol[:, 14:15], lncol[:, 13:14], AF.Exp, scale=-0.5), [t_sd])
            t_x2 = []
            for og in range(4):
                def mk_x2(og=og, tt=tt):
                    def f(E):
                        rb = lncol[:, 14:15].broadcast_to([128, 512])
                        return E.e.scalar_tensor_tensor(
                            x2f[:, tt * 2048 + og * 512: tt * 2048 + (og + 1) * 512],
                            x1b[:, og * 512:(og + 1) * 512], lncol[:, 10:11], rb,
                            ALU.subtract, ALU.mult)
                    return f
                t_x2.append(S.eop("vector", mk_x2(), [t_rstd, t_mu, t_x1[og],
                                                      p4p.get("x2f")]))
            t_x2b = S.eop("vector", (lambda tt=tt: lambda E: E.e.tensor_copy(
                x2b[:, :], x2f[:, tt * 2048:(tt + 1) * 2048]))(),
                t_x2 + [p4p.get("x2b")])
            for g4 in range(4):
                def mk_xt(g4=g4):
                    def f(E):
                        mm = None
                        for j in range(4):
                            kb2 = g4 * 4 + j
                            mm = E.e.matmul(q128(3, j), x2b[:, kb2 * 128:(kb2 + 1) * 128],
                                            IDN, start=True, stop=True)
                        return mm
                    return f
                tk = S.eop("tensor", mk_xt(), [t_x2b, S.need_w(3)])
                S.bw(3, tk)
                def mk_xte(g4=g4, tt=tt):
                    def f(E):
                        dst = x2T[:, :].rearrange("p (kb t) -> p kb t", t=512)[
                            :, g4 * 4:(g4 + 1) * 4, tt * 128:(tt + 1) * 128]
                        return E.e.activation(dst, pb[3][:, :].rearrange(
                            "p (a b) -> p a b", b=128), AF.Copy)
                    return f
                evt = S.eop("scalar", mk_xte(), [tk, p4p.get("x2T")])
                S.br_(3, evt)
            p4p["xs"] = t_x1[-1]
            p4p["x1b"] = t_x2[-1]
            p4p["sgl"] = t_s2
            p4p[("x2T", t5, tt)] = evt
            if t5 == 0 and tt == 0:
                _d1 = dbg_dump("x1_0", [128, 2048], F32, x1b[:, :], t_x1)
                _d2 = dbg_dump("attnb0", [128, 8192], BF16, fmst[:, :],
                               [t_attn_ev[obx] for obx in range(16)])
                if _d1 is not None or _d2 is not None:
                    p4p["dbg4"] = ("s_dbg", S.cnt["s_dbg"])
        p4p["x2b"] = t_x2b
        p4p["fmst"] = t_at_trs[-1]
        t_x2T_all = [p4p[("x2T", t5, tt2)] for tt2 in range(4)]

        t_hff = {}
        for gb in range(44):
            pg = gb % 2
            t_lg = S.dma("sync", scrA[pg][:, :], sh_wgu[gb, :, :],
                         [p4p.get(("gfree", t5, gb - 2)), t_barrier, p4p.get("oT5r")],
                         stream=f"s_gu{pg}")
            t_lu = S.dma("sync", scrA[2 + pg][:, :], sh_wgu[44 + gb, :, :],
                         [p4p.get(("ufree", t5, gb - 2)), t_barrier, p4p.get("oT5r")],
                         stream=f"s_gu{pg}")
            bank, bank_u = pg, 2 + pg
            def mk_gu(pg=pg, bank=bank, bank_u=bank_u):
                def f(E):
                    mm = None
                    for kbi in range(16):
                        E.e.matmul(pb[bank][:, :], scrA[pg][:, kbi * 128:(kbi + 1) * 128],
                                   x2T[:, kbi * 512:(kbi + 1) * 512],
                                   start=(kbi == 0), stop=(kbi == 15))
                    for kbi in range(16):
                        mm = E.e.matmul(pb[bank_u][:, :],
                                        scrA[2 + pg][:, kbi * 128:(kbi + 1) * 128],
                                        x2T[:, kbi * 512:(kbi + 1) * 512],
                                        start=(kbi == 0), stop=(kbi == 15))
                    return mm
                return f
            tk = S.eop("tensor", mk_gu(), t_x2T_all + [t_lu, S.need_w(bank),
                                                       S.need_w(bank_u)])
            S.bw(bank, tk)
            S.bw(bank_u, tk)
            t_sil = S.eop("scalar", (lambda bank=bank: lambda E: E.e.activation(
                sgl[:, :], pb[bank][:, :], AF.Silu))(), [tk, p4p.get("sgl")])
            S.br_(bank, t_sil)
            def mk_h(gb=gb, bank_u=bank_u):
                def f(E):
                    return E.e.tensor_tensor(hff[:, gb * 512:(gb + 1) * 512],
                                             sgl[:, :], pb[bank_u][:, :], ALU.mult)
                return f
            t_h = S.eop("vector", mk_h(), [t_sil, p4p.get(("hffw", t5, gb))])
            S.br_(bank_u, t_h)
            p4p["sgl"] = t_h
            p4p[("gfree", t5, gb)] = tk
            p4p[("ufree", t5, gb)] = tk
            t_hff[gb] = t_h
        p4p[("scrA_gu", t5)] = tk
        t_hff_all = [t_hff[g2] for g2 in range(44)]

        t_dn_ev = {}
        for ob in range(16):
            pg = ob % 2
            t_ld = S.dma("sync", dnslab[pg][:, :], sh_wdn[ob, :, :],
                         [p4p.get(("dfree", t5, ob - 2)), t_barrier], stream=f"s_dn{pg}")
            bank = 4 + (ob % 2)
            def mk_dn(pg=pg, bank=bank):
                def f(E):
                    mm = None
                    for ikb in range(NIK):
                        mm = E.e.matmul(pb[bank][:, :], dnslab[pg][:, ikb * 128:(ikb + 1) * 128],
                                        hff[:, ikb * 512:(ikb + 1) * 512],
                                        start=(ikb == 0), stop=(ikb == NIK - 1))
                    return mm
                return f
            tk = S.eop("tensor", mk_dn(), t_hff_all + [t_ld, S.need_w(bank)])
            S.bw(bank, tk)
            evt = S.eop("scalar", (lambda ob=ob, bank=bank: lambda E: E.e.activation(
                fmst[:, ob * 512:(ob + 1) * 512], pb[bank][:, :], AF.Copy))(),
                [tk, p4p.get("fmst2"), p4p.get("dbg4")])
            S.br_(bank, evt)
            t_dn_ev[ob] = evt
            p4p[("dfree", t5, ob)] = tk

        for tt in range(4):
            t_ytr = []
            for og in range(4):
                def mk_yt(og=og, tt=tt):
                    def f(E):
                        mm = None
                        for j in range(4):
                            ob = og * 4 + j
                            mm = E.e.matmul(q128(6, j),
                                            fmst[:, ob * 512 + tt * 128: ob * 512 + (tt + 1) * 128],
                                            IDN, start=True, stop=True)
                        return mm
                    return f
                tk = S.eop("tensor", mk_yt(), [t_dn_ev[og * 4 + j2] for j2 in range(4)] +
                           [S.need_w(6)])
                S.bw(6, tk)
                evt = S.eop("scalar", (lambda og=og: lambda E: E.e.activation(
                    xout[:, og * 512:(og + 1) * 512], pb[6][:, :], AF.Copy))(),
                    [tk, p4p.get("xout")])
                S.br_(6, evt)
                t_ytr.append(evt)
            t_sqy = []
            for og2 in range(4):
                t_sqy.append(S.eop("scalar", (lambda og=og2: lambda E: E.e.activation(
                    sgl[:, :], xout[:, og * 512:(og + 1) * 512], AF.Square,
                    accum_out=lncol[:, og:og + 1]))(),
                    [t_ytr[og2], p4p.get("sgl"), (t_sqy[-1] if t_sqy else None)]))
            t_sy = S.eop("vector", lambda E: E.e.reduce_sum(
                lncol[:, 8:9], lncol[:, 0:4], axis=AXX), t_sqy)
            t_sdy = S.eop("scalar", lambda E: E.e.activation(
                lncol[:, 9:10], lncol[:, 8:9], AF.Ln, scale=1.0 / 2048, bias=1e-6), [t_sy])
            t_ry = S.eop("scalar", lambda E: E.e.activation(
                lncol[:, 10:11], lncol[:, 9:10], AF.Exp, scale=-0.5), [t_sdy])
            t_x3 = []
            for og in range(4):
                def mk_x3(og=og, tt=tt):
                    def f(E):
                        return E.e.scalar_tensor_tensor(
                            x1b[:, og * 512:(og + 1) * 512], xout[:, og * 512:(og + 1) * 512],
                            lncol[:, 10:11],
                            x2f[:, tt * 2048 + og * 512: tt * 2048 + (og + 1) * 512],
                            ALU.mult, ALU.add, accum_out=lncol[:, 4 + og:5 + og])
                    return f
                t_x3.append(S.eop("vector", mk_x3(), [t_ry, t_ytr[og], p4p.get("x1c")]))
            t_sqx3 = []
            for og2 in range(4):
                t_sqx3.append(S.eop("scalar", (lambda og=og2: lambda E: E.e.activation(
                    sgl[:, :], x1b[:, og * 512:(og + 1) * 512], AF.Square,
                    accum_out=lncol[:, 11 + og:12 + og]))(),
                    [t_x3[og2], p4p.get("sgl"), (t_sqx3[-1] if t_sqx3 else None)]))
            t_s1b = S.eop("vector", lambda E: E.e.reduce_sum(
                lncol[:, 8:9], lncol[:, 4:8], axis=AXX), t_x3)
            t_s2b = S.eop("vector", lambda E: E.e.reduce_sum(
                lncol[:, 9:10], lncol[:, 11:15], axis=AXX), t_sqx3)
            t_mub = S.eop("scalar", lambda E: E.e.activation(
                lncol[:, 10:11], lncol[:, 8:9], AF.Copy, scale=1.0 / 2048), [t_s1b])
            t_mu2b = S.eop("scalar", lambda E: E.e.activation(
                lncol[:, 15:16], lncol[:, 10:11], AF.Square), [t_mub])
            t_varb = S.eop("vector", lambda E: E.e.scalar_tensor_tensor(
                lncol[:, 12:13], lncol[:, 9:10], 1.0 / 2048, lncol[:, 15:16],
                ALU.mult, ALU.subtract), [t_s2b, t_mu2b])
            t_sdb = S.eop("scalar", lambda E: E.e.activation(
                lncol[:, 13:14], lncol[:, 12:13], AF.Ln, bias=1e-5), [t_varb])
            t_rsb = S.eop("scalar", lambda E: E.e.activation(
                lncol[:, 14:15], lncol[:, 13:14], AF.Exp, scale=-0.5), [t_sdb])
            t_x4 = []
            for og in range(4):
                def mk_x4(og=og):
                    def f(E):
                        rb = lncol[:, 14:15].broadcast_to([128, 512])
                        return E.e.scalar_tensor_tensor(
                            xout[:, og * 512:(og + 1) * 512], x1b[:, og * 512:(og + 1) * 512],
                            lncol[:, 10:11], rb, ALU.subtract, ALU.mult)
                    return f
                t_x4.append(S.eop("vector", mk_x4(), [t_rsb, t_mub, t_sqx3[og],
                                                      t_sqy[og]]))
            t_wr = S.dma("sync", out_my[t5 * 512 + tt * 128: t5 * 512 + (tt + 1) * 128, :],
                         xout[:, :], t_x4, stream="s_out")
            t_out_toks.append(t_wr)
            p4p["xout"] = t_wr
            p4p["x1c"] = t_sqx3[-1]
            p4p["sgl"] = t_sqx3[-1]
        p4p["fmst2"] = tk
        p4p["hff"] = tk
        p4p["x2f"] = t_x4[-1]

    t_fin = S.op("sync", lambda E: E.e.nop(),
                 t_out_toks + ([t_pdump] if t_pdump else []), "sY", 1)

    def run_for(eng_name):
        def f(engine):
            E = Eng(engine, sems)
            for (en, waits, fn, inc) in S.ops:
                if en != eng_name:
                    continue
                for w in waits:
                    E.wait(w)
                instr = fn(E)
                if inc is not None:
                    instr.then_inc(sems[inc[0]], inc[1])
        return f

    blk.sync(run_for("sync"))
    blk.gpsimd(run_for("gpsimd"))
    blk.tensor(run_for("tensor"))
    blk.vector(run_for("vector"))
    blk.scalar(run_for("scalar"))

    p4s.close()
    big.close()
    return nc, dbg


def kernel(**inputs):
    consts = make_consts()
    nonce = np.array([[int(np.random.randint(1, 2 ** 24 - 16)), 0, 0, 0]], dtype=np.int32)
    nc, _ = build_program()
    in_maps = [make_inputs_for_core(c, inputs, nonce, consts) for c in range(8)]
    res = run_bass_kernel_spmd(nc, in_maps, core_ids=list(range(8)))
    out = np.zeros((B, T, HID), np.float32)
    for c in range(8):
        b, half = c // 2, c & 1
        out[b, half * 1024:(half + 1) * 1024] = res.results[c]["out_my"]
    return out


# version 17
# speedup vs baseline: 1.1479x; 1.0186x over previous
import contextlib
import numpy as np
import ml_dtypes
import concourse.bass as bass
import concourse.mybir as mybir
from concourse.bass_utils import run_bass_kernel_spmd

F32 = mybir.dt.float32
F16 = mybir.dt.float16
BF16 = mybir.dt.bfloat16
I32 = mybir.dt.int32
AF = mybir.ActivationFunctionType
ALU = mybir.AluOpType
AXX = mybir.AxisListType.X

B, T, HID = 4, 2048, 2048
DBG_HEADS = 99
DBG_CHUNKS = 99
SCALE = 128.0 ** -0.5
NCH = 16
HL = 8
NIK = 44
RLNC = float(np.log(0.5) + 0.5 * np.log(128.0))

ESEM = {"tensor": "sT", "vector": "sV", "scalar": "sS", "gpsimd": "sP", "sync": "sY"}
DMA_STREAMS = ["s_p0", "s_h", "s_pk", "s_osh", "s_oT5", "s_gu0", "s_gu1", "s_dn0",
               "s_dn1", "s_xs", "s_out", "g_p0", "g_w0", "g_w1", "g_wo0", "g_wo1",
               "a_ldq", "a_cp", "a_flg", "s_dbg"]
ALL_SEMS = sorted(set(list(ESEM.values()) + DMA_STREAMS + ["sB"]))


def _tile_w(Wf, half):
    Wh = Wf[:, half * 1024:(half + 1) * 1024]
    return np.ascontiguousarray(
        Wh.reshape(16, 128, 8, 128).transpose(2, 1, 0, 3).reshape(8, 128, 2048))


def make_consts():
    s = np.arange(128)[:, None]
    t = np.arange(128)[None, :]
    relm = np.zeros((128, 136), np.float32)
    relm[:, :128] = (s <= t).astype(np.float32) - (s <= 63).astype(np.float32)
    relm[:, 128] = (np.arange(128) <= 63).astype(np.float32)
    relm[:, 129] = (np.arange(128) > 63).astype(np.float32)
    relm[:, 132] = 1e-6
    relm[:, 133] = 1e-5
    relm[:, 134] = float(np.log(0.5) + 0.5 * np.log(128.0))
    ident = np.eye(128, dtype=np.float32)
    mask = (s <= t).astype(np.float32)
    cbf = np.concatenate([ident, mask, np.ones((128, 128), np.float32)], axis=1)
    return {"relm": relm, "cbf16": cbf.astype(ml_dtypes.bfloat16)}


def make_inputs_for_core(c, inp, nonce, consts):
    b, half = c // 2, c & 1
    g = lambda k: np.asarray(inp[k])
    hb = g("L0_attn_hidden_states")[b]
    if half == 0:
        gu_rows = np.r_[0:2816, 5632:8448]
    else:
        gu_rows = np.r_[2816:5632, 8448:11264]
    d = {
        "h_my": np.ascontiguousarray(hb),
        "x_res": np.ascontiguousarray(hb[half * 1024:(half + 1) * 1024]),
        "wq_t": _tile_w(g("L0_attn_Wq"), half),
        "wk_t": _tile_w(g("L0_attn_Wk"), half),
        "wv_t": _tile_w(g("L0_attn_Wv"), half),
        "ww_t": _tile_w(g("L0_attn_Ww"), half),
        "wo_t": np.ascontiguousarray(
            g("L0_attn_Wo").reshape(16, 128, 16, 128).transpose(2, 1, 0, 3).reshape(16, 128, 2048)),
        "wf1_t": np.ascontiguousarray(
            g("L0_attn_Wf1").reshape(16, 128, 128).transpose(1, 0, 2).reshape(128, 2048)),
        "wg1_t": np.ascontiguousarray(
            g("L0_attn_Wg1").reshape(16, 128, 128).transpose(1, 0, 2).reshape(128, 2048)),
        "wf2_h": np.ascontiguousarray(g("L0_attn_Wf2")[:, half * 1024:(half + 1) * 1024]),
        "wg2_h": np.ascontiguousarray(g("L0_attn_Wg2")[:, half * 1024:(half + 1) * 1024]),
        "wb_t": np.ascontiguousarray(
            g("L0_attn_Wb")[:, half * 8:(half + 1) * 8].reshape(16, 128, 8)
            .transpose(1, 0, 2).reshape(128, 128)),
        "conv_t": np.ascontiguousarray(np.stack(
            [g("L0_attn_cq")[half * 1024:(half + 1) * 1024].reshape(8, 128, 4).transpose(1, 0, 2),
             g("L0_attn_ck")[half * 1024:(half + 1) * 1024].reshape(8, 128, 4).transpose(1, 0, 2),
             g("L0_attn_cv")[half * 1024:(half + 1) * 1024].reshape(8, 128, 4).transpose(1, 0, 2)],
            axis=1).reshape(128, 96)),
        "bg2_my": np.ascontiguousarray(g("L0_attn_bg2")[half * 1024:(half + 1) * 1024][None, :]),
        "nw_col": np.ascontiguousarray(g("L0_attn_norm_w")[:, None]),
        "gup_pk": np.ascontiguousarray(g("L0_ffn_gate_up_packed")[gu_rows]),
        "gup_sc": np.ascontiguousarray(
            g("L0_ffn_gate_up_scales").T[gu_rows].reshape(44, 128, 16)
            .transpose(1, 0, 2).reshape(128, 704)),
        "dn_pk": np.ascontiguousarray(g("L0_ffn_down_packed")[half * 1024:(half + 1) * 1024]),
        "dn_sc": np.ascontiguousarray(
            g("L0_ffn_down_scales").T[half * 1024:(half + 1) * 1024].reshape(8, 128, 44)
            .transpose(1, 0, 2).reshape(128, 352)),
        "nonce": nonce,
    }
    d.update(consts)
    return d


class Sched:
    def __init__(self):
        self.ops = []
        self.cnt = {s: 0 for s in ALL_SEMS}
        self.bank_w = [None] * 8
        self.bank_r = [None] * 8

    def op(self, eng, emit_fn, waits=(), inc=None, inc_amt=0, manual=False):
        tokv = None
        if inc is not None:
            self.cnt[inc] += inc_amt
            tokv = (inc, self.cnt[inc])
        self.ops.append((eng, tuple(w for w in waits if w is not None), emit_fn,
                         None if (manual or inc is None) else (inc, inc_amt)))
        return tokv

    def dma(self, eng, dst, src, waits=(), stream=None):
        assert stream in DMA_STREAMS, stream
        return self.op(eng, lambda E: E.e.dma_start(dst, src), waits, stream, 16)

    def eop(self, eng, emit_fn, waits=()):
        return self.op(eng, emit_fn, waits, ESEM[eng], 1)

    def bw(self, i, tok):
        self.bank_w[i] = tok

    def br_(self, i, tok):
        self.bank_r[i] = tok

    def need_w(self, i):
        return self.bank_r[i]

    def need_r(self, i):
        return self.bank_w[i]


class Eng:
    def __init__(self, engine, sems):
        self.e = engine
        self.sems = sems
        self.waited = {}

    def wait(self, tokv):
        s, v = tokv
        if self.waited.get(s, 0) >= v:
            return
        self.e.wait_ge(self.sems[s], v)
        self.waited[s] = v


def build_program(debug=(), no_poll=False, sim_local=False, stop_after="full"):
    nc = bass.Bass("TRN2", target_bir_lowering=False)
    S = Sched()

    din = {}
    for name, shape, dt in [
        ("h_my", [2048, 2048], F32), ("x_res", [1024, 2048], F32),
        ("wq_t", [8, 128, 2048], F32), ("wk_t", [8, 128, 2048], F32),
        ("wv_t", [8, 128, 2048], F32), ("ww_t", [8, 128, 2048], F32),
        ("wo_t", [16, 128, 2048], F32),
        ("wf1_t", [128, 2048], F32), ("wg1_t", [128, 2048], F32),
        ("wf2_h", [128, 1024], F32), ("wg2_h", [128, 1024], F32),
        ("wb_t", [128, 128], F32), ("conv_t", [128, 96], F32),
        ("bg2_my", [1, 1024], F32), ("nw_col", [128, 1], F32),
        ("gup_pk", [5632, 256], I32), ("gup_sc", [128, 704], F16),
        ("dn_pk", [1024, 704], I32), ("dn_sc", [128, 352], F16),
        ("nonce", [1, 4], I32),
        ("relm", [128, 136], F32), ("cbf16", [128, 384], BF16),
    ]:
        din[name] = nc.dram_tensor(name, shape, dt, kind="ExternalInput").ap()
    out_my = nc.dram_tensor("out_my", [1024, 2048], F32, kind="ExternalOutput").ap()

    dbg = {}

    def dbg_dump(name, shape, dt, src_ap, wait_toks):
        if name not in debug:
            return None
        dbg[name] = nc.dram_tensor("dbg_" + name, list(shape), dt, kind="ExternalOutput").ap()
        return S.dma("sync", dbg[name], src_ap, wait_toks, stream="s_dbg")

    _aspace = "Local" if sim_local else "Shared"
    sh_wgu = nc.dram_tensor("sh_wgu", [88, 128, 2048], BF16, addr_space=_aspace).ap()
    sh_wdn = nc.dram_tensor("sh_wdn", [16, 128, NIK * 128], BF16, addr_space=_aspace).ap()
    sh_o = nc.dram_tensor("sh_o", [2, 1024, 2048], BF16, addr_space=_aspace).ap()
    sh_flags = nc.dram_tensor("sh_flags", [2, 16], I32, addr_space=_aspace).ap()
    ldq_gu = nc.dram_tensor("ldq_gu", [44, 128, 2048], BF16).ap()
    ldq_dn = nc.dram_tensor("ldq_dn", [8, 128, NIK * 128], BF16).ap()

    big = contextlib.ExitStack()

    def sbuf(st, name, shape, dt):
        return st.enter_context(nc.sbuf_tensor(name, shape, dt))

    fT = sbuf(big, "fT", [128, 2048], BF16)
    fgT = sbuf(big, "fgT", [128, 2048], BF16)
    wf2s = sbuf(big, "wf2s", [128, 1024], BF16)
    wg2s = sbuf(big, "wg2s", [128, 1024], BF16)
    bg2s = sbuf(big, "bg2s", [1, 1024], BF16)
    nwc = sbuf(big, "nwc", [128, 1], F32)
    relmS = sbuf(big, "relmS", [128, 136], F32)
    cbS = sbuf(big, "cbS", [128, 384], BF16)
    noncS = sbuf(big, "noncS", [1, 4], I32)
    bbuf = sbuf(big, "bbuf", [128, NCH * 8], F32)
    convS = sbuf(big, "convS", [128, 96], F32)
    wbS = sbuf(big, "wbS", [128, 128], BF16)
    scg = sbuf(big, "scg", [128, 704], F32)
    scd = sbuf(big, "scd", [128, 352], F32)
    flagw = sbuf(big, "flagw", [1, 4], I32)
    flagr = sbuf(big, "flagr", [1, 32], I32)

    pb = [big.enter_context(nc.psum_tensor(f"pb{i}", [128, 512], F32)) for i in range(8)]
    sems = {s: big.enter_context(nc.semaphore(s)) for s in ALL_SEMS}
    blk = big.enter_context(nc.Block())

    def q128(bank, quarter):
        return pb[bank][:, quarter * 128:(quarter + 1) * 128]

    nc.const_aps.register(F32, 1e-6, relmS[:, 132:133])
    nc.const_aps.register(F32, 1e-5, relmS[:, 133:134])
    nc.const_aps.register(F32, float(np.log(0.5) + 0.5 * np.log(128.0)), relmS[:, 134:135])
    IDN = cbS[:, 0:128]
    MSK = cbS[:, 128:256]
    ONE = cbS[:, 256:384]

    t_relm = S.dma("sync", relmS[:, :], din["relm"][:, :], stream="s_p0")
    t_cb = S.dma("sync", cbS[:, :], din["cbf16"][:, :], stream="s_p0")
    t_non = S.dma("sync", noncS[:, :], din["nonce"][:, :], stream="s_p0")
    t_conv = S.dma("sync", convS[:, :], din["conv_t"][:, :], stream="s_p0")
    t_bg2 = S.dma("gpsimd", bg2s[:, :], din["bg2_my"][:, :], stream="g_p0")
    t_nwc = S.dma("sync", nwc[:, :], din["nw_col"][:, :], stream="s_p0")
    t_wf2 = S.dma("gpsimd", wf2s[:, :], din["wf2_h"][:, :], stream="g_p0")
    t_wg2 = S.dma("gpsimd", wg2s[:, :], din["wg2_h"][:, :], stream="g_p0")
    t_wb = S.dma("gpsimd", wbS[:, :], din["wb_t"][:, :], stream="g_p0")
    t_scg = S.dma("gpsimd", scg[:, :], din["gup_sc"][:, :], stream="g_p0")
    t_scd = S.dma("gpsimd", scd[:, :], din["dn_sc"][:, :], stream="g_p0")
    t_relm = t_cb = t_non = t_conv = t_nwc = ("s_p0", S.cnt["s_p0"])
    t_bg2 = t_wf2 = t_wg2 = t_wb = t_scg = t_scd = ("g_p0", S.cnt["g_p0"] + 16)
    t_scg2 = S.eop("vector", lambda E: E.e.tensor_scalar(
        scg[:, :], scg[:, :], float(2.0 ** 126), None, ALU.mult), [t_scg])
    t_scd2 = S.eop("vector", lambda E: E.e.tensor_scalar(
        scd[:, :], scd[:, :], float(2.0 ** 126), None, ALU.mult), [t_scd])
    t_flag1 = S.eop("vector", lambda E: E.e.tensor_scalar(
        flagw[0:1, 1:2], noncS[0:1, 0:1], 1, None, ALU.add), [t_non])
    t_flag2 = S.eop("vector", lambda E: E.e.tensor_scalar(
        flagw[0:1, 2:3], noncS[0:1, 0:1], 2, None, ALU.add), [t_non])

    p23 = contextlib.ExitStack()
    hT = sbuf(p23, "hT", [128, 16 * 2048], BF16)
    wtile = [sbuf(p23, f"wtile{i}", [128, 4 * 2048], BF16) for i in range(2)]
    pre = sbuf(p23, "pre", [128, 2051], BF16)
    qp = sbuf(p23, "qp", [128, 2048], BF16)
    kp = sbuf(p23, "kp", [128, 2048], BF16)
    vp = sbuf(p23, "vp", [128, 2048], BF16)
    wp = sbuf(p23, "wp", [128, 2048], BF16)
    lsb = sbuf(p23, "lsb", [128, 16 * 128], F32)
    zscr = sbuf(p23, "zscr", [128, 512], F32)
    ccol = sbuf(p23, "ccol", [128, NCH], F32)
    diag = sbuf(p23, "diag", [128, 4 * 128], BF16)
    ostg = sbuf(p23, "ostg", [128, 2048], BF16)
    thbuf = sbuf(p23, "thbuf", [128, 2048], BF16)
    sktS = sbuf(p23, "sktS", [128, HL * 128], BF16)
    svtS = sbuf(p23, "svtS", [128, HL * 128], BF16)
    lrelT = sbuf(p23, "lrelT", [128, 256], BF16)
    emtB = sbuf(p23, "emtB", [128, 128], BF16)
    lme3 = sbuf(p23, "lme3", [128, 4], F32)
    upT = sbuf(p23, "upT", [128, 256], BF16)
    uu = sbuf(p23, "uu", [128, 256], BF16)
    kvtd = sbuf(p23, "kvtd", [128, 512], BF16)
    sk0s = sbuf(p23, "sk0s", [128, 128], BF16)
    psmB = sbuf(p23, "psmB", [128, 256], BF16)
    scrb = sbuf(p23, "scrb", [128, 128], BF16)
    ptil = sbuf(p23, "ptil", [128, 128], BF16)
    appB = sbuf(p23, "appB", [128, 128], BF16)
    appT = sbuf(p23, "appT", [128, 128], BF16)
    w2mB = sbuf(p23, "w2mB", [128, 128], BF16)
    denom = sbuf(p23, "denom", [128, 1], F32)
    o2s = sbuf(p23, "o2s", [128, 1], F32)
    rcol = sbuf(p23, "rcol", [128, 4], F32)
    sqs = sbuf(p23, "sqs", [128, 128], BF16)
    omb = sbuf(p23, "omb", [128, 128], BF16)
    ofin = sbuf(p23, "ofin", [128, 128], BF16)
    sktsB = sbuf(p23, "sktsB", [128, 128], BF16)
    svtsB = sbuf(p23, "svtsB", [128, 128], BF16)
    _pkb1 = sbuf(p23, "pkb0", [128, 704], I32)
    pkb = [_pkb1, _pkb1]
    inib = sbuf(p23, "inib", [128, 1408], I32)
    iv3 = sbuf(p23, "iv3", [128, 1408], I32)
    dqt = sbuf(p23, "dqt", [128, 2 * 1408], BF16)
    stg = sbuf(p23, "stg", [128, 2 * 1408], BF16)

    hsc = contextlib.ExitStack()
    _htmp321 = sbuf(hsc, "htmp32_0", [128, 2048], F32)
    htmp32 = [_htmp321, _htmp321]
    _htmpb1 = sbuf(hsc, "htmpb_0", [128, 2048], BF16)
    htmpb = [_htmpb1, _htmpb1]
    wf1s = sbuf(hsc, "wf1s", [128, 2048], BF16)
    wg1s = wf1s
    t_wf1 = S.dma("gpsimd", wf1s[:, :], din["wf1_t"][:, :], stream="g_p0")
    dc_tok = {}
    ev_last = {}
    for tb in range(16):
        pp = tb % 2
        ld = S.dma("sync", htmp32[pp][:, :], din["h_my"][tb * 128:(tb + 1) * 128, :],
                   [dc_tok.get(tb - 1)], stream="s_h")
        dc_tok[tb] = S.eop("vector", (lambda pp: lambda E: E.e.tensor_copy(
            htmpb[pp][:, :], htmp32[pp][:, :]))(pp), [ld, ev_last.get(tb - 1)])
        for g4 in range(4):
            bank = g4 % 2
            def mk_tr(pp=pp, g4=g4, bank=bank):
                def f(E):
                    mm = None
                    for j in range(4):
                        db = g4 * 4 + j
                        mm = E.e.matmul(pb[bank][:, j * 128:(j + 1) * 128],
                                        htmpb[pp][:, db * 128:(db + 1) * 128],
                                        IDN, start=True, stop=True)
                    return mm
                return f
            tk = S.eop("tensor", mk_tr(), [dc_tok[tb], t_cb, S.need_w(bank)])
            S.bw(bank, tk)
            def mk_ev(tb=tb, g4=g4, bank=bank):
                def f(E):
                    dst = hT[:, :].rearrange("p (kb t) -> p kb t", t=2048)[
                        :, g4 * 4:(g4 + 1) * 4, tb * 128:(tb + 1) * 128]
                    src = pb[bank][:, :].rearrange("p (a b) -> p a b", b=128)
                    return E.e.activation(dst, src, AF.Copy)
                return f
            evt = S.eop("scalar", mk_ev(), [tk])
            S.br_(bank, evt)
        ev_last[tb] = evt
    t_hT = ev_last[15]

    t_fT_first = None
    for widx, (wsb, dstT) in enumerate([(wf1s, fT), (wg1s, fgT)]):
        if widx == 0:
            t_w = t_wf1
        else:
            t_w = S.dma("gpsimd", wg1s[:, :], din["wg1_t"][:, :], [t_fT_first], stream="g_p0")
        for tc in range(4):
            bank = 2 + (tc % 2)
            def mk_f(wsb=wsb, tc=tc, bank=bank):
                def f(E):
                    mm = None
                    for kbi in range(16):
                        mm = E.e.matmul(pb[bank][:, :], wsb[:, kbi * 128:(kbi + 1) * 128],
                                        hT[:, kbi * 2048 + tc * 512: kbi * 2048 + (tc + 1) * 512],
                                        start=(kbi == 0), stop=(kbi == 15))
                    return mm
                return f
            tk = S.eop("tensor", mk_f(), [t_w, t_hT, S.need_w(bank)])
            S.bw(bank, tk)
            evt = S.eop("scalar", (lambda dstT=dstT, tc=tc, bank=bank:
                                   lambda E: E.e.activation(
                                       dstT[:, tc * 512:(tc + 1) * 512], pb[bank][:, :],
                                       AF.Copy))(), [tk])
            S.br_(bank, evt)
        if widx == 0:
            t_fT_first = tk
    t_fT = evt
    hsc.close()

    t_beta = None
    for ch in range(NCH):
        def mk_b(ch=ch):
            def f(E):
                mm = None
                for kbi in range(16):
                    mm = E.e.matmul(pb[4][:, 0:8],
                                    hT[:, kbi * 2048 + ch * 128: kbi * 2048 + (ch + 1) * 128],
                                    wbS[:, kbi * 8:(kbi + 1) * 8],
                                    start=(kbi == 0), stop=(kbi == 15))
                return mm
            return f
        tk = S.eop("tensor", mk_b(), [t_wb, t_hT, S.need_w(4)])
        S.bw(4, tk)
        t_beta = S.eop("scalar", (lambda ch=ch: lambda E: E.e.activation(
            bbuf[:, ch * 8:(ch + 1) * 8], pb[4][:, 0:8], AF.Sigmoid))(), [tk])
        S.br_(4, t_beta)

    dq = {}

    def _dq_params(kind):
        if kind == "gu":
            return din["gup_pk"], scg, t_scg2, 256, 2, ldq_gu
        return din["dn_pk"], scd, t_scd2, 704, 4, ldq_dn

    dq_passes = [("gu", o, q) for o in range(44) for q in range(2)] + \
                [("dn", o, q) for o in range(8) for q in range(4)]
    dq_state = {"a": 0, "pending": [], "gu_last": None, "dn_last": None,
                "last_val": None, "dqt_tr": [None, None], "stg_any": None,
                "par": {}}

    def emit_dq_A(kind, o, q):
        pk_ap, sc_sb, t_sc, wrow, npass, _ = _dq_params(kind)
        par = len(dq_state["par"]) % 2
        dq_state["par"][(kind, o, q)] = par
        base = par * 1408
        if q == 0:
            guard = dq.get((kind, "pkread", o - 1)) if o > 0 else \
                (dq.get(("gu", "pkread", 43)) if kind == "dn" else None)
            dq[(kind, "pk", o)] = S.dma(
                "sync", pkb[0][:, 0:wrow], pk_ap[o * 128:(o + 1) * 128, :],
                [guard], stream="s_pk")
        t_pk = dq[(kind, "pk", o)]
        wpp = wrow // npass
        nin = wpp * 8

        def mk_v3(q=q, wpp=wpp):
            def f(E):
                last = None
                for i in range(8):
                    sh = 22 - 4 * i
                    op = ALU.logical_shift_left if sh >= 0 else ALU.logical_shift_right
                    last = E.e.tensor_scalar(iv3[:, i * wpp:(i + 1) * wpp],
                                             pkb[0][:, q * wpp:(q + 1) * wpp],
                                             abs(sh), 7 << 22, op, ALU.bitwise_and)
                return last
            return f
        t_v3 = S.eop("vector", mk_v3(), [t_pk, dq_state["last_val"]])

        def mk_gs1(q=q, wpp=wpp):
            def f(E):
                last = None
                for i in range(8):
                    last = E.e.tensor_scalar(inib[:, i * wpp:(i + 1) * wpp],
                                             pkb[0][:, q * wpp:(q + 1) * wpp],
                                             28 - 4 * i, -2147483648,
                                             ALU.logical_shift_left, ALU.bitwise_and)
                return last
            return f
        t_g = S.eop("vector", mk_gs1(), [t_pk, dq_state["last_val"]])
        t_gn = t_g

        def mk_vb(wpp=wpp):
            def f(E):
                last = None
                for i in range(8):
                    last = E.e.tensor_tensor(iv3[:, i * wpp:(i + 1) * wpp],
                                             iv3[:, i * wpp:(i + 1) * wpp],
                                             inib[:, i * wpp:(i + 1) * wpp], ALU.add)
                return last
            return f
        t_v = S.eop("gpsimd", mk_vb(), [t_v3, t_g])
        dq[(kind, "pkread", o)] = t_gn
        ng = wpp // 16

        def mk_val(o=o, q=q, nin=nin, ng=ng, base=base, wrow=wrow, sc_sb=sc_sb):
            def f(E):
                src = iv3[:, 0:nin].bitcast(F32).rearrange(
                    "p (i g w) -> p i g w", g=ng, w=16)
                scs = sc_sb[:, o * (wrow // 16) + q * ng: o * (wrow // 16) + (q + 1) * ng] \
                    .unsqueeze(1).unsqueeze(3).broadcast_to([128, 8, ng, 16])
                dst = dqt[:, base:base + nin].rearrange("p (g w i) -> p i g w", w=16, i=8)
                return E.e.tensor_tensor(dst, src, scs, ALU.mult)
            return f
        t_val = S.eop("vector", mk_val(), [t_v, t_sc, dq_state["dqt_tr"][par]])
        dq[(kind, "int", o, q)] = t_val
        dq_state["last_val"] = t_val

    def emit_dq_B(kind, o, q):
        _, _, _, wrow, npass, ldq_dst = _dq_params(kind)
        par = dq_state["par"][(kind, o, q)]
        base = par * 1408
        wpp = wrow // npass
        nin = wpp * 8
        nikp = nin // 128
        t_val = dq[(kind, "int", o, q)]
        t_ev = None
        t_trlast = None
        for g4 in range((nikp + 3) // 4):
            bank = 2 + (g4 % 2)
            nj = min(4, nikp - g4 * 4)

            def mk_tr(g4=g4, bank=bank, nj=nj, base=base):
                def f(E):
                    mm = None
                    for j in range(nj):
                        ikb = g4 * 4 + j
                        lhs = dqt[:, base + ikb * 128:base + (ikb + 1) * 128]
                        mm = E.e.matmul(pb[bank][:, j * 128:(j + 1) * 128],
                                        lhs, IDN, start=True, stop=True)
                    return mm
                return f
            tk = S.eop("tensor", mk_tr(), [t_val, S.need_w(bank)])
            S.bw(bank, tk)
            t_trlast = tk

            def mk_ev(g4=g4, bank=bank, nj=nj, base=base):
                def f(E):
                    sb = base + (g4 * 4) * 128
                    return E.e.activation(stg[:, sb: sb + nj * 128],
                                          pb[bank][:, 0:nj * 128], AF.Copy)
                return f
            evt = S.eop("scalar", mk_ev(), [tk, dq_state["stg_any"]])
            S.br_(bank, evt)
            t_ev = evt
        dq_state["dqt_tr"][par] = t_trlast
        last_dma = S.dma("scalar",
                         ldq_dst[o, :, q * nikp * 128:(q + 1) * nikp * 128],
                         stg[:, base:base + nikp * 128], [t_ev], stream="a_ldq")
        dq_state["stg_any"] = last_dma
        dq_state["gu_last" if kind == "gu" else "dn_last"] = last_dma

    def emit_dq_slot():
        for p in dq_state["pending"]:
            emit_dq_B(*p)
        dq_state["pending"] = []
        for _ in range(2):
            if dq_state["a"] < len(dq_passes):
                pp = dq_passes[dq_state["a"]]
                dq_state["a"] += 1
                emit_dq_A(*pp)
                dq_state["pending"].append(pp)

    def flush_dq():
        while dq_state["pending"] or dq_state["a"] < len(dq_passes):
            emit_dq_slot()

    def mk_dqcopy(E):
        last = None
        for arm in nc.switch_core_id(engines=[E.e], n=128):
            half = (arm.logical % 8) & 1
            if half == 0:
                i1 = E.e.dma_start(sh_wgu[0:22, :, :], ldq_gu[0:22, :, :])
                i2 = E.e.dma_start(sh_wgu[44:66, :, :], ldq_gu[22:44, :, :])
                i3 = E.e.dma_start(sh_wdn[0:8, :, :], ldq_dn[:, :, :])
            else:
                i1 = E.e.dma_start(sh_wgu[22:44, :, :], ldq_gu[0:22, :, :])
                i2 = E.e.dma_start(sh_wgu[66:88, :, :], ldq_gu[22:44, :, :])
                i3 = E.e.dma_start(sh_wdn[8:16, :, :], ldq_dn[:, :, :])
            for i in (i1, i2, i3):
                i.then_inc(E.sems["a_cp"], 16)
            last = i3
        return last

    def mk_flag(stage):
        def f(E):
            last = None
            for arm in nc.switch_core_id(engines=[E.e], n=128):
                slot = (arm.logical % 8) & 1
                i = E.e.dma_start(sh_flags[slot:slot + 1, stage:stage + 1],
                                  flagw[0:1, stage:stage + 1])
                i.then_inc(E.sems["a_flg"], 16)
                last = i
            return last
        return f

    if stop_after == "deq":
        flush_dq()

    prev = {}
    _heads = HL if stop_after != "deq" else 0
    _heads = min(_heads, DBG_HEADS)
    _nch = min(NCH, DBG_CHUNKS)
    for hl in range(_heads):
        ping = hl % 2
        t_wld = []
        for ki, wname in enumerate(["wq_t", "wk_t", "wv_t", "ww_t"]):
            t_wld.append(S.dma("gpsimd", wtile[ping][:, ki * 2048:(ki + 1) * 2048],
                               din[wname][hl, :, :], [prev.get(("scan", hl - 2))],
                               stream=f"g_w{ping}"))
        if hl == 0:
            t_pad = S.eop("vector", lambda E: E.e.memset(pre[:, 0:3], 0.0), [])
        else:
            t_pad = None

        post_sb = {0: qp, 1: kp, 2: vp, 3: wp}
        kind3_of = {0: 0, 1: 1, 2: 2, 3: 2}
        t_post = {}
        t_convprev = None
        for ki in range(4):
            k3 = kind3_of[ki]
            def mk_diag(hl=hl, k3=k3):
                def f(E):
                    i = None
                    for tap in range(4):
                        i = E.e.tensor_scalar(
                            diag[:, tap * 128:(tap + 1) * 128],
                            IDN, convS[:, k3 * 32 + hl * 4 + tap: k3 * 32 + hl * 4 + tap + 1],
                            None, ALU.mult)
                    return i
                return f
            t_diag = S.eop("vector", mk_diag(), [t_conv, t_cb, t_convprev,
                                                 prev.get(("scan", hl - 1))])
            for tc in range(4):
                bank = tc % 2
                def mk_p(hl=hl, ki=ki, tc=tc, bank=bank, ping=ping):
                    def f(E):
                        mm = None
                        for kbi in range(16):
                            mm = E.e.matmul(
                                pb[bank][:, :],
                                wtile[ping][:, ki * 2048 + kbi * 128: ki * 2048 + (kbi + 1) * 128],
                                hT[:, kbi * 2048 + tc * 512: kbi * 2048 + (tc + 1) * 512],
                                start=(kbi == 0), stop=(kbi == 15))
                        return mm
                    return f
                tk = S.eop("tensor", mk_p(), [t_wld[3], t_hT, S.need_w(bank)])
                S.bw(bank, tk)
                def mk_ev(ki=ki, tc=tc, bank=bank):
                    def f(E):
                        return E.e.activation(
                            pre[:, 3 + tc * 512: 3 + (tc + 1) * 512],
                            pb[bank][:, :], AF.Copy)
                    return f
                evt = S.eop("scalar", mk_ev(), [tk, prev.get(("scan", hl - 1)), t_pad])
                S.br_(bank, evt)
                cbank = 2 + (tc % 2)
                def mk_c(ki=ki, tc=tc, cbank=cbank, hl=hl):
                    def f(E):
                        mm = None
                        for tap in range(4):
                            mm = E.e.matmul(
                                pb[cbank][:, :],
                                diag[:, tap * 128:(tap + 1) * 128],
                                pre[:, tap + tc * 512: tap + (tc + 1) * 512],
                                start=(tap == 0), stop=(tap == 3))
                        return mm
                    return f
                ctk = S.eop("tensor", mk_c(), [evt, t_diag, S.need_w(cbank)])
                S.bw(cbank, ctk)
                def mk_s(ki=ki, tc=tc, cbank=cbank):
                    def f(E):
                        return E.e.activation(post_sb[ki][:, tc * 512:(tc + 1) * 512],
                                              pb[cbank][:, :], AF.Silu)
                    return f
                st = S.eop("scalar", mk_s(), [ctk, prev.get(("scan", hl - 1)),
                                              prev.get(("dbgall", hl - 1))])
                S.br_(cbank, st)
                t_post[(ki, tc)] = st
                t_convprev = ctk
                if tc in (0, 2):
                    emit_dq_slot()
        t_qs = S.eop("vector", lambda E: E.e.tensor_scalar(
            qp[:, :], qp[:, :], float(SCALE), None, ALU.mult),
            [t_post[(0, tc2)] for tc2 in range(4)])

        t_ls = {}
        for g4 in range(4):
            def mk_z(hl=hl, g4=g4):
                def f(E):
                    mm = None
                    for j in range(4):
                        ch = g4 * 4 + j
                        mm = E.e.matmul(pb[4][:, j * 128:(j + 1) * 128],
                                        fT[:, ch * 128:(ch + 1) * 128],
                                        wf2s[:, hl * 128:(hl + 1) * 128],
                                        start=True, stop=True)
                    return mm
                return f
            ztk = S.eop("tensor", mk_z(), [t_fT, t_wf2, S.need_w(4),
                                           prev.get(("ls", hl - 1))])
            S.bw(4, ztk)
            t_a = S.eop("scalar", (lambda g4=g4: lambda E: E.e.activation(
                lsb[:, g4 * 512:(g4 + 1) * 512], pb[4][:, :], AF.Abs))(),
                [ztk, prev.get(("dbgall", hl - 1))])
            t_b = S.eop("scalar", (lambda g4=g4: lambda E: E.e.activation(
                lsb[:, g4 * 512:(g4 + 1) * 512], lsb[:, g4 * 512:(g4 + 1) * 512],
                AF.Exp, scale=-1.0))(), [t_a])
            t_c = S.eop("scalar", (lambda g4=g4: lambda E: E.e.activation(
                lsb[:, g4 * 512:(g4 + 1) * 512], lsb[:, g4 * 512:(g4 + 1) * 512],
                AF.Ln, bias=1.0))(), [t_b])
            t_rl = S.eop("scalar", (lambda g4=g4: lambda E: E.e.activation(
                zscr[:, :], pb[4][:, :], AF.Relu, scale=-1.0))(),
                [t_c, (t_ls[g4 - 1] if g4 > 0 else None)])
            S.br_(4, t_rl)
            t_ls[g4] = S.eop("vector", (lambda g4=g4: lambda E: E.e.tensor_tensor(
                lsb[:, g4 * 512:(g4 + 1) * 512], lsb[:, g4 * 512:(g4 + 1) * 512],
                zscr[:, :], ALU.add))(), [t_rl, t_c])

        t_thh = None
        for g4 in range(4):
            def mk_gh(hl=hl, g4=g4):
                def f(E):
                    mm = None
                    for j in range(4):
                        ch = g4 * 4 + j
                        E.e.matmul(pb[3][:, j * 128:(j + 1) * 128],
                                   fgT[:, ch * 128:(ch + 1) * 128],
                                   wg2s[:, hl * 128:(hl + 1) * 128],
                                   start=True, stop=False)
                        mm = E.e.matmul(pb[3][:, j * 128:(j + 1) * 128],
                                        ONE[0:1, 0:128],
                                        bg2s[0:1, hl * 128:(hl + 1) * 128],
                                        start=False, stop=True)
                    return mm
                return f
            gtk = S.eop("tensor", mk_gh(), [t_fT, t_wg2, t_bg2, S.need_w(3),
                                            prev.get(("thb", hl - 1))])
            S.bw(3, gtk)
            t_thh = S.eop("scalar", (lambda g4=g4: lambda E: E.e.activation(
                thbuf[:, g4 * 512:(g4 + 1) * 512], pb[3][:, :], AF.Tanh, scale=0.5))(), [gtk])
            S.br_(3, t_thh)
        prev[("thb", hl)] = None

        t_w2c = []
        for ch in range(NCH):
            t_w2c.append(S.eop("scalar", (lambda ch=ch: lambda E: E.e.activation(
                sqs[:, :], wp[:, ch * 128:(ch + 1) * 128], AF.Square))(),
                [t_post[(3, ch // 4)], prev.get(("sw", ch - 1, hl)), prev.get(("sqs", hl - 1))]))
            def mk_sw(ch=ch):
                def f(E):
                    return E.e.matmul(pb[3][:, 256 + ch:256 + ch + 1], sqs[:, :],
                                      ONE[:, 0:1], start=True, stop=True)
                return f
            tk = S.eop("tensor", mk_sw(), [t_w2c[-1], S.need_w(3)])
            prev[("sw", ch, hl)] = tk
        S.bw(3, tk)
        t_sq = S.eop("scalar", lambda E: E.e.activation(
            ccol[:, :], pb[3][:, 256:256 + NCH], AF.Ln, bias=1e-6),
            [tk, prev.get(("dbgall", hl - 1))])
        S.br_(3, t_sq)
        t_rc1 = S.eop("scalar", lambda E: E.e.activation(
            ccol[:, :], ccol[:, :], AF.Exp, scale=-0.5), [t_sq])
        def mk_cc(hl=hl):
            def f(E):
                bsl = bbuf[:, :].rearrange("p (c h) -> p c h", h=8)[:, :, hl]
                return E.e.tensor_tensor(ccol[:, :], ccol[:, :], bsl, ALU.mult)
            return f
        t_cc = S.eop("vector", mk_cc(), [t_rc1, t_beta])

        if hl == 0:
            t_w2z = S.eop("vector", lambda E: E.e.memset(w2mB[:, :], 0.0), [])
        t_st0 = S.eop("vector", (lambda hl=hl: lambda E: E.e.memset(
            sktS[:, hl * 128:(hl + 1) * 128], 0.0))(), [prev.get(("scan", hl - 1))])
        t_sv0 = S.eop("vector", (lambda hl=hl: lambda E: E.e.memset(
            svtS[:, hl * 128:(hl + 1) * 128], 0.0))(), [t_st0])

        P = {}

        def mb(ch, b):
            return b if ch % 2 == 0 else b - 4

        def qm(ch, b, quarter):
            return q128(mb(ch, b), quarter)

        def F(ch):
            sl0, sl1 = ch * 128, (ch + 1) * 128
            b4 = mb(ch, 4)
            par2 = (ch % 2) * 2
            pp0, pp1 = (ch % 2) * 128, (ch % 2) * 128 + 128
            def mk_rel(b4=b4, sl0=sl0, sl1=sl1):
                def f(E):
                    E.e.matmul(pb[b4][:, 0:128], relmS[:, 0:128], lsb[:, sl0:sl1],
                               start=True, stop=True)
                    return E.e.matmul(pb[b4][:, 128:258], lsb[:, sl0:sl1], relmS[:, 0:130],
                                      start=True, stop=True)
                return f
            t_rel = S.eop("tensor", mk_rel(), [t_ls[ch // 4], t_relm, t_cc,
                                               P.get(("svna", ch - 2)),
                                               (S.need_w(b4) if ch < 2 else None)])
            t_lrel = S.eop("scalar", (lambda b4=b4, pp0=pp0, pp1=pp1: lambda E: E.e.activation(
                lrelT[:, pp0:pp1], pb[b4][:, 0:128], AF.Exp, scale=-1.0))(),
                [t_rel, P.get(("app", ch - 2))])
            t_emt = S.eop("scalar", (lambda b4=b4: lambda E: E.e.activation(
                emtB[:, :], pb[b4][:, 128:256], AF.Exp))(),
                [t_rel, P.get(("up", ch - 1))])
            t_lme = S.eop("scalar", (lambda b4=b4, par2=par2: lambda E: E.e.activation(
                lme3[:, par2:par2 + 2], pb[b4][:, 256:258], AF.Exp, scale=-1.0))(),
                [t_rel, P.get(("svn", ch - 2))])
            def mk_up(sl0=sl0, sl1=sl1, pp0=pp0, pp1=pp1):
                def f(E):
                    return E.e.tensor_tensor(upT[:, pp0:pp1], wp[:, sl0:sl1],
                                             emtB[:, :], ALU.mult)
                return f
            t_up = S.eop("vector", mk_up(), [t_emt, t_post[(3, ch // 4)],
                                             P.get(("w2mm", ch - 2))])
            t_utr = S.eop("tensor", (lambda ch=ch, pp0=pp0, pp1=pp1: lambda E: E.e.matmul(
                qm(ch, 6, 0), upT[:, pp0:pp1], IDN, start=True, stop=True))(),
                [t_up, P.get(("w2m", ch - 2)),
                 (S.need_w(mb(ch, 6)) if ch < 2 else None)])
            def mk_uev(ch=ch, pp0=pp0, pp1=pp1):
                def f(E):
                    return E.e.tensor_scalar(uu[:, pp0:pp1], qm(ch, 6, 0),
                                             ccol[:, ch:ch + 1], None, ALU.mult)
                return f
            t_uev = S.eop("vector", mk_uev(), [t_utr, t_cc, P.get(("dmm", ch - 2))])
            def mk_tr2(ch=ch, sl0=sl0, sl1=sl1):
                def f(E):
                    E.e.matmul(qm(ch, 5, 0), kp[:, sl0:sl1], IDN, start=True, stop=True)
                    return E.e.matmul(qm(ch, 5, 1), vp[:, sl0:sl1], IDN, start=True, stop=True)
                return f
            t_tr2 = S.eop("tensor", mk_tr2(), [t_post[(1, ch // 4)], t_post[(2, ch // 4)],
                                               P.get(("appe", ch - 2)),
                                               (S.need_w(mb(ch, 5)) if ch < 2 else None)])
            t_kev = t_vev = S.eop("scalar", (lambda ch=ch: lambda E: E.e.activation(
                kvtd[:, (ch % 2) * 256:(ch % 2) * 256 + 256],
                pb[mb(ch, 5)][:, 0:256], AF.Copy))(),
                [t_tr2, P.get(("omm", ch - 2))])
            def mk_ps(ch=ch, sl0=sl0, sl1=sl1):
                def f(E):
                    return E.e.matmul(qm(ch, 6, 1), kp[:, sl0:sl1], qp[:, sl0:sl1],
                                      start=True, stop=True)
                return f
            t_ps = S.eop("tensor", mk_ps(), [t_qs, t_post[(1, ch // 4)], t_uev])
            def mk_psm(ch=ch, pp0=pp0, pp1=pp1):
                def f(E):
                    return E.e.scalar_tensor_tensor(psmB[:, pp0:pp1], qm(ch, 6, 1),
                                                    ccol[:, ch:ch + 1], MSK,
                                                    ALU.mult, ALU.mult)
                return f
            t_psm = S.eop("vector", mk_psm(), [t_ps, t_cc, P.get(("i1", ch - 2))])
            P[("rel", ch)] = t_rel
            P[("lrel", ch)] = t_lrel
            P[("lme", ch)] = t_lme
            P[("up", ch)] = t_up
            P[("uev", ch)] = t_uev
            P[("kev", ch)] = t_kev
            P[("vev", ch)] = t_vev
            P[("psm", ch)] = t_psm

        def M(ch):
            sl0, sl1 = ch * 128, (ch + 1) * 128
            par2 = (ch % 2) * 2
            pp0, pp1 = (ch % 2) * 128, (ch % 2) * 128 + 128
            def mk_skts(hl=hl, par2=par2):
                def f(E):
                    return E.e.tensor_scalar(sktsB[:, :], sktS[:, hl * 128:(hl + 1) * 128],
                                             lme3[:, par2:par2 + 1], None, ALU.mult)
                return f
            t_skts = S.eop("vector", mk_skts(), [P[("lme", ch)], t_st0,
                                                 P.get(("skn", ch - 1))])
            def mk_svts(hl=hl, par2=par2):
                def f(E):
                    return E.e.tensor_scalar(svtsB[:, :], svtS[:, hl * 128:(hl + 1) * 128],
                                             lme3[:, par2:par2 + 1], None, ALU.mult)
                return f
            t_svts = S.eop("vector", mk_svts(), [P[("lme", ch)], t_sv0,
                                                 P.get(("svn", ch - 1))])
            t_sk0t = S.eop("tensor", (lambda ch=ch: lambda E: E.e.matmul(
                qm(ch, 5, 2), sktsB[:, :], IDN, start=True, stop=True))(),
                [t_skts, P[("kev", ch)], P[("vev", ch)]])
            t_sk0e = S.eop("scalar", (lambda ch=ch: lambda E: E.e.activation(
                sk0s[:, :], qm(ch, 5, 2), AF.Copy))(), [t_sk0t, P.get(("i1", ch - 1))])
            def mk_i1(ch=ch, sl0=sl0, sl1=sl1, pp0=pp0, pp1=pp1):
                def f(E):
                    E.e.matmul(qm(ch, 6, 2), psmB[:, pp0:pp1], uu[:, pp0:pp1],
                               start=True, stop=False)
                    return E.e.matmul(qm(ch, 6, 2), qp[:, sl0:sl1], sk0s[:, :],
                                      start=False, stop=True)
                return f
            t_i1 = S.eop("tensor", mk_i1(), [P[("psm", ch)], P[("uev", ch)], t_sk0e, t_qs])
            t_scr = S.eop("vector", (lambda ch=ch, pp0=pp0, pp1=pp1: lambda E: E.e.tensor_tensor(
                scrb[:, :], qm(ch, 6, 2), lrelT[:, pp0:pp1], ALU.mult))(),
                [t_i1, P[("lrel", ch)], P.get(("pt", ch - 1))])
            t_pt = S.eop("scalar", lambda E: E.e.activation(
                ptil[:, :], scrb[:, :], AF.Exp, accum_out=denom[:, 0:1]),
                [t_scr, P.get(("app", ch - 1)), P.get(("d2", ch - 1))])
            t_app = S.eop("vector", (lambda pp0=pp0, pp1=pp1: lambda E: E.e.tensor_tensor(
                appB[:, :], ptil[:, :], lrelT[:, pp0:pp1], ALU.mult))(),
                [t_pt, P.get(("appt", ch - 1))])
            t_appt = S.eop("tensor", (lambda ch=ch: lambda E: E.e.matmul(
                qm(ch, 5, 3), appB[:, :], IDN, start=True, stop=True))(),
                [t_app, t_sk0e])
            t_appe = S.eop("scalar", (lambda ch=ch: lambda E: E.e.activation(
                appT[:, :], qm(ch, 5, 3), AF.Copy))(),
                [t_appt, P.get(("omm", ch - 1))])
            b6 = mb(ch, 6)
            def mk_w2mm(b6=b6, pp0=pp0, pp1=pp1):
                def f(E):
                    E.e.matmul(pb[b6][0:64, 384:448], upT[:, pp0:pp0 + 64], appT[:, 0:64],
                               start=True, stop=True)
                    return E.e.matmul(pb[b6][:, 448:512], upT[:, pp0:pp1], appT[:, 64:128],
                                      start=True, stop=True)
                return f
            t_w2mm = S.eop("tensor", mk_w2mm(), [t_appe, P[("up", ch)], t_scr])
            def mk_w2ma(ch=ch, b6=b6):
                def f(E):
                    return E.e.scalar_tensor_tensor(w2mB[0:64, 0:64], pb[b6][0:64, 384:448],
                                                    ccol[0:64, ch:ch + 1], MSK[0:64, 0:64],
                                                    ALU.mult, ALU.mult)
                return f
            def mk_w2mb(ch=ch, b6=b6):
                def f(E):
                    return E.e.scalar_tensor_tensor(w2mB[:, 64:128], pb[b6][:, 448:512],
                                                    ccol[:, ch:ch + 1], MSK[:, 64:128],
                                                    ALU.mult, ALU.mult)
                return f
            t_w2ma = S.eop("vector", mk_w2ma(), [t_w2mm, P.get(("omm", ch - 1)),
                                                 (t_w2z if (hl == 0 and ch == 0) else None)])
            t_w2m = S.eop("vector", mk_w2mb(), [t_w2ma])
            b4 = mb(ch, 4)
            def mk_d12(ch=ch, b4=b4, pp0=pp0, pp1=pp1):
                def f(E):
                    kb = (ch % 2) * 256
                    E.e.matmul(pb[b4][:, 0:128], uu[:, pp0:pp1], kvtd[:, kb:kb + 128],
                               start=True, stop=True)
                    return E.e.matmul(pb[b4][:, 128:256], uu[:, pp0:pp1],
                                      kvtd[:, kb + 128:kb + 256], start=True, stop=True)
                return f
            t_dmm = S.eop("tensor", mk_d12(), [P[("uev", ch)], P[("kev", ch)],
                                               P[("vev", ch)], P[("lme", ch)],
                                               P.get(("svna", ch - 2))])
            b7 = mb(ch, 7)
            def mk_o(b7=b7, ch=ch):
                def f(E):
                    vb = (ch % 2) * 256 + 128
                    E.e.matmul(pb[b7][:, 384:512], w2mB[:, :], kvtd[:, vb:vb + 128],
                               start=True, stop=False)
                    return E.e.matmul(pb[b7][:, 384:512], appT[:, :], svtsB[:, :],
                                      start=False, stop=True)
                return f
            t_omm = S.eop("tensor", mk_o(), [t_w2m, P[("vev", ch)], t_svts, t_appe,
                                             P.get(("oev", ch - 2)),
                                             (S.need_w(b7) if ch < 2 else None)])
            P[("skts", ch)] = t_skts
            P[("svts", ch)] = t_svts
            P[("sk0e", ch)] = t_sk0e
            P[("i1", ch)] = t_i1
            P[("pt", ch)] = t_pt
            P[("app", ch)] = t_app
            P[("appt", ch)] = t_appt
            P[("appe", ch)] = t_appe
            P[("w2mm", ch)] = t_w2mm
            P[("w2m", ch)] = t_w2m
            P[("dmm", ch)] = t_dmm
            P[("omm", ch)] = t_omm

        def B1(ch):
            par2 = (ch % 2) * 2
            b7 = mb(ch, 7)
            t_skn_a = S.eop("vector", (lambda hl=hl, ch=ch: lambda E: E.e.scalar_tensor_tensor(
                sktS[:, hl * 128:(hl + 1) * 128], qm(ch, 4, 0),
                1.0, sktsB[:, :], ALU.mult, ALU.add))(),
                [P[("dmm", ch)], P[("skts", ch)]])
            t_skn = S.eop("vector", (lambda hl=hl, par2=par2: lambda E: E.e.tensor_scalar(
                sktS[:, hl * 128:(hl + 1) * 128], sktS[:, hl * 128:(hl + 1) * 128],
                lme3[:, par2 + 1:par2 + 2], None, ALU.mult))(), [t_skn_a])
            t_svn_a = S.eop("vector", (lambda hl=hl, ch=ch: lambda E: E.e.scalar_tensor_tensor(
                svtS[:, hl * 128:(hl + 1) * 128], qm(ch, 4, 1),
                1.0, svtsB[:, :], ALU.mult, ALU.add))(),
                [P[("dmm", ch)], P[("svts", ch)]])
            t_svn = S.eop("vector", (lambda hl=hl, par2=par2: lambda E: E.e.tensor_scalar(
                svtS[:, hl * 128:(hl + 1) * 128], svtS[:, hl * 128:(hl + 1) * 128],
                lme3[:, par2 + 1:par2 + 2], None, ALU.mult))(), [t_svn_a])
            t_sqo = S.eop("scalar", (lambda b7=b7: lambda E: E.e.activation(
                sqs[:, :], pb[b7][:, 384:512], AF.Square, accum_out=o2s[:, 0:1]))(),
                [P[("omm", ch)], P.get(("arg", ch - 1))])
            t_d2 = S.eop("vector", lambda E: E.e.tensor_tensor(
                rcol[:, 0:1], denom[:, 0:1], denom[:, 0:1], ALU.mult),
                [P[("pt", ch)], P.get(("arg", ch - 1))])
            t_arg = S.eop("vector", lambda E: E.e.scalar_tensor_tensor(
                rcol[:, 1:2], rcol[:, 0:1], 128e-5, o2s[:, 0:1], ALU.mult, ALU.add),
                [t_d2, t_sqo, P.get(("lnr", ch - 1))])
            t_lnr = S.eop("scalar", lambda E: E.e.activation(
                rcol[:, 2:3], rcol[:, 1:2], AF.Ln), [t_arg, P.get(("r", ch - 1))])
            t_r = S.eop("scalar", lambda E: E.e.activation(
                rcol[:, 3:4], rcol[:, 2:3], AF.Exp, scale=-0.5, bias=RLNC),
                [t_lnr, P.get(("om", ch - 1))])
            P[("sqo", ch)] = t_sqo
            P[("d2", ch)] = t_d2
            P[("arg", ch)] = t_arg
            P[("lnr", ch)] = t_lnr
            P[("r", ch)] = t_r
            P[("skn", ch)] = t_skn
            P[("svna", ch)] = t_svn_a
            P[("svn", ch)] = t_svn

        def B2(ch):
            sl0, sl1 = ch * 128, (ch + 1) * 128
            b7 = mb(ch, 7)
            t_om = S.eop("vector", (lambda b7=b7: lambda E: E.e.tensor_scalar(
                omb[:, :], pb[b7][:, 384:512], rcol[:, 3:4], None, ALU.mult))(),
                [P[("r", ch)], P[("omm", ch)], P.get(("of", ch - 1))])
            def mk_of(sl0=sl0, sl1=sl1):
                def f(E):
                    return E.e.scalar_tensor_tensor(ofin[:, :], thbuf[:, sl0:sl1], 1.0,
                                                    omb[:, :], ALU.add, ALU.mult)
                return f
            t_of = S.eop("vector", mk_of(), [t_om, t_thh, P.get(("otr", ch - 1))])
            t_otr = S.eop("tensor", (lambda ch=ch: lambda E: E.e.matmul(
                qm(ch, 7, 2), ofin[:, :], IDN, start=True, stop=True))(),
                [t_of])
            def mk_oev(ch=ch):
                def f(E):
                    return E.e.activation(ostg[:, ch * 128:(ch + 1) * 128],
                                          qm(ch, 7, 2), AF.Copy)
                return f
            t_oev = S.eop("scalar", mk_oev(), [t_otr, prev.get(("ostg", hl - 1)),
                                               prev.get(("dbgall", hl - 1))])
            P[("om", ch)] = t_om
            P[("of", ch)] = t_of
            P[("otr", ch)] = t_otr
            P[("oev", ch)] = t_oev

        for ch in range(_nch):
            if ch >= 2:
                B1(ch - 2)
            F(ch)
            if ch >= 1:
                M(ch - 1)
            if ch >= 2:
                B2(ch - 2)
        if _nch >= 2:
            B1(_nch - 2)
        M(_nch - 1)
        if _nch >= 2:
            B2(_nch - 2)
        B1(_nch - 1)
        B2(_nch - 1)

        for par in (0, 1):
            lc = _nch - 1 if (_nch - 1) % 2 == par else _nch - 2
            if lc < 0:
                continue
            off = 0 if par == 0 else -4
            S.bw(4 + off, P[("dmm", lc)])
            S.br_(4 + off, P[("svna", lc)])
            S.bw(5 + off, P[("appt", lc)])
            S.br_(5 + off, P[("appe", lc)])
            S.bw(6 + off, P[("w2mm", lc)])
            S.br_(6 + off, P[("w2m", lc)])
            S.bw(7 + off, P[("otr", lc)])
            S.br_(7 + off, P[("oev", lc)])

        def mk_oship(hl=hl):
            def f(E):
                last = None
                for arm in nc.switch_core_id(engines=[E.e], n=128):
                    slot = (arm.logical % 8) & 1
                    i = E.e.dma_start(sh_o[slot, hl * 128:(hl + 1) * 128, :], ostg[:, :])
                    i.then_inc(E.sems["s_osh"], 16)
                    last = i
                return last
            return f
        t_oship = S.op("sync", mk_oship(), [P[("oev", _nch - 1)]], "s_osh", 16, manual=True)
        prev[("ostg", hl)] = t_oship
        prev[("thb", hl)] = P[("of", _nch - 1)]
        prev[("scan", hl)] = P[("svn", _nch - 1)]
        prev[("ls", hl)] = P[("rel", _nch - 1)]
        prev[("sqs", hl)] = P[("arg", _nch - 1)]
        if hl == 0:
            _dts = [
                dbg_dump("qp0", [128, 2048], BF16, qp[:, :], [t_qs]),
                dbg_dump("kp0", [128, 2048], BF16, kp[:, :], [t_post[(1, tc2)] for tc2 in range(4)]),
                dbg_dump("wp0", [128, 2048], BF16, wp[:, :], [t_post[(3, tc2)] for tc2 in range(4)]),
                dbg_dump("lsb0", [128, 2048], F32, lsb[:, :], [t_ls[g] for g in range(4)]),
                dbg_dump("ccol0", [128, NCH], F32, ccol[:, :], [t_cc]),
                dbg_dump("ostg0", [128, 2048], BF16, ostg[:, :], [P[("oev", _nch - 1)]]),
                dbg_dump("fT0", [128, 2048], BF16, fT[:, :], [t_fT]),
                dbg_dump("bbuf0", [128, 128], F32, bbuf[:, :], [t_beta]),
            ]
            _dts = [t for t in _dts if t is not None]
            prev[("dbgall", 0)] = ("s_dbg", S.cnt["s_dbg"]) if _dts else None
    flush_dq()
    t_dn_l = dq_state["dn_last"]
    t_dqcp = S.op("scalar", mk_dqcopy, [t_dn_l], "a_cp", 48, manual=True)
    t_dqflag = S.op("scalar", mk_flag(1), [t_dqcp, t_flag1], "a_flg", 16, manual=True)
    if stop_after != "deq" and _heads > 0:
        t_scan_all = prev.get(("scan", _heads - 1))
        t_oflag = S.op("scalar", mk_flag(2),
                       [prev[("ostg", _heads - 1)], t_flag2, t_dqflag],
                       "a_flg", 16, manual=True)
    else:
        t_oflag = None

    def mk_poll(E):
        if no_poll:
            return E.e.nop()
        with E.e.register("expd") as expd, E.e.register("expo") as expo, \
                E.e.register("wv") as wv, E.e.register("cnt") as cnt:
            nr = E.e.value_load(noncS[0:1, 0:1])
            E.e.reg_alu(expd, nr, 1, op=ALU.add)
            E.e.reg_alu(expo, nr, 2, op=ALU.add)
            E.e.reg_mov(wv, 0)
            E.e.reg_mov(cnt, 0)
            nc.br("ploop", engines=[E.e.engine])
            with nc.body("ploop", valid_engines=[E.e.engine]):
                E.e.dma_start(flagr[0:1, 0:32],
                              sh_flags[0:2, 0:16].rearrange("a b -> (a b)").unsqueeze(0)) \
                    .then_inc(E.sems["sB"], 16)
                E.e.reg_alu(wv, wv, 16, op=ALU.add)
                E.e.wait_ge(E.sems["sB"], wv)
                E.e.reg_alu(cnt, cnt, 1, op=ALU.add)
                f0 = E.e.value_load(flagr[0:1, 1:2])
                f1 = E.e.value_load(flagr[0:1, 17:18])
                f2 = E.e.value_load(flagr[0:1, 2:3])
                f3 = E.e.value_load(flagr[0:1, 18:19])
                E.e.br_cmp(f0, expd, on_true="pm1", on_false="pchk", comp_op="IS_EQ")
            with nc.body("pm1", valid_engines=[E.e.engine]):
                E.e.br_cmp(f1, expd, on_true="pm2", on_false="pchk", comp_op="IS_EQ")
            with nc.body("pm2", valid_engines=[E.e.engine]):
                E.e.br_cmp(f2, expo, on_true="pm3", on_false="pchk", comp_op="IS_EQ")
            with nc.body("pm3", valid_engines=[E.e.engine]):
                E.e.br_cmp(f3, expo, on_true="pend", on_false="pchk", comp_op="IS_EQ")
            with nc.body("pchk", valid_engines=[E.e.engine]):
                E.e.br_cmp(cnt, 3000000, on_true="ploop", on_false="pend", comp_op="IS_LT")
            nc.switch_bb("pend")
            E.e.reg_save(flagr[0:1, 30:31], expd)
            E.e.reg_save(flagr[0:1, 31:32], cnt)
        return E.e.nop()
    if stop_after in ("deq", "scan"):
        no_poll = True
    t_barrier = S.op("gpsimd", mk_poll, [t_dqflag, t_oflag, t_non], "sP", 1)
    dbg_poll = None
    t_pdump = None
    if "poll" in debug:
        dbg_poll = nc.dram_tensor("dbg_poll", [1, 32], I32, kind="ExternalOutput").ap()
        def mk_pdump(E):
            return E.e.dma_start(dbg_poll, flagr[0:1, 0:32])
        t_pdump = S.op("gpsimd", mk_pdump, [t_barrier], "sB", 16)

    if stop_after != "full":
        def mk_dummy_out(E):
            return E.e.dma_start(out_my[0:128, 0:136], relmS[:, :])
        t_do = S.dma("sync", out_my[0:128, 0:136], relmS[:, :], [t_barrier], stream="s_out")
        t_fin0 = S.op("sync", lambda E: E.e.nop(),
                      [t_do] + ([t_pdump] if t_pdump else []), "sY", 1)
        def run_for(eng_name):
            def f(engine):
                E = Eng(engine, sems)
                for (en, waits, fn, inc) in S.ops:
                    if en != eng_name:
                        continue
                    for w in waits:
                        E.wait(w)
                    instr = fn(E)
                    if inc is not None:
                        instr.then_inc(sems[inc[0]], inc[1])
            return f
        blk.sync(run_for("sync"))
        blk.gpsimd(run_for("gpsimd"))
        blk.tensor(run_for("tensor"))
        blk.vector(run_for("vector"))
        blk.scalar(run_for("scalar"))
        p4s_dummy = None
        p23.close()
        big.close()
        return nc, dbg
    p23.close()
    p4s = contextlib.ExitStack()
    x2f = sbuf(p4s, "x2f", [128, 4 * 2048], F32)
    x2T = sbuf(p4s, "x2T", [128, 16 * 512], BF16)
    x2b = sbuf(p4s, "x2b", [128, 2048], BF16)
    hff = sbuf(p4s, "hff", [128, NIK * 512], BF16)
    fmst = sbuf(p4s, "fmst", [128, 16 * 512], BF16)
    scrA = [sbuf(p4s, f"scrA{i}", [128, 2048], BF16) for i in range(4)]
    dnslab = [sbuf(p4s, f"dnslab{i}", [128, NIK * 128], BF16) for i in range(2)]
    xs = sbuf(p4s, "xs", [128, 2048], F32)
    x1b = sbuf(p4s, "x1b", [128, 2048], F32)
    xout = sbuf(p4s, "xout", [128, 2048], F32)
    sgl = sbuf(p4s, "sgl", [128, 512], F32)
    lncol = sbuf(p4s, "lncol", [128, 16], F32)
    oT5 = hff[:, 0:16 * 512]

    t_out_toks = []
    p4p = {}
    for t5 in range(2):
        def mk_oT5(t5=t5):
            def f(E):
                last = None
                for arm in nc.switch_core_id(engines=[E.e], n=128):
                    half = (arm.logical % 8) & 1
                    c0 = half * 1024 + t5 * 512
                    i1 = E.e.dma_start(
                        oT5[:, 0:8 * 512].rearrange("p (h t) -> p h t", t=512),
                        sh_o[0, :, c0:c0 + 512].rearrange("(h p) t -> p h t", p=128))
                    i1.then_inc(E.sems["s_oT5"], 16)
                    i2 = E.e.dma_start(
                        oT5[:, 8 * 512:16 * 512].rearrange("p (h t) -> p h t", t=512),
                        sh_o[1, :, c0:c0 + 512].rearrange("(h p) t -> p h t", p=128))
                    i2.then_inc(E.sems["s_oT5"], 16)
                    last = i2
                return last
            return f
        t_o2 = S.op("sync", mk_oT5(), [t_barrier, p4p.get("hff")],
                    "s_oT5", 32, manual=True)
        loc_gate = [prev.get(("ostg", _heads - 1)), prev.get(("scan", _heads - 1))] \
            if (_heads > 0 and stop_after == "full") else [t_barrier]
        t_attn_ev = {}
        for ob in range(16):
            pg = ob % 4
            t_wo = S.dma("gpsimd", scrA[pg][:, :], din["wo_t"][ob, :, :],
                         [p4p.get(("wofree", t5, ob - 4))] + loc_gate +
                         [p4p.get(("scrA_gu", t5 - 1))], stream=f"g_wo{pg % 2}")
            t_nw = S.eop("vector", (lambda pg=pg: lambda E: E.e.tensor_scalar(
                scrA[pg][:, :], scrA[pg][:, :], nwc[:, 0:1], None, ALU.mult))(),
                [t_wo, t_nwc])
            bank = ob % 2
            def mk_at(pg=pg, bank=bank):
                def f(E):
                    mm = None
                    for j in range(16):
                        mm = E.e.matmul(pb[bank][:, :], scrA[pg][:, j * 128:(j + 1) * 128],
                                        oT5[:, j * 512:(j + 1) * 512],
                                        start=(j == 0), stop=(j == 15))
                    return mm
                return f
            tk = S.eop("tensor", mk_at(), [t_nw, t_o2, S.need_w(bank)])
            S.bw(bank, tk)
            evt = S.eop("scalar", (lambda ob=ob, bank=bank: lambda E: E.e.activation(
                fmst[:, ob * 512:(ob + 1) * 512], pb[bank][:, :], AF.Copy))(),
                [tk, p4p.get("fmst")])
            S.br_(bank, evt)
            t_attn_ev[ob] = evt
            p4p[("wofree", t5, ob)] = tk
        p4p["oT5r"] = p4p[("wofree", t5, 15)]
        t_at_trs = []
        for tt in range(4):
            t_xs = S.dma("sync", xs[:, :],
                         din["x_res"][t5 * 512 + tt * 128: t5 * 512 + (tt + 1) * 128, :],
                         [p4p.get("xs")] + loc_gate, stream="s_xs")
            t_x1 = []
            for og in range(4):
                def mk_atr(og=og, tt=tt):
                    def f(E):
                        mm = None
                        for j in range(4):
                            ob = og * 4 + j
                            mm = E.e.matmul(q128(2, j),
                                            fmst[:, ob * 512 + tt * 128: ob * 512 + (tt + 1) * 128],
                                            IDN, start=True, stop=True)
                        return mm
                    return f
                tk = S.eop("tensor", mk_atr(), [t_attn_ev[og * 4 + j2] for j2 in range(4)] +
                           [S.need_w(2)])
                S.bw(2, tk)
                t_at_trs.append(tk)
                def mk_x1(og=og):
                    def f(E):
                        return E.e.scalar_tensor_tensor(
                            x1b[:, og * 512:(og + 1) * 512], pb[2][:, :], 1.0,
                            xs[:, og * 512:(og + 1) * 512], ALU.mult, ALU.add,
                            accum_out=lncol[:, og:og + 1])
                    return f
                t1 = S.eop("vector", mk_x1(), [tk, t_xs, p4p.get("x1b"), p4p.get("dbg4")])
                S.br_(2, t1)
                t_x1.append(t1)
            t_sqx = []
            for og2 in range(4):
                t_sqx.append(S.eop("scalar", (lambda og=og2: lambda E: E.e.activation(
                    sgl[:, :], x1b[:, og * 512:(og + 1) * 512], AF.Square,
                    accum_out=lncol[:, 4 + og:5 + og]))(),
                    [t_x1[og2], p4p.get("sgl"), (t_sqx[-1] if t_sqx else None)]))
            t_s1 = S.eop("vector", lambda E: E.e.reduce_sum(
                lncol[:, 8:9], lncol[:, 0:4], axis=AXX), t_x1)
            t_s2 = S.eop("vector", lambda E: E.e.reduce_sum(
                lncol[:, 9:10], lncol[:, 4:8], axis=AXX), t_sqx)
            t_mu = S.eop("scalar", lambda E: E.e.activation(
                lncol[:, 10:11], lncol[:, 8:9], AF.Copy, scale=1.0 / 2048), [t_s1])
            t_mu2 = S.eop("scalar", lambda E: E.e.activation(
                lncol[:, 11:12], lncol[:, 10:11], AF.Square), [t_mu])
            t_var = S.eop("vector", lambda E: E.e.scalar_tensor_tensor(
                lncol[:, 12:13], lncol[:, 9:10], 1.0 / 2048, lncol[:, 11:12],
                ALU.mult, ALU.subtract), [t_s2, t_mu2])
            t_sd = S.eop("scalar", lambda E: E.e.activation(
                lncol[:, 13:14], lncol[:, 12:13], AF.Ln, bias=1e-5), [t_var])
            t_rstd = S.eop("scalar", lambda E: E.e.activation(
                lncol[:, 14:15], lncol[:, 13:14], AF.Exp, scale=-0.5), [t_sd])
            t_x2 = []
            for og in range(4):
                def mk_x2(og=og, tt=tt):
                    def f(E):
                        rb = lncol[:, 14:15].broadcast_to([128, 512])
                        return E.e.scalar_tensor_tensor(
                            x2f[:, tt * 2048 + og * 512: tt * 2048 + (og + 1) * 512],
                            x1b[:, og * 512:(og + 1) * 512], lncol[:, 10:11], rb,
                            ALU.subtract, ALU.mult)
                    return f
                t_x2.append(S.eop("vector", mk_x2(), [t_rstd, t_mu, t_x1[og],
                                                      p4p.get("x2f")]))
            t_x2b = S.eop("vector", (lambda tt=tt: lambda E: E.e.tensor_copy(
                x2b[:, :], x2f[:, tt * 2048:(tt + 1) * 2048]))(),
                t_x2 + [p4p.get("x2b")])
            for g4 in range(4):
                def mk_xt(g4=g4):
                    def f(E):
                        mm = None
                        for j in range(4):
                            kb2 = g4 * 4 + j
                            mm = E.e.matmul(q128(3, j), x2b[:, kb2 * 128:(kb2 + 1) * 128],
                                            IDN, start=True, stop=True)
                        return mm
                    return f
                tk = S.eop("tensor", mk_xt(), [t_x2b, S.need_w(3)])
                S.bw(3, tk)
                def mk_xte(g4=g4, tt=tt):
                    def f(E):
                        dst = x2T[:, :].rearrange("p (kb t) -> p kb t", t=512)[
                            :, g4 * 4:(g4 + 1) * 4, tt * 128:(tt + 1) * 128]
                        return E.e.activation(dst, pb[3][:, :].rearrange(
                            "p (a b) -> p a b", b=128), AF.Copy)
                    return f
                evt = S.eop("scalar", mk_xte(), [tk, p4p.get("x2T")])
                S.br_(3, evt)
            p4p["xs"] = t_x1[-1]
            p4p["x1b"] = t_x2[-1]
            p4p["sgl"] = t_s2
            p4p[("x2T", t5, tt)] = evt
            if t5 == 0 and tt == 0:
                _d1 = dbg_dump("x1_0", [128, 2048], F32, x1b[:, :], t_x1)
                _d2 = dbg_dump("attnb0", [128, 8192], BF16, fmst[:, :],
                               [t_attn_ev[obx] for obx in range(16)])
                if _d1 is not None or _d2 is not None:
                    p4p["dbg4"] = ("s_dbg", S.cnt["s_dbg"])
        p4p["x2b"] = t_x2b
        p4p["fmst"] = t_at_trs[-1]
        t_x2T_all = [p4p[("x2T", t5, tt2)] for tt2 in range(4)]

        t_hff = {}
        for gb in range(44):
            pg = gb % 2
            t_lg = S.dma("sync", scrA[pg][:, :], sh_wgu[gb, :, :],
                         [p4p.get(("gfree", t5, gb - 2)), t_barrier, p4p.get("oT5r")],
                         stream=f"s_gu{pg}")
            t_lu = S.dma("sync", scrA[2 + pg][:, :], sh_wgu[44 + gb, :, :],
                         [p4p.get(("ufree", t5, gb - 2)), t_barrier, p4p.get("oT5r")],
                         stream=f"s_gu{pg}")
            bank, bank_u = pg, 2 + pg
            def mk_gu(pg=pg, bank=bank, bank_u=bank_u):
                def f(E):
                    mm = None
                    for kbi in range(16):
                        E.e.matmul(pb[bank][:, :], scrA[pg][:, kbi * 128:(kbi + 1) * 128],
                                   x2T[:, kbi * 512:(kbi + 1) * 512],
                                   start=(kbi == 0), stop=(kbi == 15))
                    for kbi in range(16):
                        mm = E.e.matmul(pb[bank_u][:, :],
                                        scrA[2 + pg][:, kbi * 128:(kbi + 1) * 128],
                                        x2T[:, kbi * 512:(kbi + 1) * 512],
                                        start=(kbi == 0), stop=(kbi == 15))
                    return mm
                return f
            tk = S.eop("tensor", mk_gu(), t_x2T_all + [t_lu, S.need_w(bank),
                                                       S.need_w(bank_u)])
            S.bw(bank, tk)
            S.bw(bank_u, tk)
            t_sil = S.eop("scalar", (lambda bank=bank: lambda E: E.e.activation(
                sgl[:, :], pb[bank][:, :], AF.Silu))(), [tk, p4p.get("sgl")])
            S.br_(bank, t_sil)
            def mk_h(gb=gb, bank_u=bank_u):
                def f(E):
                    return E.e.tensor_tensor(hff[:, gb * 512:(gb + 1) * 512],
                                             sgl[:, :], pb[bank_u][:, :], ALU.mult)
                return f
            t_h = S.eop("vector", mk_h(), [t_sil, p4p.get(("hffw", t5, gb))])
            S.br_(bank_u, t_h)
            p4p["sgl"] = t_h
            p4p[("gfree", t5, gb)] = tk
            p4p[("ufree", t5, gb)] = tk
            t_hff[gb] = t_h
        p4p[("scrA_gu", t5)] = tk
        t_hff_all = [t_hff[g2] for g2 in range(44)]

        t_dn_ev = {}
        for ob in range(16):
            pg = ob % 2
            t_ld = S.dma("sync", dnslab[pg][:, :], sh_wdn[ob, :, :],
                         [p4p.get(("dfree", t5, ob - 2)), t_barrier], stream=f"s_dn{pg}")
            bank = 4 + (ob % 2)
            def mk_dn(pg=pg, bank=bank):
                def f(E):
                    mm = None
                    for ikb in range(NIK):
                        mm = E.e.matmul(pb[bank][:, :], dnslab[pg][:, ikb * 128:(ikb + 1) * 128],
                                        hff[:, ikb * 512:(ikb + 1) * 512],
                                        start=(ikb == 0), stop=(ikb == NIK - 1))
                    return mm
                return f
            tk = S.eop("tensor", mk_dn(), t_hff_all + [t_ld, S.need_w(bank)])
            S.bw(bank, tk)
            evt = S.eop("scalar", (lambda ob=ob, bank=bank: lambda E: E.e.activation(
                fmst[:, ob * 512:(ob + 1) * 512], pb[bank][:, :], AF.Copy))(),
                [tk, p4p.get("fmst2"), p4p.get("dbg4")])
            S.br_(bank, evt)
            t_dn_ev[ob] = evt
            p4p[("dfree", t5, ob)] = tk

        for tt in range(4):
            t_ytr = []
            for og in range(4):
                def mk_yt(og=og, tt=tt):
                    def f(E):
                        mm = None
                        for j in range(4):
                            ob = og * 4 + j
                            mm = E.e.matmul(q128(6, j),
                                            fmst[:, ob * 512 + tt * 128: ob * 512 + (tt + 1) * 128],
                                            IDN, start=True, stop=True)
                        return mm
                    return f
                tk = S.eop("tensor", mk_yt(), [t_dn_ev[og * 4 + j2] for j2 in range(4)] +
                           [S.need_w(6)])
                S.bw(6, tk)
                evt = S.eop("scalar", (lambda og=og: lambda E: E.e.activation(
                    xout[:, og * 512:(og + 1) * 512], pb[6][:, :], AF.Copy))(),
                    [tk, p4p.get("xout")])
                S.br_(6, evt)
                t_ytr.append(evt)
            t_sqy = []
            for og2 in range(4):
                t_sqy.append(S.eop("scalar", (lambda og=og2: lambda E: E.e.activation(
                    sgl[:, :], xout[:, og * 512:(og + 1) * 512], AF.Square,
                    accum_out=lncol[:, og:og + 1]))(),
                    [t_ytr[og2], p4p.get("sgl"), (t_sqy[-1] if t_sqy else None)]))
            t_sy = S.eop("vector", lambda E: E.e.reduce_sum(
                lncol[:, 8:9], lncol[:, 0:4], axis=AXX), t_sqy)
            t_sdy = S.eop("scalar", lambda E: E.e.activation(
                lncol[:, 9:10], lncol[:, 8:9], AF.Ln, scale=1.0 / 2048, bias=1e-6), [t_sy])
            t_ry = S.eop("scalar", lambda E: E.e.activation(
                lncol[:, 10:11], lncol[:, 9:10], AF.Exp, scale=-0.5), [t_sdy])
            t_x3 = []
            for og in range(4):
                def mk_x3(og=og, tt=tt):
                    def f(E):
                        return E.e.scalar_tensor_tensor(
                            x1b[:, og * 512:(og + 1) * 512], xout[:, og * 512:(og + 1) * 512],
                            lncol[:, 10:11],
                            x2f[:, tt * 2048 + og * 512: tt * 2048 + (og + 1) * 512],
                            ALU.mult, ALU.add, accum_out=lncol[:, 4 + og:5 + og])
                    return f
                t_x3.append(S.eop("vector", mk_x3(), [t_ry, t_ytr[og], p4p.get("x1c")]))
            t_sqx3 = []
            for og2 in range(4):
                t_sqx3.append(S.eop("scalar", (lambda og=og2: lambda E: E.e.activation(
                    sgl[:, :], x1b[:, og * 512:(og + 1) * 512], AF.Square,
                    accum_out=lncol[:, 11 + og:12 + og]))(),
                    [t_x3[og2], p4p.get("sgl"), (t_sqx3[-1] if t_sqx3 else None)]))
            t_s1b = S.eop("vector", lambda E: E.e.reduce_sum(
                lncol[:, 8:9], lncol[:, 4:8], axis=AXX), t_x3)
            t_s2b = S.eop("vector", lambda E: E.e.reduce_sum(
                lncol[:, 9:10], lncol[:, 11:15], axis=AXX), t_sqx3)
            t_mub = S.eop("scalar", lambda E: E.e.activation(
                lncol[:, 10:11], lncol[:, 8:9], AF.Copy, scale=1.0 / 2048), [t_s1b])
            t_mu2b = S.eop("scalar", lambda E: E.e.activation(
                lncol[:, 15:16], lncol[:, 10:11], AF.Square), [t_mub])
            t_varb = S.eop("vector", lambda E: E.e.scalar_tensor_tensor(
                lncol[:, 12:13], lncol[:, 9:10], 1.0 / 2048, lncol[:, 15:16],
                ALU.mult, ALU.subtract), [t_s2b, t_mu2b])
            t_sdb = S.eop("scalar", lambda E: E.e.activation(
                lncol[:, 13:14], lncol[:, 12:13], AF.Ln, bias=1e-5), [t_varb])
            t_rsb = S.eop("scalar", lambda E: E.e.activation(
                lncol[:, 14:15], lncol[:, 13:14], AF.Exp, scale=-0.5), [t_sdb])
            t_x4 = []
            for og in range(4):
                def mk_x4(og=og):
                    def f(E):
                        rb = lncol[:, 14:15].broadcast_to([128, 512])
                        return E.e.scalar_tensor_tensor(
                            xout[:, og * 512:(og + 1) * 512], x1b[:, og * 512:(og + 1) * 512],
                            lncol[:, 10:11], rb, ALU.subtract, ALU.mult)
                    return f
                t_x4.append(S.eop("vector", mk_x4(), [t_rsb, t_mub, t_sqx3[og],
                                                      t_sqy[og]]))
            t_wr = S.dma("sync", out_my[t5 * 512 + tt * 128: t5 * 512 + (tt + 1) * 128, :],
                         xout[:, :], t_x4, stream="s_out")
            t_out_toks.append(t_wr)
            p4p["xout"] = t_wr
            p4p["x1c"] = t_sqx3[-1]
            p4p["sgl"] = t_sqx3[-1]
        p4p["fmst2"] = tk
        p4p["hff"] = tk
        p4p["x2f"] = t_x4[-1]

    t_fin = S.op("sync", lambda E: E.e.nop(),
                 t_out_toks + ([t_pdump] if t_pdump else []), "sY", 1)

    def run_for(eng_name):
        def f(engine):
            E = Eng(engine, sems)
            for (en, waits, fn, inc) in S.ops:
                if en != eng_name:
                    continue
                for w in waits:
                    E.wait(w)
                instr = fn(E)
                if inc is not None:
                    instr.then_inc(sems[inc[0]], inc[1])
        return f

    blk.sync(run_for("sync"))
    blk.gpsimd(run_for("gpsimd"))
    blk.tensor(run_for("tensor"))
    blk.vector(run_for("vector"))
    blk.scalar(run_for("scalar"))

    p4s.close()
    big.close()
    return nc, dbg


def kernel(**inputs):
    consts = make_consts()
    nonce = np.array([[int(np.random.randint(1, 2 ** 24 - 16)), 0, 0, 0]], dtype=np.int32)
    nc, _ = build_program()
    in_maps = [make_inputs_for_core(c, inputs, nonce, consts) for c in range(8)]
    res = run_bass_kernel_spmd(nc, in_maps, core_ids=list(range(8)))
    out = np.zeros((B, T, HID), np.float32)
    for c in range(8):
        b, half = c // 2, c & 1
        out[b, half * 1024:(half + 1) * 1024] = res.results[c]["out_my"]
    return out
